# version 13
# baseline (speedup 1.0000x reference)
import numpy as np
from contextlib import ExitStack

import concourse.bass as bass
import concourse.mybir as mybir
from concourse.bass_utils import run_bass_kernel_spmd

F32 = mybir.dt.float32
BF16 = mybir.dt.bfloat16
AF = mybir.ActivationFunctionType
ALU = mybir.AluOpType
AX = mybir.AxisListType

D = 1024
T = 4096
NH = 16
KKEEP = 391
NSEL = KKEEP - 64
NTOK = 2048
NE = 8
DI = 4096
NBIS = 42

DEBUG = False
PHASES = 6
NEXP = 9
MOE_PART = 3
WBC = True
ENGINES = ("tensor", "vector", "scalar", "gpsimd", "sync")


class Prog:
    def __init__(self, nc, blk, sems, dma_hints=None):
        self.nc, self.blk = nc, blk
        self.sems = sems
        self.cnt = {s: 0 for s in sems}
        self.q = {e: [] for e in ENGINES}
        self.known = {e: {s: 0 for s in sems} for e in ENGINES}
        self.measuring = dma_hints is None
        self.dma_hints = {s: [] for s in sems} if self.measuring else dma_hints
        self.hint_idx = {s: 0 for s in sems}

    def emit(self, eng, fn, waits=(), inc=None, dma_sem=None):
        real = []
        for s, v in waits:
            assert v is not None, f"None wait token on {s} for {eng}"
            if v > 0 and self.known[eng][s] < v:
                real.append((s, v))
                self.known[eng][s] = v
        tok = None
        if inc is not None:
            s, n = inc
            self.cnt[s] += n
            tok = self.cnt[s]
        self.q[eng].append((fn, real, inc, dma_sem))
        return tok

    def pe(self, fn, waits=(), inc=True):
        return self.emit("tensor", fn, waits, ("pe", 1) if inc else None)

    def v(self, fn, waits=(), sw=True):
        w = list(waits) + ([("dve", self.cnt["dve"])] if sw else [])
        return self.emit("vector", fn, w, ("dve", 1))

    def s(self, fn, waits=(), sw=True):
        w = list(waits) + ([("act", self.cnt["act"])] if sw else [])
        return self.emit("scalar", fn, w, ("act", 1))

    def dma(self, queue, sem, fn, waits=()):
        if self.measuring:
            inc = 16
        else:
            inc = self.dma_hints[sem][self.hint_idx[sem]]
            self.hint_idx[sem] += 1
        return self.emit(queue, fn, waits, (sem, inc), dma_sem=sem)

    def wall(self, name):
        return (name, self.cnt[name])

    def flush(self):
        for e in ENGINES:
            items = self.q[e]
            if not items:
                continue
            sems = self.sems

            nc = self.nc
            measuring = self.measuring
            hints = self.dma_hints

            def run(eng, items=tuple(items)):
                for fn, waits, inc, dma_sem in items:
                    for s, v in waits:
                        eng.wait_ge(sems[s], v)
                    n0 = nc.n_instructions() if (measuring and dma_sem) else 0
                    ins = fn(eng)
                    if inc is not None:
                        ins.then_inc(sems[inc[0]], 16 if dma_sem else inc[1])
                        if measuring and dma_sem:
                            names = nc.inst_names()[n0:]
                            nupd = 0
                            for nm2 in names:
                                i2 = nc.find_inst_by_name(nm2)
                                si = i2.sync_info if i2 is not None else None
                                if si is not None and any(
                                        u.sync_type == 'semaphore' and u.ant_name == dma_sem
                                        for u in si.on_update):
                                    nupd += 1
                            hints[dma_sem].append(16 * max(1, nupd))
            getattr(self.blk, e)(run)
            self.q[e] = []

    def barrier(self):
        snap = [(s, self.cnt[s]) for s in self.sems if self.cnt[s] > 0]
        for e in ENGINES:
            for s, v in snap:
                if self.known[e][s] < v:
                    self.q[e].append((lambda eng: None, [(s, v)], None, None))
                    self.known[e][s] = v
        self.flush()


def _host_consts():
    C = np.zeros((128, 402), dtype=np.float32)
    C[:, 0:128] = np.eye(128, dtype=np.float32)
    C[:, 128:256] = 1.0
    C[0:64, 256] = 1.0
    C[64:128, 257] = 1.0
    for p in range(128):
        C[p, 258 + p // 8] = 1.0
        g = p // 8
        C[p, 274 + 8 * g:274 + 8 * g + 8] = 1.0
    return C


def build_nc(dma_hints=None):
    nc = bass.Bass("TRN2", target_bir_lowering=False)
    ap = lambda t: t.ap()
    x_d = ap(nc.dram_tensor("x", [T, D], F32, kind="ExternalInput"))
    xq_d = ap(nc.dram_tensor("xq", [NTOK, D], F32, kind="ExternalInput"))
    wattn_d = ap(nc.dram_tensor("wattn", [D, 3 * D], F32, kind="ExternalInput"))
    wproj_d = ap(nc.dram_tensor("wproj", [D, D], F32, kind="ExternalInput"))
    gatew_d = ap(nc.dram_tensor("gatew", [D, NE], F32, kind="ExternalInput"))
    gateb_d = ap(nc.dram_tensor("gateb", [1, NE], F32, kind="ExternalInput"))
    upw_d = ap(nc.dram_tensor("upw", [NE, D, DI], F32, kind="ExternalInput"))
    upb_d = ap(nc.dram_tensor("upb", [NE, DI], F32, kind="ExternalInput"))
    downw_d = ap(nc.dram_tensor("downw", [NE, DI, D], F32, kind="ExternalInput"))
    downb_d = ap(nc.dram_tensor("downb", [NE, D], F32, kind="ExternalInput"))
    swup_d = ap(nc.dram_tensor("swup", [D, DI], F32, kind="ExternalInput"))
    sbup_d = ap(nc.dram_tensor("sbup", [1, DI], F32, kind="ExternalInput"))
    swdown_d = ap(nc.dram_tensor("swdown", [DI, D], F32, kind="ExternalInput"))
    sbdown_d = ap(nc.dram_tensor("sbdown", [1, D], F32, kind="ExternalInput"))
    consts_d = ap(nc.dram_tensor("consts", [128, 402], F32, kind="ExternalInput"))
    y_d = ap(nc.dram_tensor("y", [NTOK, D], F32, kind="ExternalOutput"))
    qfm_d = ap(nc.dram_tensor("qfm_s", [D, NTOK], BF16, kind="Internal"))
    xres_d = ap(nc.dram_tensor("xres_s", [D, NTOK], F32, kind="Internal"))
    x1_d = ap(nc.dram_tensor("x1_s", [D, NTOK], F32, kind="Internal"))
    x1bf_d = ap(nc.dram_tensor("x1bf_s", [D, NTOK], BF16, kind="Internal"))
    ltc_d = ap(nc.dram_tensor("ltc_s", [16, 32 * 128], F32, kind="Internal"))
    taps = {}
    if DEBUG:
        for nm, shp in [("t_last", [128, 512]), ("t_hi", [128, 16]),
                        ("t_x1", [128, 8 * 512]), ("t_wt", [8, NTOK]),
                        ("t_kfm", [128, 512]), ("t_q", [128, 512])]:
            taps[nm] = ap(nc.dram_tensor(nm, shp, F32, kind="ExternalOutput"))

    es = ExitStack()
    sem_names = ["pe", "dve", "act", "sy", "gd", "x0", "x1", "up0", "up1",
                 "dn0", "dn1", "dn2", "dn3", "pf0", "pf1", "out"]
    sems = {n: es.enter_context(nc.semaphore(n)) for n in sem_names}

    CONSTS = es.enter_context(nc.sbuf_tensor("CONSTS", [128, 402], F32))
    BIAST = es.enter_context(nc.sbuf_tensor("BIAST", [128, 4 * 128], F32))
    QNBUF = es.enter_context(nc.sbuf_tensor("QNBUF", [128, 64], F32))
    KNBUF = es.enter_context(nc.sbuf_tensor("KNBUF", [128, 64], F32))
    XLASTBF = es.enter_context(nc.sbuf_tensor("XLASTBF", [128, 8], BF16))
    QBD = es.enter_context(nc.sbuf_tensor("QBD", [128, 8 * 16], BF16))
    EPSC = es.enter_context(nc.sbuf_tensor("EPSC", [128, 1], F32))
    _open_stacks = []
    stKV = ExitStack()
    _open_stacks.append(stKV)
    KFM = stKV.enter_context(nc.sbuf_tensor("KFM", [128, 8 * T], BF16))
    VTM = stKV.enter_context(nc.sbuf_tensor("VTM", [128, 32 * NH * 65], BF16))
    WPROJS = stKV.enter_context(nc.sbuf_tensor("WPROJS", [128, 8 * 1024], BF16))

    blk = es.enter_context(nc.Block())
    P = Prog(nc, blk, sems, dma_hints)
    IDENT = CONSTS[:, 0:128]
    ONES = CONSTS[:, 128:256]
    BLK2 = CONSTS[:, 256:258]
    SAME = CONSTS[:, 274:402]

    from concourse.bass_types import AP as _APT

    def _ap(x):
        return x if isinstance(x, _APT) else x[:]

    def MM(ps, lhsT, rhs, start, stop, waits=(), tp=None):
        return P.pe(lambda t, a=_ap(ps), l=_ap(lhsT), r=_ap(rhs), s1=start, s2=stop, q=tp:
                    t.matmul(a, l, r, start=s1, stop=s2, tile_position=q), waits, inc=True)

    def TR(ps, src, waits=()):
        sap = _ap(src)
        n = sap.shape[0]
        return P.pe(lambda t, a=_ap(ps), s=sap, i=IDENT[0:n, 0:n]: t.transpose(a, s, i), waits)


    def _trunc_tail():
        for st_ in reversed(_open_stacks):
            st_.close()
        OUTZ = es.enter_context(nc.sbuf_tensor("OUTZ", [128, 1024], F32))
        tok_z = P.v(lambda e: e.memset(OUTZ[:], 0.0))
        for tc in range(16):
            P.dma("sync", "out", lambda e, t=tc: e.dma_start(
                out=y_d[t * 128:(t + 1) * 128, :], in_=OUTZ[:]), waits=[("dve", tok_z)])
        P.emit("sync", lambda e: None,
               waits=[("out", P.cnt["out"]), P.wall("sy"), P.wall("gd"),
                      P.wall("up0"), P.wall("up1"), P.wall("dn0"), P.wall("dn1"),
                      P.wall("dn2"), P.wall("dn3"),
                      P.wall("x0"), P.wall("x1"), P.wall("pf0"), P.wall("pf1"),
                      ("dve", P.cnt["dve"]), ("act", P.cnt["act"]), ("pe", P.cnt["pe"])])
        P.flush()
        es.close()
        return nc, P.dma_hints

    tok_consts = P.dma("sync", "sy", lambda e: e.dma_start(out=CONSTS[:], in_=consts_d))
    tok_eps = P.v(lambda e: e.memset(EPSC[:], 1.0e-5))
    tok_qnz = P.v(lambda e: e.memset(QNBUF[:], 0.0))
    tok_knz = P.v(lambda e: e.memset(KNBUF[:], 0.0))
    wc = [("sy", tok_consts)]

    stA = ExitStack()
    psbA = lambda name, shape: stA.enter_context(nc.psum_tensor(name, shape, F32))
    WQK = stA.enter_context(nc.sbuf_tensor("WQK", [128, 8 * 1024], BF16))
    XT4 = [stA.enter_context(nc.sbuf_tensor(f"XT4_{i}", [128, D], F32)) for i in range(2)]
    XFMF = stA.enter_context(nc.sbuf_tensor("XFMF", [128, 8 * 512], F32))
    XFM = stA.enter_context(nc.sbuf_tensor("XFM", [128, 8 * 512], BF16))
    QST2 = [stA.enter_context(nc.sbuf_tensor(f"QST2_{i}", [128, 512], BF16)) for i in range(2)]
    SQA = stA.enter_context(nc.sbuf_tensor("SQA", [128, 512], F32))
    QLASTBF = stA.enter_context(nc.sbuf_tensor("QLASTBF", [128, 8], BF16))
    PS_TRF = [psbA(f"PS_TRF{i}", [128, 128]) for i in range(2)]
    PS_MM = [psbA(f"PS_MM{i}", [128, 512]) for i in range(2)]
    PS_N = psbA("PS_N", [2, 512])
    PS_Q1 = psbA("PS_Q1", [128, 16])

    def load_wqk(col0, waits=()):
        return P.dma("gpsimd", "gd", lambda e: e.dma_start(
            out=WQK.ap().rearrange("p (dc f) -> p dc f", f=1024),
            in_=wattn_d[:, col0:col0 + 1024].rearrange("(dc p) f -> p dc f", p=128)),
            waits=waits)

    tok_wqk = load_wqk(1024)

    xt_free = [0, 0]
    tr_bank = [0, 0]
    mm_bank = [0, 0]
    sq_free = 0
    psn_free = 0
    xstash_sy = 0

    half_i = [0]
    pending = []

    def x_load_next():
        if not pending:
            return
        rows = pending.pop(0)
        slot = half_i[0] % 2
        half_i[0] += 1
        src = rows.rearrange("(tc p) d -> p tc d", p=128)
        P.dma("sync", "x0" if slot == 0 else "x1",
              lambda e, s=src, sl=slot: e.dma_start(
                  out=XT4[sl].ap().rearrange("p (tc d) -> p tc d", d=D), in_=s),
              waits=[("pe", xt_free[slot])])

    def norm_stat(src_bf, buf, col, wtok):
        nonlocal sq_free, psn_free
        t_sq = P.s(lambda e, s=src_bf: e.activation(SQA[:], s, AF.Square),
                   waits=wtok + [("pe", sq_free)])
        tok = MM(PS_N[:], BLK2, SQA[:], True, True,
                 waits=[("act", t_sq), ("dve", psn_free)] + wc)
        sq_free = tok
        psn_free = P.v(lambda e, b=buf, c=col: e.reduce_max(
            b[0:2, c:c + 1], PS_N[0:2, :], axis=AX.X), waits=[("pe", tok)])
        return psn_free

    proc_i = [0]

    def transpose_pass(make_bf16):
        ev = 0
        for h2 in range(4):
            slot = proc_i[0] % 2
            sem = "x0" if slot == 0 else "x1"
            proc_i[0] += 1
            for dcc in range(8):
                src = XT4[slot][:, dcc * 128: dcc * 128 + 128]
                TR(PS_TRF[dcc % 2], src,
                   waits=[P.wall(sem), ("dve", tr_bank[dcc % 2])] + wc)
                ev = P.v(lambda e, b2=dcc % 2, tc=h2, dc=dcc: e.tensor_copy(
                    XFMF[:, dc * 512 + tc * 128: dc * 512 + tc * 128 + 128], PS_TRF[b2][:]),
                    waits=[("pe", P.cnt["pe"]), ("sy", xstash_sy)])
                tr_bank[dcc % 2] = ev
            xt_free[slot] = P.cnt["pe"]
            x_load_next()
        if make_bf16:
            tok_bf = P.v(lambda e: e.tensor_copy(XFM[:], XFMF[:]), waits=[("dve", ev)])
            return ev, tok_bf
        return ev, ev

    tok_xl = 0
    for st in range(8):
        for h2 in range(4):
            pending.append(x_d[st * 512 + h2 * 128: st * 512 + (h2 + 1) * 128, :])
    for qst in range(4):
        for h2 in range(4):
            pending.append(xq_d[qst * 512 + h2 * 128: qst * 512 + (h2 + 1) * 128, :])
    x_load_next()
    x_load_next()
    for st in range(8):
        ev, tok_bf = transpose_pass(True)
        for kfc in range(8):
            for dc in range(8):
                tok_mm = MM(PS_MM[kfc % 2],
                            WQK[:, dc * 1024 + kfc * 128: dc * 1024 + kfc * 128 + 128],
                            XFM[:, dc * 512:(dc + 1) * 512],
                            start=(dc == 0), stop=(dc == 7),
                            waits=[("gd", tok_wqk), ("dve", tok_bf),
                                   ("dve", mm_bank[kfc % 2])])
            tok_ev = P.v(lambda e, kf=kfc, s=st: e.tensor_copy(
                KFM[:, kf * T + s * 512: kf * T + s * 512 + 512], PS_MM[kf % 2][:]),
                waits=[("pe", tok_mm)])
            mm_bank[kfc % 2] = tok_ev
            norm_stat(KFM[:, kfc * T + st * 512: kfc * T + st * 512 + 512],
                      KNBUF, st * 8 + kfc, [("dve", tok_ev)])
        if st == 7:
            tok_xl = P.v(lambda e: e.tensor_copy(
                XLASTBF[:], XFMF.ap().rearrange("p (dc t) -> p dc t", t=512)[:, :, 511:512]),
                waits=[("dve", ev)])

    tok_wqk = load_wqk(0, waits=[("pe", P.cnt["pe"])])
    wq_gd = [("gd", tok_wqk)]
    for qst in range(4):
        ev, tok_bf = transpose_pass(True)
        for qfc in range(8):
            for dc in range(8):
                tok_mm = MM(PS_MM[qfc % 2],
                            WQK[:, dc * 1024 + qfc * 128: dc * 1024 + qfc * 128 + 128],
                            XFM[:, dc * 512:(dc + 1) * 512],
                            start=(dc == 0), stop=(dc == 7),
                            waits=[("dve", tok_bf), ("dve", mm_bank[qfc % 2])] + wq_gd)
            tok_ev = P.v(lambda e, qf=qfc: e.tensor_copy(
                QST2[qf % 2][:], PS_MM[qf % 2][:]),
                waits=[("pe", tok_mm), P.wall("up0" if qfc % 2 == 0 else "up1")])
            mm_bank[qfc % 2] = tok_ev
            norm_stat(QST2[qfc % 2][:], QNBUF,
                      qst * 8 + qfc, [("dve", tok_ev)])
            P.dma("gpsimd", "up0" if qfc % 2 == 0 else "up1",
                  lambda e, q=qst, qf=qfc: e.dma_start(
                      out=qfm_d[qf * 128:(qf + 1) * 128, q * 512:(q + 1) * 512],
                      in_=QST2[qf % 2][:]), waits=[("dve", tok_ev)])
        P.dma("sync", "sy", lambda e, q=qst: e.dma_start(
            out=xres_d.rearrange("(dc p) t -> p dc t", p=128)[:, :, q * 512:(q + 1) * 512],
            in_=XFMF.ap().rearrange("p (dc t) -> p dc t", t=512)),
            waits=[("dve", tok_bf)])
        xstash_sy = P.cnt["sy"]

    tok_qle = 0
    for qfc in range(8):
        for dc in range(8):
            tok_mm = MM(PS_Q1[:, 0:1],
                        WQK[:, dc * 1024 + qfc * 128: dc * 1024 + qfc * 128 + 128],
                        XLASTBF[:, dc:dc + 1], start=(dc == 0), stop=(dc == 7),
                        waits=[("dve", tok_xl), ("dve", tok_qle)])
        tok_qle = P.v(lambda e, qf=qfc: e.tensor_copy(QLASTBF[:, qf:qf + 1], PS_Q1[:, 0:1]),
                      waits=[("pe", tok_mm)])
    tok_qbd = P.v(lambda e: e.memset(QBD[:], 0.0))
    for dc in range(8):
        P.v(lambda e, d=dc: e.tensor_copy(QBD[0:64, d * 16 + 2 * d: d * 16 + 2 * d + 1],
                                          QLASTBF[0:64, d:d + 1]),
            waits=[("dve", tok_qle), ("dve", tok_qbd)])
        tok_qbd = P.v(lambda e, d=dc: e.tensor_copy(
            QBD[64:128, d * 16 + 2 * d + 1: d * 16 + 2 * d + 2],
            QLASTBF[64:128, d:d + 1]))
    P.barrier()
    stA.close()
    if PHASES <= 1:
        return _trunc_tail()

    stV = ExitStack()
    psbV = lambda name, shape: stV.enter_context(nc.psum_tensor(name, shape, F32))
    WV = stV.enter_context(nc.sbuf_tensor("WV", [128, 8 * 1024], BF16))
    XT1 = stV.enter_context(nc.sbuf_tensor("XT1", [128, 4 * D], F32))
    XFMF2 = stV.enter_context(nc.sbuf_tensor("XFMF2", [128, 8 * 512], F32))
    XFM2 = stV.enter_context(nc.sbuf_tensor("XFM2", [128, 8 * 512], BF16))
    PS_TRV = [psbV(f"PS_TRV{i}", [128, 128]) for i in range(2)]
    PS_MV = [psbV(f"PS_MV{i}", [128, 512]) for i in range(2)]
    tok_wv = P.dma("gpsimd", "gd", lambda e: e.dma_start(
        out=WV.ap().rearrange("p (dc f) -> p dc f", f=1024),
        in_=wattn_d[:, 2048:3072].rearrange("(dc p) f -> p dc f", p=128)))
    tok_vones = P.v(lambda e: e.memset(
        VTM.ap().rearrange("p (tc h c) -> p tc h c", h=NH, c=65)[:, :, :, 64:65], 1.0))
    trv_bank = [0, 0]
    vev = [0, 0]
    xt1_free = 0
    for st in range(8):
        src = x_d[st * 512:(st + 1) * 512, :].rearrange("(tc p) d -> p tc d", p=128)
        P.dma("sync", "x0", lambda e, s=src: e.dma_start(
            out=XT1.ap().rearrange("p (tc d) -> p tc d", d=D), in_=s),
            waits=[("pe", xt1_free)])
        evl = 0
        for b in range(32):
            tc4, dcc = b // 8, b % 8
            srcb = XT1[:, tc4 * D + dcc * 128: tc4 * D + dcc * 128 + 128]
            TR(PS_TRV[b % 2], srcb, waits=[P.wall("x0"), ("dve", trv_bank[b % 2])] + wc)
            evl = P.v(lambda e, b2=b % 2, tc=tc4, dc=dcc: e.tensor_copy(
                XFMF2[:, dc * 512 + tc * 128: dc * 512 + tc * 128 + 128], PS_TRV[b2][:]),
                waits=[("pe", P.cnt["pe"])])
            trv_bank[b % 2] = evl
        xt1_free = P.cnt["pe"]
        tok_bf = P.v(lambda e: e.tensor_copy(XFM2[:], XFMF2[:]), waits=[("dve", evl)])
        for tc4 in range(4):
            for dc in range(8):
                w = [("gd", tok_wv), ("dve", tok_bf)]
                if dc == 0:
                    w += [("dve", vev[0]), ("dve", vev[1])]
                lhsT = XFM2[:, dc * 512 + tc4 * 128: dc * 512 + tc4 * 128 + 128]
                MM(PS_MV[0], lhsT, WV[:, dc * 1024: dc * 1024 + 512],
                   start=(dc == 0), stop=(dc == 7), waits=w)
                tok_mm = MM(PS_MV[1], lhsT, WV[:, dc * 1024 + 512: dc * 1024 + 1024],
                            start=(dc == 0), stop=(dc == 7))
            tcg = st * 4 + tc4
            for vfh in range(2):
                off = tcg * NH * 65 + vfh * 8 * 65
                dst = VTM[:, off: off + 8 * 65].rearrange("p (h c) -> p h c", c=65)[:, :, 0:64]
                vev[vfh] = P.v(lambda e, d=dst, v=vfh: e.tensor_copy(d, PS_MV[v][:]),
                               waits=[("pe", tok_mm), ("dve", tok_vones)])
    P.barrier()
    stV.close()
    if PHASES <= 2:
        return _trunc_tail()

    stL = ExitStack()
    psbL = lambda name, shape: stL.enter_context(nc.psum_tensor(name, shape, F32))
    LAST = stL.enter_context(nc.sbuf_tensor("LAST", [128, 512], F32))
    LTMP = [stL.enter_context(nc.sbuf_tensor(f"LTMP{i}", [128, 16], F32)) for i in range(2)]
    LTC = stL.enter_context(nc.sbuf_tensor("LTC", [16, 32 * 128], F32))
    NEGROW = stL.enter_context(nc.sbuf_tensor("NEGROW", [16, 64], F32))
    SC = stL.enter_context(nc.sbuf_tensor("SC", [128, 16], F32))
    GS = stL.enter_context(nc.sbuf_tensor("GS", [1, 8], F32))
    MASKB = stL.enter_context(nc.sbuf_tensor("MASKB", [128, 512], F32))
    MASKT = stL.enter_context(nc.sbuf_tensor("MASKT", [128, 4 * 128], F32))
    PS_SM = [psbL(f"PS_SM{i}", [128, 16]) for i in range(2)]
    PS_T16 = [psbL(f"PS_T16{i}", [16, 128]) for i in range(2)]
    PS_TRL = [psbL(f"PS_TRL{i}", [128, 128]) for i in range(2)]
    PS_C = psbL("PS_C", [128, 4])

    tok_wproj = P.dma("gpsimd", "gd", lambda e: e.dma_start(
        out=WPROJS.ap().rearrange("p (hp f) -> p hp f", f=1024),
        in_=wproj_d.rearrange("(hp p) f -> p hp f", p=128)))

    le_bank = [0, 0]
    lt_bank = [0, 0]
    for kc in range(32):
        for kfc in range(8):
            tok_mm = MM(PS_SM[kc % 2],
                        KFM[:, kfc * T + kc * 128: kfc * T + kc * 128 + 128],
                        QBD[:, kfc * 16:(kfc + 1) * 16],
                        start=(kfc == 0), stop=(kfc == 7),
                        waits=[("dve", tok_qbd), ("dve", le_bank[kc % 2])])
        tok_le = P.v(lambda e, k=kc % 2: e.tensor_copy(LTMP[k][:], PS_SM[k][:]),
                     waits=[("pe", tok_mm)])
        le_bank[kc % 2] = tok_le
        TR(PS_T16[kc % 2], LTMP[kc % 2][:],
           waits=[("dve", tok_le), ("dve", lt_bank[kc % 2])])
        tok_lt = P.v(lambda e, k=kc: e.tensor_copy(
            LTC[:, k * 128:(k + 1) * 128], PS_T16[k % 2][0:16, :]),
            waits=[("pe", P.cnt["pe"])])
        lt_bank[kc % 2] = tok_lt
    tok_neg = P.v(lambda e: e.memset(LTC[0:16, 4032:4096], -1.0e30),
                  waits=[("dve", lt_bank[0]), ("dve", lt_bank[1])])
    P.dma("sync", "sy", lambda e: e.dma_start(out=ltc_d, in_=LTC[:]),
          waits=[("dve", tok_neg), ("pe", P.cnt["pe"])])
    P.dma("sync", "sy", lambda e: e.dma_start(
        out=LAST[:],
        in_=ltc_d.rearrange("h (j f) -> (h j) f", j=8)),
        waits=[P.wall("sy")])
    wlast = [P.wall("sy")]
    tok_rmin = P.v(lambda e: e.tensor_reduce(SC[:, 0:1], LAST[:, 0:448], axis=AX.X, op=ALU.min),
                   waits=wlast)
    tok_rmax = P.v(lambda e: e.reduce_max(SC[:, 1:2], LAST[:], axis=AX.X), waits=wlast)
    TR(PS_T16[0][0:1, :], SC[:, 0:1], waits=[("dve", tok_rmax), ("dve", tok_rmin),
                                             ("dve", lt_bank[0])])
    TR(PS_T16[1][0:1, :], SC[:, 1:2], waits=[("dve", tok_rmax), ("dve", lt_bank[1])])
    tok_tr2 = P.cnt["pe"]
    tok_gmin = P.v(lambda e: e.tensor_reduce(GS[0:1, 0:1], PS_T16[0][0:1, :], axis=AX.X,
                                             op=ALU.min), waits=[("pe", tok_tr2)])
    tok_gmax = P.v(lambda e: e.reduce_max(GS[0:1, 1:2], PS_T16[1][0:1, :], axis=AX.X))
    tok_gm2 = P.v(lambda e: e.tensor_scalar(GS[0:1, 0:1], GS[0:1, 0:1], -10.0, None, op0=ALU.add))
    tok_mmlo = MM(PS_C[:, 0:2], ONES[0:1, :], GS[0:1, 0:2], True, True,
                  waits=[("dve", tok_gm2), ("dve", tok_gmax)])
    tok_dve = P.v(lambda e: e.tensor_copy(SC[:, 0:2], PS_C[:, 0:2]), waits=[("pe", tok_mmlo)])
    for it in range(NBIS):
        P.v(lambda e: e.tensor_tensor(SC[:, 2:3], SC[:, 0:1], SC[:, 1:2], op=ALU.add),
            waits=[("dve", tok_dve)])
        P.v(lambda e: e.tensor_scalar(SC[:, 2:3], SC[:, 2:3], 0.5, None, op0=ALU.mult))
        tok_cp = P.v(lambda e: e.tensor_scalar(MASKB[:], LAST[:], SC[:, 2:3], None, op0=ALU.is_gt),
                     waits=wlast)
        tok_cp = P.v(lambda e: e.reduce_sum(SC[:, 3:4], MASKB[:], axis=AX.X))
        tok_cmm = MM(PS_C[:, 2:3], SAME, SC[:, 3:4], True, True,
                     waits=[("dve", tok_cp)] + wc)
        P.v(lambda e: e.tensor_copy(SC[:, 4:5], PS_C[:, 2:3]), waits=[("pe", tok_cmm)])
        P.v(lambda e: e.tensor_scalar(SC[:, 5:6], SC[:, 4:5], float(NSEL), None, op0=ALU.is_gt))
        P.v(lambda e: e.tensor_tensor(SC[:, 6:7], SC[:, 2:3], SC[:, 0:1], op=ALU.subtract))
        P.v(lambda e: e.tensor_tensor(SC[:, 7:8], SC[:, 2:3], SC[:, 1:2], op=ALU.subtract))
        P.v(lambda e: e.tensor_scalar(SC[:, 6:7], SC[:, 6:7], SC[:, 5:6], None, op0=ALU.mult))
        P.v(lambda e: e.tensor_scalar(SC[:, 8:9], SC[:, 7:8], SC[:, 5:6], None, op0=ALU.mult))
        P.v(lambda e: e.tensor_tensor(SC[:, 7:8], SC[:, 7:8], SC[:, 8:9], op=ALU.subtract))
        P.v(lambda e: e.tensor_tensor(SC[:, 0:1], SC[:, 0:1], SC[:, 6:7], op=ALU.add))
        tok_dve = P.v(lambda e: e.tensor_tensor(SC[:, 1:2], SC[:, 1:2], SC[:, 7:8], op=ALU.add))
    tok_mask = P.v(lambda e: e.tensor_scalar(MASKB[:], LAST[:], SC[:, 1:2], None, op0=ALU.is_gt),
                   waits=[("dve", tok_dve)])
    tok_qn = P.v(lambda e: e.reduce_max(SC[:, 9:10], QNBUF[:, 0:32], axis=AX.X))
    tok_kn = P.v(lambda e: e.reduce_max(SC[:, 10:11], KNBUF[:, 0:64], axis=AX.X))
    TR(PS_T16[0][0:1, :], SC[:, 9:10], waits=[("dve", tok_qn), ("dve", tok_gmax)])
    TR(PS_T16[1][0:1, :], SC[:, 10:11], waits=[("dve", tok_kn), ("dve", tok_gmax)])
    tok_tr3 = P.cnt["pe"]
    tok_nm = P.v(lambda e: e.reduce_max(GS[0:1, 2:3], PS_T16[0][0:1, :], axis=AX.X),
                 waits=[("pe", tok_tr3)])
    tok_nm2 = P.v(lambda e: e.reduce_max(GS[0:1, 3:4], PS_T16[1][0:1, :], axis=AX.X))
    tok_sq1 = P.s(lambda e: e.activation(GS[0:1, 4:5], GS[0:1, 2:3], AF.Sqrt),
                  waits=[("dve", tok_nm), ("dve", tok_nm2)])
    tok_sq2 = P.s(lambda e: e.activation(GS[0:1, 5:6], GS[0:1, 3:4], AF.Sqrt))
    tok_m0 = P.v(lambda e: e.tensor_tensor(GS[0:1, 6:7], GS[0:1, 4:5], GS[0:1, 5:6], op=ALU.mult),
                 waits=[("act", tok_sq2)])
    tok_m0mm = MM(PS_C[:, 3:4], ONES[0:1, :], GS[0:1, 6:7], True, True, waits=[("dve", tok_m0)])
    tok_m0c = P.v(lambda e: e.tensor_copy(SC[:, 12:13], PS_C[:, 3:4]), waits=[("pe", tok_m0mm)])
    tok_off = P.v(lambda e: e.tensor_scalar(SC[:, 11:12], SC[:, 12:13], 0.125, 1.0e30,
                                            op0=ALU.mult, op1=ALU.add))
    bt = []
    for c in range(4):
        TR(PS_TRL[c % 2], MASKB[:, c * 128:(c + 1) * 128],
           waits=[("dve", tok_mask)] + ([("dve", bt[-2])] if len(bt) >= 2 else []))
        bt.append(P.v(lambda e, c2=c: e.tensor_copy(
            MASKT[:, c2 * 128:(c2 + 1) * 128], PS_TRL[c % 2 if False else c2 % 2][:]),
            waits=[("pe", P.cnt["pe"])]))
    tok_mrec = P.v(lambda e: e.memset(MASKT[64:128, 3 * 128 + 7: 4 * 128: 8], 1.0),
                   waits=[("dve", bt[3])])
    tok_bias = 0
    for c in range(4):
        tok_bias = P.v(lambda e, c2=c: e.tensor_scalar(
            BIAST[:, c2 * 128:(c2 + 1) * 128], MASKT[:, c2 * 128:(c2 + 1) * 128],
            1.0e30, SC[:, 11:12], op0=ALU.mult, op1=ALU.subtract),
            waits=[("dve", tok_mrec), ("dve", tok_off)])
    if DEBUG:
        P.dma("sync", "sy", lambda e: e.dma_start(out=taps["t_last"], in_=LAST[:]),
              waits=[("dve", tok_mask)])
        P.dma("sync", "sy", lambda e: e.dma_start(out=taps["t_hi"], in_=SC[:]),
              waits=[("dve", tok_off)])
        P.dma("gpsimd", "gd", lambda e: e.dma_start(out=taps["t_kfm"], in_=KFM[:, 0:512]))
    P.barrier()
    stL.close()
    if PHASES <= 3:
        return _trunc_tail()

    stB = ExitStack()
    QB = stB.enter_context(nc.sbuf_tensor("QB", [128, 8 * 512], BF16))
    XRES = stB.enter_context(nc.sbuf_tensor("XRES", [128, 8 * 512], F32))
    AO = stB.enter_context(nc.sbuf_tensor("AO", [128, 8 * 512], BF16))
    UT = [stB.enter_context(nc.sbuf_tensor(f"UT{i}", [128, 512], BF16)) for i in range(4)]
    RECIP = stB.enter_context(nc.sbuf_tensor("RECIP", [1, 512], F32))
    RB = stB.enter_context(nc.sbuf_tensor("RB", [64, 512], F32))
    SQB = [stB.enter_context(nc.sbuf_tensor("SQB0", [128, 512], F32))] * 2
    ROWS = stB.enter_context(nc.sbuf_tensor("ROWS", [1, 4 * 512], F32))
    AE = stB.enter_context(nc.sbuf_tensor("AE", [128, 512], F32))
    BE = stB.enter_context(nc.sbuf_tensor("BE", [128, 512], F32))
    T1B = stB.enter_context(nc.sbuf_tensor("T1B", [128, 512], F32))
    X1BST = [stB.enter_context(nc.sbuf_tensor(f"X1BST{i}", [128, 512], BF16)) for i in range(2)]

    x1b_free = [0, 0]
    PS_S = [stB.enter_context(nc.psum_tensor(f"PS_S{i}", [128, 512], F32))
            for i in range(4)]
    PS_AV = [stB.enter_context(nc.psum_tensor(f"PS_AV{i}", [65, 512], F32))
             for i in range(2)]
    PS_PJ = [stB.enter_context(nc.psum_tensor(f"PS_PJ{i}", [128, 512], F32))
             for i in range(2)]
    exp_toks = [0, 0, 0, 0]
    stat_toks = [0, 0, 0, 0]
    pj_bank = [0, 0]
    for qst in range(4):
        pf = "pf0" if qst % 2 == 0 else "pf1"
        wq = [("pe", P.cnt["pe"]), ("dve", P.cnt["dve"]), ("act", P.cnt["act"]),
              P.wall("sy"), P.wall("up0"), P.wall("up1")]
        if qst == 0:
            P.dma("sync", pf, lambda e, q=qst: e.dma_start(
                out=QB.ap().rearrange("p (qf t) -> p qf t", t=512),
                in_=qfm_d.rearrange("(qf p) t -> p qf t", p=128)[:, :, q * 512:(q + 1) * 512]),
                waits=wq)
        P.dma("sync", pf, lambda e, q=qst: e.dma_start(
            out=XRES.ap().rearrange("p (dc t) -> p dc t", t=512),
            in_=xres_d.rearrange("(dc p) t -> p dc t", p=128)[:, :, q * 512:(q + 1) * 512]),
            waits=wq)
        wqb = [P.wall(pf)]
        av_toks = {}
        ao_toks = [0, 0]
        tok_rbe = 0
        for hp in range(8):
            def emit_av(kc2, sub, hp2=hp):
                h2 = 2 * hp2 + sub
                av_toks[(kc2, sub)] = MM(
                    PS_AV[sub],
                    VTM[:, kc2 * NH * 65 + h2 * 65: kc2 * NH * 65 + (h2 + 1) * 65],
                    UT[(kc2 * 2 + sub) % 4][:], start=(kc2 == 0), stop=(kc2 == 31),
                    waits=[("act", exp_toks[(kc2 * 2 + sub) % 4])] +
                          ([("dve", ao_toks[sub])] if kc2 == 0 else []))

            for kc in range(32):
                if kc >= 2:
                    emit_av(kc - 2, 0)
                    emit_av(kc - 2, 1)
                for sub in range(2):
                    h = 2 * hp + sub
                    bank = (kc * 2 + sub) % 4
                    wS = [("act", exp_toks[bank]), ("dve", tok_bias),
                          ("dve", stat_toks[bank])] + wqb
                    tok_s = MM(PS_S[bank],
                               KFM[sub * 64:sub * 64 + 64,
                                   hp * T + kc * 128: hp * T + kc * 128 + 128],
                               QB[sub * 64:sub * 64 + 64, hp * 512:(hp + 1) * 512],
                               start=True, stop=True, waits=wS, tp=(sub * 64, 0))
                    bias_col = BIAST[:, (kc % 4) * 128 + h * 8 + kc // 4:
                                     (kc % 4) * 128 + h * 8 + kc // 4 + 1]
                    exp_toks[bank] = P.s(
                        lambda e, k=bank, bc=bias_col: e.activation(
                            UT[k][:], PS_S[k][:], AF.Exp, bias=bc, scale=0.125),
                        waits=[("pe", tok_s)] +
                              ([("pe", av_toks[(kc - 2, sub)])] if (kc - 2, sub) in av_toks
                               else []),
                        sw=False)
            for kc2 in (30, 31):
                emit_av(kc2, 0)
                emit_av(kc2, 1)
            av31 = [av_toks[(31, 0)], av_toks[(31, 1)]]
            av_toks = {}
            for sub in range(2):
                tok_rc = P.v(lambda e, sb=sub: e.reciprocal(RECIP[:], PS_AV[sb][64:65, :]),
                             waits=[("pe", av31[sub])])
                tok_rb = MM(PS_PJ[sub][0:64, :], ONES[0:1, 0:64], RECIP[:], True, True,
                            waits=[("dve", tok_rc), ("dve", tok_rbe),
                                   ("dve", pj_bank[sub])])
                tok_rbe = P.v(lambda e, sb=sub: e.tensor_copy(RB[:], PS_PJ[sb][0:64, :]),
                              waits=[("pe", tok_rb)])
                pj_bank[sub] = tok_rbe
                ao_toks[sub] = P.v(lambda e, kt2=hp, b2=sub * 64, sb=sub: e.tensor_tensor(
                    AO[b2:b2 + 64, kt2 * 512:(kt2 + 1) * 512], PS_AV[sb][0:64, :], RB[:],
                    op=ALU.mult), waits=[("dve", tok_rbe)])
        tok_ao = ao_toks[1]
        if qst < 3:
            pf2 = "pf0" if (qst + 1) % 2 == 0 else "pf1"
            P.dma("sync", pf2, lambda e, q=qst + 1: e.dma_start(
                out=QB.ap().rearrange("p (qf t) -> p qf t", t=512),
                in_=qfm_d.rearrange("(qf p) t -> p qf t", p=128)[:, :, q * 512:(q + 1) * 512]),
                waits=[("pe", P.cnt["pe"])])
        ev_toks = []
        for hc in range(8):
            for hp in range(8):
                tok_pj = MM(PS_PJ[hc % 2],
                            WPROJS[:, hp * 1024 + hc * 128: hp * 1024 + hc * 128 + 128],
                            AO[:, hp * 512:(hp + 1) * 512],
                            start=(hp == 0), stop=(hp == 7),
                            waits=([("dve", tok_ao), ("gd", tok_wproj),
                                    ("dve", pj_bank[hc % 2])] +
                                   ([("dve", ev_toks[hc - 2])] if hc >= 2 else []))
                            if hp == 0 else ())
            ev_toks.append(P.v(lambda e, c=hc: e.tensor_tensor(
                XRES[:, c * 512:(c + 1) * 512], PS_PJ[c % 2][:],
                XRES[:, c * 512:(c + 1) * 512], op=ALU.add),
                waits=[("pe", tok_pj)] + wqb))
        pj_bank = [ev_toks[6], ev_toks[7]]
        sq_tok = [0, 0]
        tok_sq_mm = 0
        ev_all = ev_toks[7]
        wact = [("act", P.cnt["act"])]
        for hc in range(8):
            sq_tok[hc % 2] = P.s(lambda e, c=hc: e.activation(
                SQB[c % 2][:], XRES[:, c * 512:(c + 1) * 512], AF.Square),
                waits=[("dve", ev_toks[hc]), ("pe", tok_sq_mm)])
            MM(PS_S[0][0:1, :], ONES[:, 0:1], XRES[:, hc * 512:(hc + 1) * 512],
               start=(hc == 0), stop=(hc == 7), waits=[("dve", ev_all)] + wact)
            tok_sq_mm = MM(PS_S[1][0:1, :], ONES[:, 0:1], SQB[hc % 2][:],
                           start=(hc == 0), stop=(hc == 7),
                           waits=[("act", sq_tok[hc % 2])] + wact)
        R = lambda i: ROWS[0:1, i * 512:(i + 1) * 512]
        tok_m1 = P.v(lambda e: e.tensor_scalar(R(0), PS_S[0][0:1, :], 1.0 / D, None,
                                               op0=ALU.mult), waits=[("pe", tok_sq_mm)])
        P.v(lambda e: e.tensor_scalar(R(1), PS_S[1][0:1, :], 1.0 / D, None, op0=ALU.mult))
        P.v(lambda e: e.tensor_tensor(R(2), R(0), R(0), op=ALU.mult))
        tok_var = P.v(lambda e: e.tensor_tensor(R(3), R(1), R(2), op=ALU.subtract))
        P.s(lambda e: e.activation(R(1), R(3), AF.Ln, bias=EPSC[0:1, 0:1]),
            waits=[("dve", tok_var), ("dve", tok_eps)])
        tok_rstd = P.s(lambda e: e.activation(R(2), R(1), AF.Exp, scale=-0.5))
        P.v(lambda e: e.tensor_tensor(R(3), R(0), R(2), op=ALU.mult), waits=[("act", tok_rstd)])
        tok_nmb = P.v(lambda e: e.tensor_scalar(R(3), R(3), -1.0, None, op0=ALU.mult))
        tok_bc1 = MM(PS_S[2][:], ONES[0:1, :], R(2), True, True,
                     waits=[("dve", tok_nmb), ("act", P.cnt["act"])])
        tok_ae = P.v(lambda e: e.tensor_copy(AE[:], PS_S[2][:]), waits=[("pe", tok_bc1)])
        tok_bc2 = MM(PS_S[3][:], ONES[0:1, :], R(3), True, True,
                     waits=[("dve", tok_nmb), ("act", P.cnt["act"])])
        tok_be = P.v(lambda e: e.tensor_copy(BE[:], PS_S[3][:]), waits=[("pe", tok_bc2)])
        stat_toks = [tok_m1, tok_m1, tok_ae, tok_be]
        for hc in range(8):
            P.v(lambda e, c=hc: e.tensor_tensor(T1B[:], XRES[:, c * 512:(c + 1) * 512],
                                                AE[:], op=ALU.mult),
                waits=[("dve", tok_ae), ("dve", tok_be)])
            tok_x1 = P.v(lambda e, c=hc: e.tensor_tensor(
                XRES[:, c * 512:(c + 1) * 512], T1B[:], BE[:], op=ALU.add))
            tok_x1b = P.v(lambda e, c=hc: e.tensor_copy(X1BST[c % 2][:],
                                                        XRES[:, c * 512:(c + 1) * 512]),
                          waits=[("dve", tok_x1),
                                 ("dn0" if hc % 2 == 0 else "dn1", x1b_free[hc % 2])])
            P.dma("sync", "sy", lambda e, c=hc, q=qst: e.dma_start(
                out=x1_d[c * 128:(c + 1) * 128, q * 512:(q + 1) * 512],
                in_=XRES[:, c * 512:(c + 1) * 512]), waits=[("dve", tok_x1)])
            x1b_free[hc % 2] = P.dma(
                "gpsimd", "dn0" if hc % 2 == 0 else "dn1",
                lambda e, c=hc, q=qst: e.dma_start(
                    out=x1bf_d[c * 128:(c + 1) * 128, q * 512:(q + 1) * 512],
                    in_=X1BST[c % 2][:]), waits=[("dve", tok_x1b)])
        if DEBUG and qst == 0:
            P.dma("sync", "sy", lambda e: e.dma_start(out=taps["t_x1"], in_=XRES[:]),
                  waits=[("dve", tok_x1)])
            P.dma("sync", "sy", lambda e: e.dma_start(out=taps["t_q"], in_=SQB[0][:]), waits=[("act", P.cnt["act"])])
    P.barrier()
    stB.close()
    stKV.close()
    if PHASES <= 4:
        return _trunc_tail()

    COMB = es.enter_context(nc.sbuf_tensor("COMB", [128, 8 * NTOK], F32))
    stC = ExitStack()
    X1BF = stC.enter_context(nc.sbuf_tensor("X1BF", [128, 8 * NTOK], BF16))
    HACT = stC.enter_context(nc.sbuf_tensor("HACT", [128, 32 * 1024], BF16))
    UPWS = [stC.enter_context(nc.sbuf_tensor(f"UPWS{i}", [128, 8 * 256], BF16)) for i in range(2)]
    DNWS = [stC.enter_context(nc.sbuf_tensor(f"DNWS{i}", [128, 16 * 128], BF16)) for i in range(4)]
    WB = [stC.enter_context(nc.sbuf_tensor(f"WB{i}", [128, 512], F32)) for i in range(2)]
    WT = stC.enter_context(nc.sbuf_tensor("WT", [128, NTOK], F32))
    TEP = stC.enter_context(nc.sbuf_tensor("TEP", [128, 512], F32))
    GW = stC.enter_context(nc.sbuf_tensor("GW", [128, 8 * 8], BF16))
    GBT = stC.enter_context(nc.sbuf_tensor("GBT", [128, 8], F32))
    RS = stC.enter_context(nc.sbuf_tensor("RS", [128, 48], F32))
    UPB = stC.enter_context(nc.sbuf_tensor("UPB", [128, 9 * 32], F32))
    DNB = stC.enter_context(nc.sbuf_tensor("DNB", [128, 9 * 8], F32))
    BRAW = stC.enter_context(nc.sbuf_tensor("BRAW", [32, 128], F32))
    OHT = stC.enter_context(nc.sbuf_tensor("OHT", [128, 8], F32))
    OH1 = stC.enter_context(nc.sbuf_tensor("OH1", [8, 128], F32))

    tok_x1bf = P.dma("sync", "pf0", lambda e: e.dma_start(
        out=X1BF.ap().rearrange("p (dc t) -> p dc t", t=NTOK),
        in_=x1bf_d.rearrange("(dc p) t -> p dc t", p=128)),
        waits=[P.wall("dn0"), P.wall("dn1")])
    wx1 = [P.wall("pf0")]
    tok_gw = P.dma("gpsimd", "gd", lambda e: e.dma_start(
        out=GW.ap().rearrange("p (dc g) -> p dc g", g=8),
        in_=gatew_d.rearrange("(dc p) g -> p dc g", p=128)))

    stR = ExitStack()
    PS_R8 = stR.enter_context(nc.psum_tensor("PS_R8", [128, 8], F32))
    PS_RT = stR.enter_context(nc.psum_tensor("PS_RT", [8, 128], F32))
    PS_B = stR.enter_context(nc.psum_tensor("PS_B", [128, 40], F32))

    tok_gb = P.dma("sync", "pf1", lambda e: e.dma_start(out=RS[0:1, 0:8], in_=gateb_d))
    tok_gbmm = MM(PS_R8[:, 0:8], ONES[0:1, :], RS[0:1, 0:8], True, True,
                  waits=[P.wall("pf1")] + wc)
    tok_gbt = P.v(lambda e: e.tensor_copy(GBT[:], PS_R8[:, 0:8]), waits=[("pe", tok_gbmm)])
    tok_tr_prev = tok_gbt
    for e_i in range(9):
        if e_i < 8:
            src_u = upb_d[e_i, :].rearrange("(m p) -> m p", p=128)
            src_d = downb_d[e_i, :].rearrange("(m p) -> m p", p=128)
        else:
            src_u = sbup_d[0, :].rearrange("(m p) -> m p", p=128)
            src_d = sbdown_d[0, :].rearrange("(m p) -> m p", p=128)
        P.dma("sync", "pf1", lambda e, s=src_u: e.dma_start(out=BRAW[:], in_=s),
              waits=[("dve", tok_tr_prev)])
        TR(PS_B[:, 0:32], BRAW[:], waits=[P.wall("pf1")])
        tok_tru = P.v(lambda e, i=e_i: e.tensor_copy(UPB[:, i * 32:(i + 1) * 32], PS_B[:, 0:32]),
                      waits=[("pe", P.cnt["pe"])])
        P.dma("sync", "pf1", lambda e, s=src_d: e.dma_start(out=BRAW[0:8, :], in_=s),
              waits=[("dve", tok_tru)])
        TR(PS_B[:, 32:40], BRAW[0:8, :], waits=[P.wall("pf1"), ("dve", tok_tru)])
        tok_tr_prev = P.v(lambda e, i=e_i: e.tensor_copy(DNB[:, i * 8:(i + 1) * 8], PS_B[:, 32:40]),
                          waits=[("pe", P.cnt["pe"])])
    tok_wtz = P.v(lambda e: e.memset(WT[:], 0.0))
    tok_oht = P.v(lambda e: e.memset(OHT[:], 0.0))
    tok_wtp = 0
    tok_rs_prev = 0
    for tc in range(16):
        for dc in range(8):
            tok_r = MM(PS_R8[:, 0:8],
                       X1BF[:, dc * NTOK + tc * 128: dc * NTOK + tc * 128 + 128],
                       GW[:, dc * 8:(dc + 1) * 8], start=(dc == 0), stop=(dc == 7),
                       waits=wx1 + [P.wall("gd"), ("dve", tok_gbt), ("dve", tok_rs_prev)])
        tok_a = P.v(lambda e: e.tensor_tensor(RS[:, 16:24], PS_R8[:, 0:8], GBT[:], op=ALU.add),
                    waits=[("pe", tok_r)])
        tok_sg = P.s(lambda e: e.activation(RS[:, 24:32], RS[:, 16:24], AF.Sigmoid),
                     waits=[("dve", tok_a)])
        S8 = RS[:, 24:32]
        P.v(lambda e: e.reduce_max(RS[:, 32:33], S8, axis=AX.X), waits=[("act", tok_sg)])
        P.v(lambda e: e.tensor_scalar(RS[:, 40:48], S8, RS[:, 32:33], None, op0=ALU.is_equal))
        P.v(lambda e: e.scalar_tensor_tensor(RS[:, 40:48], RS[:, 40:48], -1.0e30, S8,
                                             op0=ALU.mult, op1=ALU.add))
        P.v(lambda e: e.reduce_max(RS[:, 33:34], RS[:, 40:48], axis=AX.X))
        P.v(lambda e: e.tensor_tensor(RS[:, 34:35], RS[:, 32:33], RS[:, 33:34], op=ALU.add))
        P.v(lambda e: e.tensor_scalar(RS[:, 34:35], RS[:, 34:35], 1.0e-6, None, op0=ALU.add))
        P.v(lambda e: e.reciprocal(RS[:, 35:36], RS[:, 34:35]))
        P.v(lambda e: e.tensor_scalar(RS[:, 35:36], RS[:, 35:36], 0.1, None, op0=ALU.mult))
        P.v(lambda e: e.tensor_scalar(RS[:, 40:48], S8, RS[:, 33:34], None, op0=ALU.is_ge))
        P.v(lambda e: e.tensor_tensor(RS[:, 40:48], S8, RS[:, 40:48], op=ALU.mult))
        tok_w2 = P.v(lambda e: e.tensor_scalar(RS[:, 40:48], RS[:, 40:48], RS[:, 35:36], None,
                                               op0=ALU.mult))
        tok_rs_prev = tok_w2
        TR(PS_RT[:], RS[:, 40:48], waits=[("dve", tok_w2), ("dve", tok_wtp)])
        tok_wtp = P.v(lambda e, t=tc: e.tensor_copy(WT[0:8, t * 128:(t + 1) * 128],
                                                    PS_RT[0:8, :]),
                      waits=[("pe", P.cnt["pe"]), ("dve", tok_wtz)])
    tok_wt = tok_wtp
    if DEBUG:
        P.dma("sync", "sy", lambda e: e.dma_start(out=taps["t_wt"], in_=WT[0:8, :]),
              waits=[("dve", tok_wt)])
    stR.close()

    stE = ExitStack()
    PS_U = [stE.enter_context(nc.psum_tensor(f"PS_U{i}", [128, 512], F32)) for i in range(4)]
    PS_D = [stE.enter_context(nc.psum_tensor(f"PS_D{i}", [128, 512], F32)) for i in range(2)]
    PS_W = [stE.enter_context(nc.psum_tensor(f"PS_W{i}", [128, 512], F32)) for i in range(2)]

    up_free = [0, 0]
    dn_free = [0, 0, 0, 0]
    gelu_tok = {(i, s): 0 for i in range(2) for s in range(2)}
    gelu_last = [0, 0]
    psd_free = [0, 0]
    wb_free = [0, 0]
    wbe_tok = [0, 0]
    comb_tok = {}
    upq = [(e_i, mh, mp) for e_i in range(NEXP) for mh in range(2) for mp in range(16)]

    def up_load(i):
        e_i, mh, mp = upq[i]
        slot = i % 2
        if e_i < 8:
            src = upw_d[e_i, :, mp * 256:(mp + 1) * 256].rearrange("(dc p) f -> p dc f", p=128)
        else:
            src = swup_d[:, mp * 256:(mp + 1) * 256].rearrange("(dc p) f -> p dc f", p=128)
        P.dma("gpsimd", "up0" if slot == 0 else "up1",
              lambda e, s=src, sl=slot: e.dma_start(
                  out=UPWS[sl].ap().rearrange("p (dc f) -> p dc f", f=256), in_=s),
              waits=[("pe", up_free[slot])])

    if upq:
        up_load(0)
        up_load(1)
    upq_i = 2
    gidx = 0
    tok_oh1 = 0
    for e_i in range(NEXP):
        is_sh = e_i == 8
        if not is_sh:
            if e_i > 0:
                P.v(lambda e, i=e_i: e.memset(OHT[:, i - 1:i], 0.0), waits=[("dve", tok_oh1)])
            tok_c1 = P.v(lambda e, i=e_i: e.memset(OHT[:, i:i + 1], 1.0),
                         waits=[("dve", tok_oht)])
            TR(PS_W[0][0:8, 0:128], OHT[:],
               waits=[("dve", tok_c1), ("dve", wbe_tok[0]), ("dve", wbe_tok[1])])
            tok_oh1 = P.v(lambda e: e.tensor_copy(OH1[:], PS_W[0][0:8, 0:128]),
                          waits=[("pe", P.cnt["pe"])])
        for mh in range(2):
            for mp in range(16):
                slot = gidx % 2
                gidx += 1
                for mi2 in range(2):
                    m = mp * 2 + mi2
                    for stq in range(2):
                        bank = mi2 * 2 + stq
                        for dc in range(8):
                            tok_u = MM(PS_U[bank],
                                       UPWS[slot][:, dc * 256 + mi2 * 128:
                                            dc * 256 + mi2 * 128 + 128],
                                       X1BF[:, dc * NTOK + mh * 1024 + stq * 512:
                                            dc * NTOK + mh * 1024 + stq * 512 + 512],
                                       start=(dc == 0), stop=(dc == 7),
                                       waits=([P.wall("up0" if slot == 0 else "up1"),
                                               ("act", gelu_tok[(mi2, stq)]), ("dve", tok_wt)]
                                              + wx1) if dc == 0 else ())
                        bias_col = (UPB[:, e_i * 32 + m: e_i * 32 + m + 1])
                        gelu_tok[(mi2, stq)] = P.s(
                            lambda e, mm_=m, sq=stq, bc=bias_col, bk=bank: e.activation(
                                HACT[:, mm_ * 1024 + sq * 512: mm_ * 1024 + sq * 512 + 512],
                                PS_U[bk][:], AF.Gelu, bias=bc),
                            waits=[("pe", tok_u)], sw=False)
                        gelu_last[stq] = gelu_tok[(mi2, stq)]
                up_free[slot] = P.cnt["pe"]
                if upq_i < len(upq):
                    up_load(upq_i)
                    upq_i += 1
            def dn_load(hc2, mseg, e2=e_i):
                slot = (hc2 * 2 + mseg) % 4
                if e2 < 8:
                    src = downw_d[e2, mseg * 2048:(mseg + 1) * 2048,
                                  hc2 * 128:(hc2 + 1) * 128].rearrange(
                                      "(mm p) f -> p mm f", p=128)
                else:
                    src = swdown_d[mseg * 2048:(mseg + 1) * 2048,
                                   hc2 * 128:(hc2 + 1) * 128].rearrange(
                                       "(mm p) f -> p mm f", p=128)
                P.dma("gpsimd", ("dn0", "dn1", "dn2", "dn3")[slot],
                      lambda e, s=src, sl=slot: e.dma_start(
                          out=DNWS[sl].ap().rearrange("p (mm f) -> p mm f", f=128), in_=s),
                      waits=[("pe", dn_free[slot])])

            for hc in range(8 if MOE_PART >= 2 else 0):
                if hc == 0:
                    dn_load(0, 0)
                    dn_load(0, 1)
                if hc < 7:
                    dn_load(hc + 1, 0)
                    dn_load(hc + 1, 1)
                if hc == 0 and not is_sh and WBC:
                    for stq in range(2):
                        tok_wb = MM(PS_W[stq][:], OH1[:],
                                    WT[0:8, mh * 1024 + stq * 512: mh * 1024 + stq * 512 + 512],
                                    True, True,
                                    waits=[("dve", tok_oh1), ("dve", tok_wt),
                                           ("dve", wbe_tok[stq])])
                        wbe_tok[stq] = P.v(
                            lambda e, sq=stq: e.tensor_copy(WB[sq][:], PS_W[sq][:]),
                            waits=[("pe", tok_wb), ("dve", wb_free[stq])])
                for stq in range(2):
                    for mseg in range(2):
                        slot = (hc * 2 + mseg) % 4
                        wD = [P.wall(("dn0", "dn1", "dn2", "dn3")[slot]),
                              ("act", gelu_last[stq]), ("dve", psd_free[stq])]
                        for mi in range(16):
                            m = mseg * 16 + mi
                            tok_d = MM(PS_D[stq],
                                       DNWS[slot][:, mi * 128:(mi + 1) * 128],
                                       HACT[:, m * 1024 + stq * 512: m * 1024 + stq * 512 + 512],
                                       start=(m == 0), stop=(m == 31),
                                       waits=wD if mi == 0 else ())
                    key = (hc, mh, stq)
                    o = mh * 1024 + stq * 512
                    if MOE_PART < 3:
                        psd_free[stq] = P.v(lambda e, sq=stq: e.tensor_copy(
                            TEP[:], PS_D[sq][:]), waits=[("pe", tok_d)])
                        comb_tok[key] = psd_free[stq]
                    elif not is_sh:
                        tok_t1 = P.v(lambda e, c=hc, i=e_i, sq=stq: e.tensor_scalar(
                            TEP[:], PS_D[sq][:], DNB[:, i * 8 + c: i * 8 + c + 1], None,
                            op0=ALU.add), waits=[("pe", tok_d)])
                        tok_t2 = P.v(lambda e, sq=stq: e.tensor_tensor(
                            TEP[:], TEP[:], WB[sq][:], op=ALU.mult),
                            waits=[("dve", wbe_tok[stq])])
                        wb_free[stq] = tok_t2
                        if e_i == 0:
                            tok_cw = P.v(lambda e, c=hc, o2=o: e.tensor_copy(
                                COMB[:, c * NTOK + o2: c * NTOK + o2 + 512], TEP[:]))
                        else:
                            tok_cw = P.v(lambda e, c=hc, o2=o: e.tensor_tensor(
                                COMB[:, c * NTOK + o2: c * NTOK + o2 + 512],
                                COMB[:, c * NTOK + o2: c * NTOK + o2 + 512], TEP[:],
                                op=ALU.add), waits=[("dve", comb_tok.get(key, 0))])
                    elif is_sh:
                        tok_t1 = P.v(lambda e, c=hc, sq=stq: e.tensor_scalar(
                            TEP[:], PS_D[sq][:], DNB[:, 8 * 8 + c: 8 * 8 + c + 1], 0.1,
                            op0=ALU.add, op1=ALU.mult), waits=[("pe", tok_d)])
                        tok_cw = P.v(lambda e, c=hc, o2=o: e.tensor_tensor(
                            COMB[:, c * NTOK + o2: c * NTOK + o2 + 512],
                            COMB[:, c * NTOK + o2: c * NTOK + o2 + 512], TEP[:],
                            op=ALU.add), waits=[("dve", comb_tok.get(key, 0))])
                    if MOE_PART >= 3:
                        comb_tok[key] = tok_cw
                        psd_free[stq] = tok_t1
                dn_free[(hc * 2) % 4] = P.cnt["pe"]
                dn_free[(hc * 2 + 1) % 4] = P.cnt["pe"]
    stE.close()
    P.barrier()
    stC.close()
    if PHASES <= 5:
        return _trunc_tail()

    stD = ExitStack()
    X1C = [stD.enter_context(nc.sbuf_tensor(f"X1C{i}", [128, 8 * 128], F32)) for i in range(2)]
    FTM = [stD.enter_context(nc.sbuf_tensor(f"FTM{i}", [128, 1024], F32)) for i in range(2)]
    X1TM = [stD.enter_context(nc.sbuf_tensor(f"X1TM{i}", [128, 1024], F32)) for i in range(2)]
    X2 = [stD.enter_context(nc.sbuf_tensor(f"X2_{i}", [128, 1024], F32)) for i in range(2)]
    SQ2 = stD.enter_context(nc.sbuf_tensor("SQ2", [128, 1024], F32))
    DS = stD.enter_context(nc.sbuf_tensor("DS", [128, 8], F32))
    PS_DT = [stD.enter_context(nc.psum_tensor(f"PS_DT{i}", [128, 128], F32)) for i in range(2)]

    x1c_free = [0, 0]
    d_free = [0, 0]

    pe_at_d = P.cnt["pe"]

    def d_load(tc):
        slot = tc % 2
        P.dma("sync", "pf0" if slot == 0 else "pf1", lambda e, t=tc, sl=slot: e.dma_start(
            out=X1C[sl].ap().rearrange("p (dc f) -> p dc f", f=128),
            in_=x1_d[:, t * 128:(t + 1) * 128].rearrange("(dc p) f -> p dc f", p=128)),
            waits=[P.wall("sy"), ("pe", max(pe_at_d, x1c_free[slot]))])

    d_load(0)
    tr_free = [0, 0]
    tok_ln = 0
    tok_s2 = 0
    for tc in range(16):
        slot = tc % 2
        if tc < 15:
            d_load(tc + 1)
        ev_last = 0
        for hc in range(8):
            TR(PS_DT[hc % 2], COMB[:, hc * NTOK + tc * 128: hc * NTOK + tc * 128 + 128],
               waits=[("dve", tr_free[hc % 2])])
            ev_last = P.v(lambda e, c=hc, sl=slot: e.tensor_copy(
                FTM[sl][:, c * 128:(c + 1) * 128], PS_DT[c % 2][:]),
                waits=[("pe", P.cnt["pe"]), ("dve", d_free[slot]),
                       P.wall("out")])
            tr_free[hc % 2] = ev_last
        for dc in range(8):
            TR(PS_DT[dc % 2], X1C[slot][:, dc * 128:(dc + 1) * 128],
               waits=[P.wall("pf0" if slot == 0 else "pf1"), ("dve", tr_free[dc % 2])])
            ev_last = P.v(lambda e, c=dc, sl=slot: e.tensor_copy(
                X1TM[sl][:, c * 128:(c + 1) * 128], PS_DT[c % 2][:]),
                waits=[("pe", P.cnt["pe"])])
            tr_free[dc % 2] = ev_last
        x1c_free[slot] = P.cnt["pe"]
        P.v(lambda e, sl=slot: e.reduce_max(DS[:, 0:1], FTM[sl][:], axis=AX.X,
                                            apply_absolute_value=True),
            waits=[("dve", ev_last)])
        P.v(lambda e: e.tensor_scalar(DS[:, 0:1], DS[:, 0:1], 1.0e-6, None, op0=ALU.add))
        tok_r = P.v(lambda e: e.reciprocal(DS[:, 1:2], DS[:, 0:1]))
        tok_x2 = P.v(lambda e, sl=slot: e.scalar_tensor_tensor(
            X2[sl][:], FTM[sl][:], DS[:, 1:2], X1TM[sl][:], op0=ALU.mult, op1=ALU.add),
            waits=[("dve", tok_r)])
        d_free[slot] = tok_x2
        tok_sq = P.s(lambda e, sl=slot: e.activation(SQ2[:], X2[sl][:], AF.Square),
                     waits=[("dve", tok_x2), ("dve", tok_s2)])
        P.v(lambda e, sl=slot: e.reduce_sum(DS[:, 2:3], X2[sl][:], axis=AX.X))
        tok_s2 = P.v(lambda e: e.reduce_sum(DS[:, 3:4], SQ2[:], axis=AX.X),
                     waits=[("act", tok_sq)])
        P.v(lambda e: e.tensor_scalar(DS[:, 4:5], DS[:, 2:3], 1.0 / D, None, op0=ALU.mult))
        P.v(lambda e: e.tensor_scalar(DS[:, 5:6], DS[:, 3:4], 1.0 / D, None, op0=ALU.mult))
        P.v(lambda e: e.tensor_tensor(DS[:, 6:7], DS[:, 4:5], DS[:, 4:5], op=ALU.mult))
        tok_v = P.v(lambda e: e.tensor_tensor(DS[:, 5:6], DS[:, 5:6], DS[:, 6:7],
                                              op=ALU.subtract))
        P.s(lambda e: e.activation(DS[:, 6:7], DS[:, 5:6], AF.Ln, bias=EPSC[:]),
            waits=[("dve", tok_v), ("dve", tok_eps)])
        tok_rs = P.s(lambda e: e.activation(DS[:, 7:8], DS[:, 6:7], AF.Exp, scale=-0.5))
        tok_ln = P.v(lambda e, sl=slot: e.tensor_scalar(
            X2[sl][:], X2[sl][:], DS[:, 4:5], DS[:, 7:8], op0=ALU.subtract, op1=ALU.mult),
            waits=[("act", tok_rs)])
        P.dma("sync", "out", lambda e, t=tc, sl=slot: e.dma_start(
            out=y_d[t * 128:(t + 1) * 128, :], in_=X2[sl][:]), waits=[("dve", tok_ln)])
    P.emit("sync", lambda e: None,
           waits=[("out", P.cnt["out"]), P.wall("sy"), P.wall("gd"),
                  P.wall("up0"), P.wall("up1"), P.wall("dn0"), P.wall("dn1"),
                  P.wall("dn2"), P.wall("dn3"),
                  P.wall("x0"), P.wall("x1"), P.wall("pf0"), P.wall("pf1"),
                  ("dve", P.cnt["dve"]), ("act", P.cnt["act"]), ("pe", P.cnt["pe"])])
    P.flush()
    stD.close()
    es.close()
    return nc, P.dma_hints


_NC_CACHE = None


def kernel(L0_attn_x, L0_attn_W_attn, L0_attn_W_proj, L0_ffn_gate_w, L0_ffn_bias,
           L0_ffn_up_w, L0_ffn_up_b, L0_ffn_down_w, L0_ffn_down_b, L0_ffn_sw_up,
           L0_ffn_sb_up, L0_ffn_sw_down, L0_ffn_sb_down):
    global _NC_CACHE
    x = np.ascontiguousarray(np.asarray(L0_attn_x, dtype=np.float32))
    common = dict(
        wattn=np.ascontiguousarray(np.asarray(L0_attn_W_attn, dtype=np.float32)),
        wproj=np.ascontiguousarray(np.asarray(L0_attn_W_proj, dtype=np.float32)),
        gatew=np.ascontiguousarray(np.asarray(L0_ffn_gate_w, dtype=np.float32)),
        gateb=np.ascontiguousarray(np.reshape(np.asarray(L0_ffn_bias, dtype=np.float32), (1, NE))),
        upw=np.ascontiguousarray(np.asarray(L0_ffn_up_w, dtype=np.float32)),
        upb=np.ascontiguousarray(np.asarray(L0_ffn_up_b, dtype=np.float32)),
        downw=np.ascontiguousarray(np.asarray(L0_ffn_down_w, dtype=np.float32)),
        downb=np.ascontiguousarray(np.asarray(L0_ffn_down_b, dtype=np.float32)),
        swup=np.ascontiguousarray(np.asarray(L0_ffn_sw_up, dtype=np.float32)),
        sbup=np.ascontiguousarray(np.reshape(np.asarray(L0_ffn_sb_up, dtype=np.float32), (1, DI))),
        swdown=np.ascontiguousarray(np.asarray(L0_ffn_sw_down, dtype=np.float32)),
        sbdown=np.ascontiguousarray(np.reshape(np.asarray(L0_ffn_sb_down, dtype=np.float32), (1, D))),
        consts=_host_consts(),
    )
    in_maps = []
    for c in range(8):
        b, hf = c // 2, c % 2
        m = dict(common)
        m["x"] = np.ascontiguousarray(x[b])
        m["xq"] = np.ascontiguousarray(x[b, hf * NTOK:(hf + 1) * NTOK, :])
        in_maps.append(m)
    if _NC_CACHE is None:
        _nc1, hints = build_nc()
        del _nc1
        _NC_CACHE, _ = build_nc(hints)
    res = run_bass_kernel_spmd(_NC_CACHE, in_maps, core_ids=list(range(8)))
    out = np.empty((4, T, D), dtype=np.float32)
    for c in range(8):
        b, hf = c // 2, c % 2
        out[b, hf * NTOK:(hf + 1) * NTOK, :] = np.asarray(res.results[c]["y"])
    return out


# version 15
# speedup vs baseline: 1.1058x; 1.1058x over previous
import numpy as np
from contextlib import ExitStack

import concourse.bass as bass
import concourse.mybir as mybir
from concourse.bass_utils import run_bass_kernel_spmd

F32 = mybir.dt.float32
BF16 = mybir.dt.bfloat16
AF = mybir.ActivationFunctionType
ALU = mybir.AluOpType
AX = mybir.AxisListType

D = 1024
T = 4096
NH = 16
KKEEP = 391
NSEL = KKEEP - 64
NTOK = 2048
NE = 8
DI = 4096
NBIS = 42

DEBUG = False
PHASES = 6
NEXP = 9
MOE_PART = 3
WBC = True
ENGINES = ("tensor", "vector", "scalar", "gpsimd", "sync")


class Prog:
    def __init__(self, nc, blk, sems, dma_hints=None):
        self.nc, self.blk = nc, blk
        self.sems = sems
        self.cnt = {s: 0 for s in sems}
        self.q = {e: [] for e in ENGINES}
        self.known = {e: {s: 0 for s in sems} for e in ENGINES}
        self.measuring = dma_hints is None
        self.dma_hints = {s: [] for s in sems} if self.measuring else dma_hints
        self.hint_idx = {s: 0 for s in sems}

    def emit(self, eng, fn, waits=(), inc=None, dma_sem=None):
        real = []
        for s, v in waits:
            assert v is not None, f"None wait token on {s} for {eng}"
            if v > 0 and self.known[eng][s] < v:
                real.append((s, v))
                self.known[eng][s] = v
        tok = None
        if inc is not None:
            s, n = inc
            self.cnt[s] += n
            tok = self.cnt[s]
        self.q[eng].append((fn, real, inc, dma_sem))
        return tok

    def pe(self, fn, waits=(), inc=True):
        return self.emit("tensor", fn, waits, ("pe", 1) if inc else None)

    def v(self, fn, waits=(), sw=True):
        w = list(waits) + ([("dve", self.cnt["dve"])] if sw else [])
        return self.emit("vector", fn, w, ("dve", 1))

    def s(self, fn, waits=(), sw=True):
        w = list(waits) + ([("act", self.cnt["act"])] if sw else [])
        return self.emit("scalar", fn, w, ("act", 1))

    def dma(self, queue, sem, fn, waits=()):
        if self.measuring:
            inc = 16
        else:
            inc = self.dma_hints[sem][self.hint_idx[sem]]
            self.hint_idx[sem] += 1
        return self.emit(queue, fn, waits, (sem, inc), dma_sem=sem)

    def wall(self, name):
        return (name, self.cnt[name])

    def flush(self):
        for e in ENGINES:
            items = self.q[e]
            if not items:
                continue
            sems = self.sems

            nc = self.nc
            measuring = self.measuring
            hints = self.dma_hints

            def run(eng, items=tuple(items)):
                for fn, waits, inc, dma_sem in items:
                    for s, v in waits:
                        eng.wait_ge(sems[s], v)
                    n0 = nc.n_instructions() if (measuring and dma_sem) else 0
                    ins = fn(eng)
                    if inc is not None:
                        ins.then_inc(sems[inc[0]], 16 if dma_sem else inc[1])
                        if measuring and dma_sem:
                            names = nc.inst_names()[n0:]
                            nupd = 0
                            for nm2 in names:
                                i2 = nc.find_inst_by_name(nm2)
                                si = i2.sync_info if i2 is not None else None
                                if si is not None and any(
                                        u.sync_type == 'semaphore' and u.ant_name == dma_sem
                                        for u in si.on_update):
                                    nupd += 1
                            hints[dma_sem].append(16 * max(1, nupd))
            getattr(self.blk, e)(run)
            self.q[e] = []

    def barrier(self):
        snap = [(s, self.cnt[s]) for s in self.sems if self.cnt[s] > 0]
        for e in ENGINES:
            for s, v in snap:
                if self.known[e][s] < v:
                    self.q[e].append((lambda eng: None, [(s, v)], None, None))
                    self.known[e][s] = v
        self.flush()


def _host_consts():
    C = np.zeros((128, 404), dtype=np.float32)
    C[:, 0:128] = np.eye(128, dtype=np.float32)
    C[:, 128:256] = 1.0
    C[0:64, 256] = 1.0
    C[64:128, 257] = 1.0
    for p in range(128):
        C[p, 258 + p // 8] = 1.0
        g = p // 8
        C[p, 274 + 8 * g:274 + 8 * g + 8] = 1.0
    C[0:64, 402] = 1.0
    return C


def _host_consts2():
    C2 = np.zeros((32, 2048), dtype=np.float32)
    for r in range(32):
        C2[r, :] = (r % 2) * 2048 + np.arange(2048) + 1.0
    return C2


def build_nc(dma_hints=None):
    nc = bass.Bass("TRN2", target_bir_lowering=False)
    ap = lambda t: t.ap()
    x_d = ap(nc.dram_tensor("x", [T, D], F32, kind="ExternalInput"))
    xq_d = ap(nc.dram_tensor("xq", [NTOK, D], F32, kind="ExternalInput"))
    wattn_d = ap(nc.dram_tensor("wattn", [D, 3 * D], F32, kind="ExternalInput"))
    wproj_d = ap(nc.dram_tensor("wproj", [D, D], F32, kind="ExternalInput"))
    gatew_d = ap(nc.dram_tensor("gatew", [D, NE], F32, kind="ExternalInput"))
    gateb_d = ap(nc.dram_tensor("gateb", [1, NE], F32, kind="ExternalInput"))
    upw_d = ap(nc.dram_tensor("upw", [NE, D, DI], F32, kind="ExternalInput"))
    upb_d = ap(nc.dram_tensor("upb", [NE, DI], F32, kind="ExternalInput"))
    downw_d = ap(nc.dram_tensor("downw", [NE, DI, D], F32, kind="ExternalInput"))
    downb_d = ap(nc.dram_tensor("downb", [NE, D], F32, kind="ExternalInput"))
    swup_d = ap(nc.dram_tensor("swup", [D, DI], F32, kind="ExternalInput"))
    sbup_d = ap(nc.dram_tensor("sbup", [1, DI], F32, kind="ExternalInput"))
    swdown_d = ap(nc.dram_tensor("swdown", [DI, D], F32, kind="ExternalInput"))
    sbdown_d = ap(nc.dram_tensor("sbdown", [1, D], F32, kind="ExternalInput"))
    consts_d = ap(nc.dram_tensor("consts", [128, 404], F32, kind="ExternalInput"))
    consts2_d = ap(nc.dram_tensor("consts2", [32, 2048], F32, kind="ExternalInput"))
    y_d = ap(nc.dram_tensor("y", [NTOK, D], F32, kind="ExternalOutput"))
    qfm_d = ap(nc.dram_tensor("qfm_s", [D, NTOK], BF16, kind="Internal"))
    xres_d = ap(nc.dram_tensor("xres_s", [D, NTOK], F32, kind="Internal"))
    x1_d = ap(nc.dram_tensor("x1_s", [D, NTOK], F32, kind="Internal"))
    x1bf_d = ap(nc.dram_tensor("x1bf_s", [D, NTOK], BF16, kind="Internal"))
    ltc_d = ap(nc.dram_tensor("ltc_s", [16, 32 * 128], F32, kind="Internal"))
    kv_d = ap(nc.dram_tensor("kv_s", [T, NH * 128], BF16, kind="Internal"))
    taps = {}
    if DEBUG:
        for nm, shp in [("t_last", [128, 512]), ("t_hi", [128, 16]),
                        ("t_x1", [128, 8 * 512]), ("t_wt", [8, NTOK]),
                        ("t_kfm", [128, 512]), ("t_q", [128, 512])]:
            taps[nm] = ap(nc.dram_tensor(nm, shp, F32, kind="ExternalOutput"))

    es = ExitStack()
    sem_names = ["pe", "dve", "act", "sy", "gd", "x0", "x1", "up0", "up1",
                 "dn0", "dn1", "dn2", "dn3", "pf0", "pf1", "out"]
    sems = {n: es.enter_context(nc.semaphore(n)) for n in sem_names}

    CONSTS = es.enter_context(nc.sbuf_tensor("CONSTS", [128, 404], F32))
    QNBUF = es.enter_context(nc.sbuf_tensor("QNBUF", [128, 64], F32))
    KNBUF = es.enter_context(nc.sbuf_tensor("KNBUF", [128, 64], F32))
    XLASTBF = es.enter_context(nc.sbuf_tensor("XLASTBF", [128, 8], BF16))
    QBD = es.enter_context(nc.sbuf_tensor("QBD", [128, 8 * 16], BF16))
    EPSC = es.enter_context(nc.sbuf_tensor("EPSC", [128, 1], F32))
    IDENTB = es.enter_context(nc.sbuf_tensor("IDENTB", [128, 128], BF16))
    _open_stacks = []
    stKV = ExitStack()
    _open_stacks.append(stKV)
    WPROJS = stKV.enter_context(nc.sbuf_tensor("WPROJS", [128, 8 * 1024], BF16))
    IOTA2P = stKV.enter_context(nc.sbuf_tensor("IOTA2P", [32, 2048], F32))
    LTC = stKV.enter_context(nc.sbuf_tensor("LTC", [16, 32 * 128], F32))
    KVG = stKV.enter_context(nc.sbuf_tensor("KVG", [128, NH * 5 * 130], BF16))
    KFG = stKV.enter_context(nc.sbuf_tensor("KFG", [128, 8 * 5 * 128], BF16))

    blk = es.enter_context(nc.Block())
    P = Prog(nc, blk, sems, dma_hints)
    IDENT = CONSTS[:, 0:128]
    ONES = CONSTS[:, 128:256]
    BLK2 = CONSTS[:, 256:258]
    SAME = CONSTS[:, 274:402]
    RECPAT = CONSTS[:, 402:403]

    from concourse.bass_types import AP as _APT

    def _ap(x):
        return x if isinstance(x, _APT) else x[:]

    def MM(ps, lhsT, rhs, start, stop, waits=(), tp=None):
        return P.pe(lambda t, a=_ap(ps), l=_ap(lhsT), r=_ap(rhs), s1=start, s2=stop, q=tp:
                    t.matmul(a, l, r, start=s1, stop=s2, tile_position=q), waits, inc=True)

    def TR(ps, src, waits=()):
        sap = _ap(src)
        n = sap.shape[0]
        if sap.tensor.dtype == BF16:
            idt, w2 = IDENTB, [("gd", tok_idb)]
        else:
            idt, w2 = CONSTS, []
        return P.pe(lambda t, a=_ap(ps), s=sap, i=idt[0:n, 0:n]: t.transpose(a, s, i),
                    list(waits) + w2)


    def _trunc_tail():
        for st_ in reversed(_open_stacks):
            st_.close()
        OUTZ = es.enter_context(nc.sbuf_tensor("OUTZ", [128, 1024], F32))
        tok_z = P.v(lambda e: e.memset(OUTZ[:], 0.0))
        for tc in range(16):
            P.dma("sync", "out", lambda e, t=tc: e.dma_start(
                out=y_d[t * 128:(t + 1) * 128, :], in_=OUTZ[:]), waits=[("dve", tok_z)])
        P.emit("sync", lambda e: None,
               waits=[("out", P.cnt["out"]), P.wall("sy"), P.wall("gd"),
                      P.wall("up0"), P.wall("up1"), P.wall("dn0"), P.wall("dn1"),
                      P.wall("dn2"), P.wall("dn3"),
                      P.wall("x0"), P.wall("x1"), P.wall("pf0"), P.wall("pf1"),
                      ("dve", P.cnt["dve"]), ("act", P.cnt["act"]), ("pe", P.cnt["pe"])])
        P.flush()
        es.close()
        return nc, P.dma_hints

    P.dma("sync", "sy", lambda e: e.dma_start(out=CONSTS[:], in_=consts_d))
    tok_consts = P.dma("sync", "sy", lambda e: e.dma_start(out=IOTA2P[:], in_=consts2_d),
                       waits=[P.wall("sy")])
    tok_idb = P.dma("gpsimd", "gd", lambda e: e.dma_start(out=IDENTB[:], in_=consts_d[:, 0:128]))
    tok_eps = P.v(lambda e: e.memset(EPSC[:], 1.0e-5))
    tok_qnz = P.v(lambda e: e.memset(QNBUF[:], 0.0))
    tok_knz = P.v(lambda e: e.memset(KNBUF[:], 0.0))
    wc = [("sy", tok_consts)]

    stA = ExitStack()
    psbA = lambda name, shape, dt=F32: stA.enter_context(nc.psum_tensor(name, shape, dt))
    WQ1 = stA.enter_context(nc.sbuf_tensor("WQ1", [128, 8 * 1024], BF16))
    WKV = stA.enter_context(nc.sbuf_tensor("WKV", [128, 8 * 2048], BF16))
    XT4 = [stA.enter_context(nc.sbuf_tensor(f"XT4_{i}", [128, D], F32)) for i in range(2)]
    XFMF = stA.enter_context(nc.sbuf_tensor("XFMF", [128, 8 * 512], F32))
    XFM = stA.enter_context(nc.sbuf_tensor("XFM", [128, 8 * 512], BF16))
    QST2 = [stA.enter_context(nc.sbuf_tensor(f"QST2_{i}", [128, 512], BF16)) for i in range(2)]
    KVST = [stA.enter_context(nc.sbuf_tensor(f"KVST{i}", [128, 512], BF16)) for i in range(4)]
    SQA = stA.enter_context(nc.sbuf_tensor("SQA", [128, 512], F32))
    QLASTBF = stA.enter_context(nc.sbuf_tensor("QLASTBF", [128, 8], BF16))
    XROWB = stA.enter_context(nc.sbuf_tensor("XROWB", [1, 1024], BF16))
    U8 = stA.enter_context(nc.sbuf_tensor("U8", [128, 8 * 16], BF16))
    WKT = [stA.enter_context(nc.sbuf_tensor(f"WKT{i}", [128, 128], BF16)) for i in range(2)]
    LT8 = [stA.enter_context(nc.sbuf_tensor(f"LT8_{i}", [128, 16], F32)) for i in range(2)]
    PS_TRF = [psbA(f"PS_TRF{i}", [128, 128]) for i in range(2)]
    PS_MM = [psbA(f"PS_MM{i}", [128, 512]) for i in range(2)]
    PS_N = psbA("PS_N", [2, 512])
    PS_Q1 = psbA("PS_Q1", [128, 16])

    P.dma("gpsimd", "gd", lambda e: e.dma_start(out=XROWB[:], in_=x_d[T - 1:T, :]))
    tok_wq = P.dma("gpsimd", "gd", lambda e: e.dma_start(
        out=WQ1.ap().rearrange("p (dc f) -> p dc f", f=1024),
        in_=wattn_d[:, 0:1024].rearrange("(dc p) f -> p dc f", p=128)))
    xt_free = [0, 0]
    tr_bank = [0, 0]
    mm_bank = [0, 0]
    sq_free = 0
    psn_free = 0
    xstash_sy = 0

    half_i = [0]
    pending = []

    def x_load_next():
        if not pending:
            return
        rows = pending.pop(0)
        slot = half_i[0] % 2
        half_i[0] += 1
        src = rows.rearrange("(tc p) d -> p tc d", p=128)
        P.dma("sync", "x0" if slot == 0 else "x1",
              lambda e, s=src, sl=slot: e.dma_start(
                  out=XT4[sl].ap().rearrange("p (tc d) -> p tc d", d=D), in_=s),
              waits=[("pe", xt_free[slot])])

    def norm_stat(src_bf, buf, col, wtok):
        nonlocal sq_free, psn_free
        t_sq = P.s(lambda e, s=src_bf: e.activation(SQA[:], s, AF.Square),
                   waits=wtok + [("pe", sq_free)])
        tok = MM(PS_N[:], BLK2, SQA[:], True, True,
                 waits=[("act", t_sq), ("dve", psn_free)] + wc)
        sq_free = tok
        psn_free = P.v(lambda e, b=buf, c=col: e.reduce_max(
            b[0:2, c:c + 1], PS_N[0:2, :], axis=AX.X), waits=[("pe", tok)])
        return psn_free

    proc_i = [0]

    def transpose_pass(make_bf16):
        ev = 0
        for h2 in range(4):
            slot = proc_i[0] % 2
            sem = "x0" if slot == 0 else "x1"
            proc_i[0] += 1
            for dcc in range(8):
                src = XT4[slot][:, dcc * 128: dcc * 128 + 128]
                TR(PS_TRF[dcc % 2], src,
                   waits=[P.wall(sem), ("dve", tr_bank[dcc % 2])] + wc)
                ev = P.v(lambda e, b2=dcc % 2, tc=h2, dc=dcc: e.tensor_copy(
                    XFMF[:, dc * 512 + tc * 128: dc * 512 + tc * 128 + 128], PS_TRF[b2][:]),
                    waits=[("pe", P.cnt["pe"]), ("sy", xstash_sy)])
                tr_bank[dcc % 2] = ev
            xt_free[slot] = P.cnt["pe"]
            x_load_next()
        if make_bf16:
            tok_bf = P.v(lambda e: e.tensor_copy(XFM[:], XFMF[:]), waits=[("dve", ev)])
            return ev, tok_bf
        return ev, ev

    tok_xlt = 0
    stXL = ExitStack()
    PS_XLB = stXL.enter_context(nc.psum_tensor("PS_XLB", [128, 16], BF16))
    for dc in range(8):
        TR(PS_XLB[:, 2 * dc:2 * dc + 1], XROWB[0:1, dc * 128:(dc + 1) * 128],
           waits=[P.wall("gd"), ("dve", tok_xlt)] + wc)
        tok_xlt = P.v(lambda e, d=dc: e.tensor_copy(XLASTBF[:, d:d + 1],
                                                    PS_XLB[:, 2 * d:2 * d + 1]),
                      waits=[("pe", P.cnt["pe"])])
    stXL.close()

    for qst in range(4):
        for h2 in range(4):
            pending.append(xq_d[qst * 512 + h2 * 128: qst * 512 + (h2 + 1) * 128, :])
    for st in range(8):
        for h2 in range(4):
            pending.append(x_d[st * 512 + h2 * 128: st * 512 + (h2 + 1) * 128, :])
    x_load_next()
    x_load_next()
    for qst in range(4):
        ev, tok_bf = transpose_pass(True)
        for qfc in range(8):
            for dc in range(8):
                tok_mm = MM(PS_MM[qfc % 2],
                            WQ1[:, dc * 1024 + qfc * 128: dc * 1024 + qfc * 128 + 128],
                            XFM[:, dc * 512:(dc + 1) * 512],
                            start=(dc == 0), stop=(dc == 7),
                            waits=[("dve", tok_bf), ("dve", mm_bank[qfc % 2]),
                                   ("gd", tok_wq)])
            tok_ev = P.v(lambda e, qf=qfc: e.tensor_copy(
                QST2[qf % 2][:], PS_MM[qf % 2][:]),
                waits=[("pe", tok_mm), P.wall("up0" if qfc % 2 == 0 else "up1")])
            mm_bank[qfc % 2] = tok_ev
            norm_stat(QST2[qfc % 2][:], QNBUF, qst * 8 + qfc, [("dve", tok_ev)])
            P.dma("gpsimd", "up0" if qfc % 2 == 0 else "up1",
                  lambda e, q=qst, qf=qfc: e.dma_start(
                      out=qfm_d[qf * 128:(qf + 1) * 128, q * 512:(q + 1) * 512],
                      in_=QST2[qf % 2][:]), waits=[("dve", tok_ev)])
        P.dma("sync", "sy", lambda e, q=qst: e.dma_start(
            out=xres_d.rearrange("(dc p) t -> p dc t", p=128)[:, :, q * 512:(q + 1) * 512],
            in_=XFMF.ap().rearrange("p (dc t) -> p dc t", t=512)),
            waits=[("dve", tok_bf)])
        xstash_sy = P.cnt["sy"]
    tok_qle = 0
    for qfc in range(8):
        for dc in range(8):
            tok_mm = MM(PS_Q1[:, 0:1],
                        WQ1[:, dc * 1024 + qfc * 128: dc * 1024 + qfc * 128 + 128],
                        XLASTBF[:, dc:dc + 1], start=(dc == 0), stop=(dc == 7),
                        waits=[("dve", tok_xlt), ("dve", tok_qle)])
        tok_qle = P.v(lambda e, qf=qfc: e.tensor_copy(QLASTBF[:, qf:qf + 1], PS_Q1[:, 0:1]),
                      waits=[("pe", tok_mm)])
    tok_qbd = P.v(lambda e: e.memset(QBD[:], 0.0))
    for dc in range(8):
        P.v(lambda e, d=dc: e.tensor_copy(QBD[0:64, d * 16 + 2 * d: d * 16 + 2 * d + 1],
                                          QLASTBF[0:64, d:d + 1]),
            waits=[("dve", tok_qle), ("dve", tok_qbd)])
        tok_qbd = P.v(lambda e, d=dc: e.tensor_copy(
            QBD[64:128, d * 16 + 2 * d + 1: d * 16 + 2 * d + 2],
            QLASTBF[64:128, d:d + 1]))

    tok_wkv = P.dma("gpsimd", "gd", lambda e: e.dma_start(
        out=WKV.ap().rearrange("p (dc f) -> p dc f", f=2048),
        in_=wattn_d[:, 1024:3072].rearrange("(dc p) f -> p dc f", p=128)),
        waits=[("pe", P.cnt["pe"])])
    tok_u = 0
    wkt_bank = [0, 0]
    stWK = ExitStack()
    PS_WKT = [stWK.enter_context(nc.psum_tensor(f"PS_WKT{i}", [128, 128], BF16))
              for i in range(2)]
    for dc in range(8):
        for kc in range(8):
            TR(PS_WKT[kc % 2], WKV[:, dc * 2048 + kc * 128: dc * 2048 + kc * 128 + 128],
               waits=[("gd", tok_wkv), ("dve", wkt_bank[kc % 2])])
            tok_wt2 = P.v(lambda e, k=kc % 2: e.tensor_copy(WKT[k][:], PS_WKT[k][:]),
                          waits=[("pe", P.cnt["pe"])])
            wkt_bank[kc % 2] = tok_wt2
            tok_umm = MM(PS_Q1[:], WKT[kc % 2][:], QBD[:, kc * 16:(kc + 1) * 16],
                         start=(kc == 0), stop=(kc == 7),
                         waits=[("dve", tok_wt2), ("dve", tok_qbd), ("dve", tok_u)])
        tok_u = P.v(lambda e, d=dc: e.tensor_copy(U8[:, d * 16:(d + 1) * 16], PS_Q1[:]),
                    waits=[("pe", tok_umm)])
    stWK.close()

    PS_T16A = [psbA(f"PS_T16A{i}", [16, 128]) for i in range(2)]
    kvst_free = [0, 0, 0, 0]
    tok_lt8 = [0, 0]
    lt_bank = [0, 0]
    for st in range(8):
        ev, tok_bf = transpose_pass(True)
        for tc4 in range(4):
            lhs_tok = tok_bf
            for part in range(2):
                for dc in range(8):
                    w = [("gd", tok_wkv), ("dve", lhs_tok)]
                    if dc == 0:
                        w += [("dve", kvst_free[0]), ("dve", kvst_free[1]),
                              ("dve", kvst_free[2]), ("dve", kvst_free[3])]
                    lhsT = XFM[:, dc * 512 + tc4 * 128: dc * 512 + tc4 * 128 + 128]
                    MM(PS_MM[0], lhsT, WKV[:, dc * 2048 + part * 1024: dc * 2048 + part * 1024 + 512],
                       start=(dc == 0), stop=(dc == 7), waits=w)
                    tok_mm = MM(PS_MM[1], lhsT,
                                WKV[:, dc * 2048 + part * 1024 + 512: dc * 2048 + part * 1024 + 1024],
                                start=(dc == 0), stop=(dc == 7))
                for half in range(2):
                    tok_ev = P.v(lambda e, p2=part, h3=half: e.tensor_copy(
                        KVST[p2 * 2 + h3][:], PS_MM[h3][:]),
                        waits=[("pe", tok_mm)])
                    kvst_free[part * 2 + half] = tok_ev
                    P.dma("sync", "out", lambda e, s2=st, t4=tc4, p2=part, h3=half: e.dma_start(
                        out=kv_d.rearrange("t (h z) -> t h z", z=128)[
                            s2 * 512 + t4 * 128: s2 * 512 + (t4 + 1) * 128,
                            h3 * 8:(h3 + 1) * 8, p2 * 64:(p2 + 1) * 64],
                        in_=KVST[p2 * 2 + h3].ap().rearrange("p (h d2) -> p h d2", d2=64)),
                        waits=[("dve", tok_ev)])
            for dc in range(8):
                tok_lmm = MM(PS_Q1[:],
                             XFM[:, dc * 512 + tc4 * 128: dc * 512 + tc4 * 128 + 128],
                             U8[:, dc * 16:(dc + 1) * 16],
                             start=(dc == 0), stop=(dc == 7),
                             waits=[("dve", tok_u), ("dve", tok_lt8[0]),
                                    ("dve", tok_lt8[1])] if dc == 0 else ())
            kb = (st * 4 + tc4) % 2
            tok_l8 = P.v(lambda e, k=kb: e.tensor_copy(LT8[k][:], PS_Q1[:]),
                         waits=[("pe", tok_lmm)])
            tok_lt8[kb] = tok_l8
            TR(PS_T16A[kb], LT8[kb][:], waits=[("dve", tok_l8), ("dve", lt_bank[kb])])
            lt_bank[kb] = P.v(lambda e, k=kb, kc2=st * 4 + tc4: e.tensor_copy(
                LTC[:, kc2 * 128:(kc2 + 1) * 128], PS_T16A[k][0:16, :]),
                waits=[("pe", P.cnt["pe"])])
    P.barrier()
    stA.close()
    if PHASES <= 1:
        return _trunc_tail()

    stLB = ExitStack()
    LAST = stLB.enter_context(nc.sbuf_tensor("LAST", [128, 512], F32))
    SC = stLB.enter_context(nc.sbuf_tensor("SC", [128, 16], F32))
    GS = stLB.enter_context(nc.sbuf_tensor("GS", [1, 8], F32))
    MASKB = stLB.enter_context(nc.sbuf_tensor("MASKB", [128, 512], F32))
    MI = stLB.enter_context(nc.sbuf_tensor("MI", [32, 2048], F32))
    COMPACT = stLB.enter_context(nc.sbuf_tensor("COMPACT", [32, 256], F32))
    TC1 = stLB.enter_context(nc.sbuf_tensor("TC1", [128, 32], F32))
    TC2 = stLB.enter_context(nc.sbuf_tensor("TC2", [128, 32], F32))
    VB1 = stLB.enter_context(nc.sbuf_tensor("VB1", [128, 32], F32))
    VB2 = stLB.enter_context(nc.sbuf_tensor("VB2", [128, 32], F32))
    BT1 = stLB.enter_context(nc.sbuf_tensor("BT1", [128, 32], F32))
    BT2 = stLB.enter_context(nc.sbuf_tensor("BT2", [128, 32], F32))
    BREC = stLB.enter_context(nc.sbuf_tensor("BREC", [128, 1], F32))
    IDX1 = stLB.enter_context(nc.sbuf_tensor("IDX1", [128, 32], mybir.dt.int32))
    IDX2 = stLB.enter_context(nc.sbuf_tensor("IDX2", [128, 32], mybir.dt.int32))
    IDF = stLB.enter_context(nc.sbuf_tensor("IDF", [128, 32], F32))
    M0COL = stLB.enter_context(nc.sbuf_tensor("M0COL", [128, 1], F32))
    OFF2 = stLB.enter_context(nc.sbuf_tensor("OFF2", [128, 1], F32))
    KNM = stLB.enter_context(nc.sbuf_tensor("KNM", [128, 4], F32))
    SQG = stLB.enter_context(nc.sbuf_tensor("SQG", [128, 64], F32))

    stL2 = ExitStack()
    PS_C = stL2.enter_context(nc.psum_tensor("PS_C", [128, 4], F32))
    PS_T16 = [stL2.enter_context(nc.psum_tensor(f"PS_T16{i}", [16, 128], F32))
              for i in range(2)]
    PS_TX = stL2.enter_context(nc.psum_tensor("PS_TX", [128, 32], F32))
    PS_KT = [stL2.enter_context(nc.psum_tensor(f"PS_KT{i}", [64, 128], BF16))
             for i in range(2)]

    tok_wproj = P.dma("gpsimd", "gd", lambda e: e.dma_start(
        out=WPROJS.ap().rearrange("p (hp f) -> p hp f", f=1024),
        in_=wproj_d.rearrange("(hp p) f -> p hp f", p=128)),
        waits=[("pe", P.cnt["pe"])])

    tok_neg = P.v(lambda e: e.memset(LTC[0:16, 4032:4096], -1.0e30),
                  waits=[("dve", lt_bank[0]), ("dve", lt_bank[1])])
    P.dma("sync", "sy", lambda e: e.dma_start(out=ltc_d, in_=LTC[:]),
          waits=[("dve", tok_neg), ("pe", P.cnt["pe"])])
    P.dma("sync", "sy", lambda e: e.dma_start(
        out=LAST[:], in_=ltc_d.rearrange("h (j f) -> (h j) f", j=8)),
        waits=[P.wall("sy")])
    wlast = [P.wall("sy")]
    tok_rmin = P.v(lambda e: e.tensor_reduce(SC[:, 0:1], LAST[:, 0:448], axis=AX.X, op=ALU.min),
                   waits=wlast)
    tok_rmax = P.v(lambda e: e.reduce_max(SC[:, 1:2], LAST[:], axis=AX.X), waits=wlast)
    TR(PS_T16[0][0:1, :], SC[:, 0:1], waits=[("dve", tok_rmax), ("dve", tok_rmin)])
    TR(PS_T16[1][0:1, :], SC[:, 1:2], waits=[("dve", tok_rmax)])
    tok_tr2 = P.cnt["pe"]
    tok_gmin = P.v(lambda e: e.tensor_reduce(GS[0:1, 0:1], PS_T16[0][0:1, :], axis=AX.X,
                                             op=ALU.min), waits=[("pe", tok_tr2)])
    tok_gmax = P.v(lambda e: e.reduce_max(GS[0:1, 1:2], PS_T16[1][0:1, :], axis=AX.X))
    tok_gm2 = P.v(lambda e: e.tensor_scalar(GS[0:1, 0:1], GS[0:1, 0:1], -10.0, None, op0=ALU.add))
    tok_mmlo = MM(PS_C[:, 0:2], ONES[0:1, :], GS[0:1, 0:2], True, True,
                  waits=[("dve", tok_gm2), ("dve", tok_gmax)])
    tok_dve = P.v(lambda e: e.tensor_copy(SC[:, 0:2], PS_C[:, 0:2]), waits=[("pe", tok_mmlo)])
    for it in range(NBIS):
        P.v(lambda e: e.tensor_tensor(SC[:, 2:3], SC[:, 0:1], SC[:, 1:2], op=ALU.add),
            waits=[("dve", tok_dve)])
        P.v(lambda e: e.tensor_scalar(SC[:, 2:3], SC[:, 2:3], 0.5, None, op0=ALU.mult))
        P.v(lambda e: e.tensor_scalar(MASKB[:], LAST[:], SC[:, 2:3], None, op0=ALU.is_gt),
            waits=wlast)
        tok_cp = P.v(lambda e: e.reduce_sum(SC[:, 3:4], MASKB[:], axis=AX.X))
        tok_cmm = MM(PS_C[:, 2:3], SAME, SC[:, 3:4], True, True,
                     waits=[("dve", tok_cp)] + wc)
        P.v(lambda e: e.tensor_copy(SC[:, 4:5], PS_C[:, 2:3]), waits=[("pe", tok_cmm)])
        P.v(lambda e: e.tensor_scalar(SC[:, 5:6], SC[:, 4:5], float(NSEL), None, op0=ALU.is_gt))
        P.v(lambda e: e.tensor_tensor(SC[:, 6:7], SC[:, 2:3], SC[:, 0:1], op=ALU.subtract))
        P.v(lambda e: e.tensor_tensor(SC[:, 7:8], SC[:, 2:3], SC[:, 1:2], op=ALU.subtract))
        P.v(lambda e: e.tensor_scalar(SC[:, 6:7], SC[:, 6:7], SC[:, 5:6], None, op0=ALU.mult))
        P.v(lambda e: e.tensor_scalar(SC[:, 8:9], SC[:, 7:8], SC[:, 5:6], None, op0=ALU.mult))
        P.v(lambda e: e.tensor_tensor(SC[:, 7:8], SC[:, 7:8], SC[:, 8:9], op=ALU.subtract))
        P.v(lambda e: e.tensor_tensor(SC[:, 0:1], SC[:, 0:1], SC[:, 6:7], op=ALU.add))
        tok_dve = P.v(lambda e: e.tensor_tensor(SC[:, 1:2], SC[:, 1:2], SC[:, 7:8], op=ALU.add))
    tok_mask = P.v(lambda e: e.tensor_scalar(MASKB[:], LAST[:], SC[:, 1:2], None, op0=ALU.is_gt),
                   waits=[("dve", tok_dve)])
    P.dma("sync", "sy", lambda e: e.dma_start(
        out=ltc_d.rearrange("h (j f) -> (h j) f", j=8), in_=MASKB[:]),
        waits=[("dve", tok_mask), P.wall("sy")])
    tok_m2 = P.dma("sync", "sy", lambda e: e.dma_start(
        out=MI[:], in_=ltc_d.rearrange("h (j2 f) -> (h j2) f", j2=2)),
        waits=[P.wall("sy")])
    P.v(lambda e: e.tensor_tensor(MI[:], MI[:], IOTA2P[:], op=ALU.mult),
        waits=[("sy", tok_m2), ("sy", tok_consts)])
    tok_mi = P.v(lambda e: e.tensor_scalar(MI[:], MI[:], -1.0, None, op0=ALU.add))
    for it in range(32):
        P.v(lambda e, i2=it: e.max(out=COMPACT[:, i2 * 8:(i2 + 1) * 8], in_=MI[0:32, :]),
            waits=[("dve", tok_mi)])
        tok_mi = P.v(lambda e, i2=it: e.match_replace(
            out=MI[0:32, :], in_to_replace=COMPACT[:, i2 * 8:(i2 + 1) * 8],
            in_values=MI[0:32, :], imm_value=-1.0))
    tok_ext = tok_mi
    TR(PS_TX, COMPACT[0:32, 0:128], waits=[("dve", tok_ext)])
    tok_tc1 = P.v(lambda e: e.tensor_copy(TC1[:], PS_TX[:]), waits=[("pe", P.cnt["pe"])])
    TR(PS_TX, COMPACT[0:32, 128:256], waits=[("dve", tok_tc1)])
    tok_tc2 = P.v(lambda e: e.tensor_copy(TC2[:], PS_TX[:]), waits=[("pe", P.cnt["pe"])])
    tok_v1 = P.v(lambda e: e.tensor_scalar(VB1[:], TC1[:], 0.5, None, op0=ALU.is_gt),
                 waits=[("dve", tok_tc2)])
    tok_v2 = P.v(lambda e: e.tensor_scalar(VB2[:], TC2[:], 0.5, None, op0=ALU.is_gt))
    P.v(lambda e: e.tensor_scalar(IDF[:], TC1[:], -1.0, None, op0=ALU.add))
    P.v(lambda e: e.tensor_tensor(IDF[:], IDF[:], VB1[:], op=ALU.mult))
    tok_idx1 = P.v(lambda e: e.tensor_copy(IDX1[:], IDF[:]))
    P.v(lambda e: e.tensor_scalar(IDF[:], TC2[:], -1.0, None, op0=ALU.add))
    P.v(lambda e: e.tensor_tensor(IDF[:], IDF[:], VB2[:], op=ALU.mult))
    tok_idx2 = P.v(lambda e: e.tensor_copy(IDX2[:], IDF[:]))
    tok_kvgz = P.v(lambda e: e.memset(KVG[:], 0.0))
    tok_ones = P.v(lambda e: e.memset(
        KVG.ap().rearrange("p (b z) -> p b z", z=130)[:, :, 128:129], 1.0),
        waits=[("dve", tok_kvgz)])
    gidx_map = {0: (IDX1, VB1, 0), 1: (IDX2, VB2, 0), 2: (IDX1, VB1, 1), 3: (IDX2, VB2, 1)}
    wg = [("dve", tok_idx1), ("dve", tok_idx2), ("dve", tok_ones), P.wall("out")]
    from concourse.bass_core_rust import IndirectOffsetOnAxis as IOA
    for h in range(NH):
        for c in range(4):
            IDXt, VBt, jj = gidx_map[c]
            col = 2 * h + jj
            P.dma("gpsimd", "gd", lambda e, h2=h, c2=c, it=IDXt, cl=col: e.dma_start(
                out=KVG[:, (h2 * 5 + c2) * 130: (h2 * 5 + c2) * 130 + 128],
                in_=kv_d[:, h2 * 128:(h2 + 1) * 128]) if False else
                e.indirect_dma_start(
                    KVG[:, (h2 * 5 + c2) * 130: (h2 * 5 + c2) * 130 + 128],
                    None,
                    kv_d,
                    IOA(it[:, cl:cl + 1], 0),
                    element_offset=h2 * 128),
                waits=wg)
        P.dma("gpsimd", "gd", lambda e, h2=h: e.dma_start(
            out=KVG[0:64, (h2 * 5 + 4) * 130: (h2 * 5 + 4) * 130 + 128],
            in_=kv_d[T - 64:T, h2 * 128:(h2 + 1) * 128]), waits=wg)
    tok_gath = P.cnt["gd"]
    tok_knz2 = P.v(lambda e: e.memset(KNM[:], 0.0))
    kt_bank = [0, 0]
    tok_knm = tok_knz2
    for h in range(NH):
        for c in range(5):
            blk = (h * 5 + c) * 130
            TR(PS_KT[c % 2], KVG[:, blk: blk + 64],
               waits=[("gd", tok_gath), ("dve", kt_bank[c % 2])])
            kt_bank[c % 2] = P.v(lambda e, h2=h, c2=c, k=c % 2: e.tensor_copy(
                KFG[(h2 % 2) * 64:(h2 % 2) * 64 + 64,
                    (h2 // 2) * 640 + c2 * 128: (h2 // 2) * 640 + (c2 + 1) * 128],
                PS_KT[k][:]), waits=[("pe", P.cnt["pe"])])
            t_s = P.s(lambda e, b=blk: e.activation(SQG[:], KVG[:, b:b + 64], AF.Square),
                      waits=[("gd", tok_gath)])
            P.v(lambda e: e.reduce_sum(KNM[:, 1:2], SQG[:], axis=AX.X),
                waits=[("act", t_s)])
            tok_knm = P.v(lambda e: e.tensor_tensor(KNM[:, 0:1], KNM[:, 0:1], KNM[:, 1:2],
                                                    op=ALU.max))
    tok_qn = P.v(lambda e: e.reduce_max(SC[:, 9:10], QNBUF[:, 0:32], axis=AX.X))
    TR(PS_T16[0][0:1, :], SC[:, 9:10], waits=[("dve", tok_qn)])
    TR(PS_T16[1][0:1, :], KNM[:, 0:1], waits=[("dve", tok_knm)])
    tok_tr3 = P.cnt["pe"]
    tok_nm = P.v(lambda e: e.reduce_max(GS[0:1, 2:3], PS_T16[0][0:1, :], axis=AX.X),
                 waits=[("pe", tok_tr3)])
    tok_nm2 = P.v(lambda e: e.reduce_max(GS[0:1, 3:4], PS_T16[1][0:1, :], axis=AX.X))
    tok_sq1 = P.s(lambda e: e.activation(GS[0:1, 4:5], GS[0:1, 2:3], AF.Sqrt),
                  waits=[("dve", tok_nm), ("dve", tok_nm2)])
    tok_sq2 = P.s(lambda e: e.activation(GS[0:1, 5:6], GS[0:1, 3:4], AF.Sqrt))
    tok_m0 = P.v(lambda e: e.tensor_tensor(GS[0:1, 6:7], GS[0:1, 4:5], GS[0:1, 5:6], op=ALU.mult),
                 waits=[("act", tok_sq2)])
    tok_m0mm = MM(PS_C[:, 3:4], ONES[0:1, :], GS[0:1, 6:7], True, True, waits=[("dve", tok_m0)])
    tok_m0c = P.v(lambda e: e.tensor_copy(M0COL[:], PS_C[:, 3:4]), waits=[("pe", tok_m0mm)])
    tok_off = P.v(lambda e: e.tensor_scalar(OFF2[:], M0COL[:], -0.125, 1.0e30,
                                            op0=ALU.mult, op1=ALU.add))
    tok_b1 = P.v(lambda e: e.tensor_scalar(BT1[:], VB1[:], OFF2[:], None, op0=ALU.mult),
                 waits=[("dve", tok_off), ("dve", tok_v1)])
    tok_b1b = P.v(lambda e: e.tensor_scalar(BT1[:], BT1[:], -1.0e30, None, op0=ALU.add))
    tok_b2 = P.v(lambda e: e.tensor_scalar(BT2[:], VB2[:], OFF2[:], None, op0=ALU.mult),
                 waits=[("dve", tok_v2)])
    tok_b2b = P.v(lambda e: e.tensor_scalar(BT2[:], BT2[:], -1.0e30, None, op0=ALU.add))
    tok_brec = P.v(lambda e: e.tensor_scalar(BREC[:], RECPAT, OFF2[:], None, op0=ALU.mult),
                   waits=[("dve", tok_off)] + wc)
    tok_bias = P.v(lambda e: e.tensor_scalar(BREC[:], BREC[:], -1.0e30, None, op0=ALU.add))
    if DEBUG:
        P.dma("sync", "sy", lambda e: e.dma_start(out=taps["t_last"], in_=LAST[:]),
              waits=[("dve", tok_mask)])
        P.dma("sync", "sy", lambda e: e.dma_start(out=taps["t_hi"], in_=SC[:]),
              waits=[("dve", tok_bias)])
    P.barrier()
    stL2.close()

    QB = stLB.enter_context(nc.sbuf_tensor("QB", [128, 8 * 512], BF16))
    XRES = stLB.enter_context(nc.sbuf_tensor("XRES", [128, 8 * 512], F32))
    AO = stLB.enter_context(nc.sbuf_tensor("AO", [128, 8 * 512], BF16))
    UT = [stLB.enter_context(nc.sbuf_tensor(f"UT{i}", [128, 512], BF16)) for i in range(4)]
    RECIP = stLB.enter_context(nc.sbuf_tensor("RECIP", [1, 512], F32))
    RB = stLB.enter_context(nc.sbuf_tensor("RB", [64, 512], F32))
    SQB = [stLB.enter_context(nc.sbuf_tensor("SQB0", [128, 512], F32))] * 2
    ROWS = stLB.enter_context(nc.sbuf_tensor("ROWS", [1, 4 * 512], F32))
    AE = stLB.enter_context(nc.sbuf_tensor("AE", [128, 512], F32))
    BE = stLB.enter_context(nc.sbuf_tensor("BE", [128, 512], F32))
    T1B = stLB.enter_context(nc.sbuf_tensor("T1B", [128, 512], F32))
    X1BST = [stLB.enter_context(nc.sbuf_tensor(f"X1BST{i}", [128, 512], BF16)) for i in range(2)]

    def bias_col(h, c):
        if c == 4:
            return BREC[:, 0:1]
        t = BT1 if c in (0, 2) else BT2
        col = 2 * h + (c // 2)
        return t[:, col:col + 1]

    x1b_free = [0, 0]
    PS_S = [stLB.enter_context(nc.psum_tensor(f"PS_S{i}", [128, 512], F32))
            for i in range(4)]
    PS_AV = [stLB.enter_context(nc.psum_tensor(f"PS_AV{i}", [65, 512], F32))
             for i in range(2)]
    PS_PJ = [stLB.enter_context(nc.psum_tensor(f"PS_PJ{i}", [128, 512], F32))
             for i in range(2)]
    exp_toks = [0, 0, 0, 0]
    stat_toks = [0, 0, 0, 0]
    pj_bank = [0, 0]
    for qst in range(4):
        pf = "pf0" if qst % 2 == 0 else "pf1"
        wq = [("pe", P.cnt["pe"]), ("dve", P.cnt["dve"]), ("act", P.cnt["act"]),
              P.wall("sy"), P.wall("up0"), P.wall("up1")]
        if qst == 0:
            P.dma("sync", pf, lambda e, q=qst: e.dma_start(
                out=QB.ap().rearrange("p (qf t) -> p qf t", t=512),
                in_=qfm_d.rearrange("(qf p) t -> p qf t", p=128)[:, :, q * 512:(q + 1) * 512]),
                waits=wq)
        P.dma("sync", pf, lambda e, q=qst: e.dma_start(
            out=XRES.ap().rearrange("p (dc t) -> p dc t", t=512),
            in_=xres_d.rearrange("(dc p) t -> p dc t", p=128)[:, :, q * 512:(q + 1) * 512]),
            waits=wq)
        wqb = [P.wall(pf)]
        av_toks = {}
        ao_toks = [0, 0]
        tok_rbe = 0
        for hp in range(8):
            def emit_av(kc2, sub, hp2=hp):
                h2 = 2 * hp2 + sub
                av_toks[(kc2, sub)] = MM(
                    PS_AV[sub],
                    KVG[:, (h2 * 5 + kc2) * 130 + 64: (h2 * 5 + kc2) * 130 + 129],
                    UT[(kc2 * 2 + sub) % 4][:], start=(kc2 == 0), stop=(kc2 == 4),
                    waits=[("act", exp_toks[(kc2 * 2 + sub) % 4])] +
                          ([("dve", ao_toks[sub])] if kc2 == 0 else []))

            for kc in range(5):
                if kc >= 2:
                    emit_av(kc - 2, 0)
                    emit_av(kc - 2, 1)
                for sub in range(2):
                    h = 2 * hp + sub
                    bank = (kc * 2 + sub) % 4
                    wS = [("act", exp_toks[bank]), ("dve", tok_bias),
                          ("dve", stat_toks[bank])] + wqb
                    tok_s = MM(PS_S[bank],
                               KFG[sub * 64:sub * 64 + 64,
                                   hp * 640 + kc * 128: hp * 640 + kc * 128 + 128],
                               QB[sub * 64:sub * 64 + 64, hp * 512:(hp + 1) * 512],
                               start=True, stop=True, waits=wS, tp=(sub * 64, 0))
                    exp_toks[bank] = P.s(
                        lambda e, k=bank, bc=bias_col(h, kc): e.activation(
                            UT[k][:], PS_S[k][:], AF.Exp, bias=bc, scale=0.125),
                        waits=[("pe", tok_s)] +
                              ([("pe", av_toks[(kc - 2, sub)])] if (kc - 2, sub) in av_toks
                               else []),
                        sw=False)
            for kc2 in (3, 4):
                emit_av(kc2, 0)
                emit_av(kc2, 1)
            av31 = [av_toks[(4, 0)], av_toks[(4, 1)]]
            av_toks = {}
            for sub in range(2):
                tok_rc = P.v(lambda e, sb=sub: e.reciprocal(RECIP[:], PS_AV[sb][64:65, :]),
                             waits=[("pe", av31[sub])])
                tok_rb = MM(PS_PJ[sub][0:64, :], ONES[0:1, 0:64], RECIP[:], True, True,
                            waits=[("dve", tok_rc), ("dve", tok_rbe),
                                   ("dve", pj_bank[sub])])
                tok_rbe = P.v(lambda e, sb=sub: e.tensor_copy(RB[:], PS_PJ[sb][0:64, :]),
                              waits=[("pe", tok_rb)])
                pj_bank[sub] = tok_rbe
                ao_toks[sub] = P.v(lambda e, kt2=hp, b2=sub * 64, sb=sub: e.tensor_tensor(
                    AO[b2:b2 + 64, kt2 * 512:(kt2 + 1) * 512], PS_AV[sb][0:64, :], RB[:],
                    op=ALU.mult), waits=[("dve", tok_rbe)])
        tok_ao = ao_toks[1]
        if qst < 3:
            pf2 = "pf0" if (qst + 1) % 2 == 0 else "pf1"
            P.dma("sync", pf2, lambda e, q=qst + 1: e.dma_start(
                out=QB.ap().rearrange("p (qf t) -> p qf t", t=512),
                in_=qfm_d.rearrange("(qf p) t -> p qf t", p=128)[:, :, q * 512:(q + 1) * 512]),
                waits=[("pe", P.cnt["pe"])])
        ev_toks = []
        for hc in range(8):
            for hp in range(8):
                tok_pj = MM(PS_PJ[hc % 2],
                            WPROJS[:, hp * 1024 + hc * 128: hp * 1024 + hc * 128 + 128],
                            AO[:, hp * 512:(hp + 1) * 512],
                            start=(hp == 0), stop=(hp == 7),
                            waits=([("dve", tok_ao), ("gd", tok_wproj),
                                    ("dve", pj_bank[hc % 2])] +
                                   ([("dve", ev_toks[hc - 2])] if hc >= 2 else []))
                            if hp == 0 else ())
            ev_toks.append(P.v(lambda e, c=hc: e.tensor_tensor(
                XRES[:, c * 512:(c + 1) * 512], PS_PJ[c % 2][:],
                XRES[:, c * 512:(c + 1) * 512], op=ALU.add),
                waits=[("pe", tok_pj)] + wqb))
        pj_bank = [ev_toks[6], ev_toks[7]]
        sq_tok = [0, 0]
        tok_sq_mm = 0
        ev_all = ev_toks[7]
        wact = [("act", P.cnt["act"])]
        for hc in range(8):
            sq_tok[hc % 2] = P.s(lambda e, c=hc: e.activation(
                SQB[c % 2][:], XRES[:, c * 512:(c + 1) * 512], AF.Square),
                waits=[("dve", ev_toks[hc]), ("pe", tok_sq_mm)])
            MM(PS_S[0][0:1, :], ONES[:, 0:1], XRES[:, hc * 512:(hc + 1) * 512],
               start=(hc == 0), stop=(hc == 7), waits=[("dve", ev_all)] + wact)
            tok_sq_mm = MM(PS_S[1][0:1, :], ONES[:, 0:1], SQB[hc % 2][:],
                           start=(hc == 0), stop=(hc == 7),
                           waits=[("act", sq_tok[hc % 2])] + wact)
        R = lambda i: ROWS[0:1, i * 512:(i + 1) * 512]
        tok_m1 = P.v(lambda e: e.tensor_scalar(R(0), PS_S[0][0:1, :], 1.0 / D, None,
                                               op0=ALU.mult), waits=[("pe", tok_sq_mm)])
        P.v(lambda e: e.tensor_scalar(R(1), PS_S[1][0:1, :], 1.0 / D, None, op0=ALU.mult))
        P.v(lambda e: e.tensor_tensor(R(2), R(0), R(0), op=ALU.mult))
        tok_var = P.v(lambda e: e.tensor_tensor(R(3), R(1), R(2), op=ALU.subtract))
        P.s(lambda e: e.activation(R(1), R(3), AF.Ln, bias=EPSC[0:1, 0:1]),
            waits=[("dve", tok_var), ("dve", tok_eps)])
        tok_rstd = P.s(lambda e: e.activation(R(2), R(1), AF.Exp, scale=-0.5))
        P.v(lambda e: e.tensor_tensor(R(3), R(0), R(2), op=ALU.mult), waits=[("act", tok_rstd)])
        tok_nmb = P.v(lambda e: e.tensor_scalar(R(3), R(3), -1.0, None, op0=ALU.mult))
        tok_bc1 = MM(PS_S[2][:], ONES[0:1, :], R(2), True, True,
                     waits=[("dve", tok_nmb), ("act", P.cnt["act"])])
        tok_ae = P.v(lambda e: e.tensor_copy(AE[:], PS_S[2][:]), waits=[("pe", tok_bc1)])
        tok_bc2 = MM(PS_S[3][:], ONES[0:1, :], R(3), True, True,
                     waits=[("dve", tok_nmb), ("act", P.cnt["act"])])
        tok_be = P.v(lambda e: e.tensor_copy(BE[:], PS_S[3][:]), waits=[("pe", tok_bc2)])
        stat_toks = [tok_m1, tok_m1, tok_ae, tok_be]
        for hc in range(8):
            P.v(lambda e, c=hc: e.tensor_tensor(T1B[:], XRES[:, c * 512:(c + 1) * 512],
                                                AE[:], op=ALU.mult),
                waits=[("dve", tok_ae), ("dve", tok_be)])
            tok_x1 = P.v(lambda e, c=hc: e.tensor_tensor(
                XRES[:, c * 512:(c + 1) * 512], T1B[:], BE[:], op=ALU.add))
            tok_x1b = P.v(lambda e, c=hc: e.tensor_copy(X1BST[c % 2][:],
                                                        XRES[:, c * 512:(c + 1) * 512]),
                          waits=[("dve", tok_x1),
                                 ("dn0" if hc % 2 == 0 else "dn1", x1b_free[hc % 2])])
            P.dma("sync", "sy", lambda e, c=hc, q=qst: e.dma_start(
                out=x1_d[c * 128:(c + 1) * 128, q * 512:(q + 1) * 512],
                in_=XRES[:, c * 512:(c + 1) * 512]), waits=[("dve", tok_x1)])
            x1b_free[hc % 2] = P.dma(
                "gpsimd", "dn0" if hc % 2 == 0 else "dn1",
                lambda e, c=hc, q=qst: e.dma_start(
                    out=x1bf_d[c * 128:(c + 1) * 128, q * 512:(q + 1) * 512],
                    in_=X1BST[c % 2][:]), waits=[("dve", tok_x1b)])
        if DEBUG and qst == 0:
            P.dma("sync", "sy", lambda e: e.dma_start(out=taps["t_x1"], in_=XRES[:]),
                  waits=[("dve", tok_x1)])
    P.barrier()
    stLB.close()
    stKV.close()
    if PHASES <= 4:
        return _trunc_tail()

    COMB = es.enter_context(nc.sbuf_tensor("COMB", [128, 8 * NTOK], F32))
    stC = ExitStack()
    X1BF = stC.enter_context(nc.sbuf_tensor("X1BF", [128, 8 * NTOK], BF16))
    HACT = stC.enter_context(nc.sbuf_tensor("HACT", [128, 32 * 1024], BF16))
    UPWS = [stC.enter_context(nc.sbuf_tensor(f"UPWS{i}", [128, 8 * 256], BF16)) for i in range(2)]
    DNWS = [stC.enter_context(nc.sbuf_tensor(f"DNWS{i}", [128, 16 * 128], BF16)) for i in range(4)]
    WB = [stC.enter_context(nc.sbuf_tensor(f"WB{i}", [128, 512], F32)) for i in range(2)]
    WT = stC.enter_context(nc.sbuf_tensor("WT", [128, NTOK], F32))
    TEP = stC.enter_context(nc.sbuf_tensor("TEP", [128, 512], F32))
    GW = stC.enter_context(nc.sbuf_tensor("GW", [128, 8 * 8], BF16))
    GBT = stC.enter_context(nc.sbuf_tensor("GBT", [128, 8], F32))
    RS = stC.enter_context(nc.sbuf_tensor("RS", [128, 48], F32))
    UPB = stC.enter_context(nc.sbuf_tensor("UPB", [128, 9 * 32], F32))
    DNB = stC.enter_context(nc.sbuf_tensor("DNB", [128, 9 * 8], F32))
    BRAW = stC.enter_context(nc.sbuf_tensor("BRAW", [32, 128], F32))
    OHT = stC.enter_context(nc.sbuf_tensor("OHT", [128, 8], F32))
    OH1 = stC.enter_context(nc.sbuf_tensor("OH1", [8, 128], F32))

    tok_x1bf = P.dma("sync", "pf0", lambda e: e.dma_start(
        out=X1BF.ap().rearrange("p (dc t) -> p dc t", t=NTOK),
        in_=x1bf_d.rearrange("(dc p) t -> p dc t", p=128)),
        waits=[P.wall("dn0"), P.wall("dn1")])
    wx1 = [P.wall("pf0")]
    tok_gw = P.dma("gpsimd", "gd", lambda e: e.dma_start(
        out=GW.ap().rearrange("p (dc g) -> p dc g", g=8),
        in_=gatew_d.rearrange("(dc p) g -> p dc g", p=128)),
        waits=[("pe", P.cnt["pe"])])

    stR = ExitStack()
    PS_R8 = stR.enter_context(nc.psum_tensor("PS_R8", [128, 8], F32))
    PS_RT = stR.enter_context(nc.psum_tensor("PS_RT", [8, 128], F32))
    PS_B = stR.enter_context(nc.psum_tensor("PS_B", [128, 40], F32))

    tok_gb = P.dma("sync", "pf1", lambda e: e.dma_start(out=RS[0:1, 0:8], in_=gateb_d))
    tok_gbmm = MM(PS_R8[:, 0:8], ONES[0:1, :], RS[0:1, 0:8], True, True,
                  waits=[P.wall("pf1")] + wc)
    tok_gbt = P.v(lambda e: e.tensor_copy(GBT[:], PS_R8[:, 0:8]), waits=[("pe", tok_gbmm)])
    tok_tr_prev = tok_gbt
    for e_i in range(9):
        if e_i < 8:
            src_u = upb_d[e_i, :].rearrange("(m p) -> m p", p=128)
            src_d = downb_d[e_i, :].rearrange("(m p) -> m p", p=128)
        else:
            src_u = sbup_d[0, :].rearrange("(m p) -> m p", p=128)
            src_d = sbdown_d[0, :].rearrange("(m p) -> m p", p=128)
        P.dma("sync", "pf1", lambda e, s=src_u: e.dma_start(out=BRAW[:], in_=s),
              waits=[("dve", tok_tr_prev)])
        TR(PS_B[:, 0:32], BRAW[:], waits=[P.wall("pf1")])
        tok_tru = P.v(lambda e, i=e_i: e.tensor_copy(UPB[:, i * 32:(i + 1) * 32], PS_B[:, 0:32]),
                      waits=[("pe", P.cnt["pe"])])
        P.dma("sync", "pf1", lambda e, s=src_d: e.dma_start(out=BRAW[0:8, :], in_=s),
              waits=[("dve", tok_tru)])
        TR(PS_B[:, 32:40], BRAW[0:8, :], waits=[P.wall("pf1"), ("dve", tok_tru)])
        tok_tr_prev = P.v(lambda e, i=e_i: e.tensor_copy(DNB[:, i * 8:(i + 1) * 8], PS_B[:, 32:40]),
                          waits=[("pe", P.cnt["pe"])])
    tok_wtz = P.v(lambda e: e.memset(WT[:], 0.0))
    tok_oht = P.v(lambda e: e.memset(OHT[:], 0.0))
    tok_wtp = 0
    tok_rs_prev = 0
    for tc in range(16):
        for dc in range(8):
            tok_r = MM(PS_R8[:, 0:8],
                       X1BF[:, dc * NTOK + tc * 128: dc * NTOK + tc * 128 + 128],
                       GW[:, dc * 8:(dc + 1) * 8], start=(dc == 0), stop=(dc == 7),
                       waits=wx1 + [P.wall("gd"), ("dve", tok_gbt), ("dve", tok_rs_prev)])
        tok_a = P.v(lambda e: e.tensor_tensor(RS[:, 16:24], PS_R8[:, 0:8], GBT[:], op=ALU.add),
                    waits=[("pe", tok_r)])
        tok_sg = P.s(lambda e: e.activation(RS[:, 24:32], RS[:, 16:24], AF.Sigmoid),
                     waits=[("dve", tok_a)])
        S8 = RS[:, 24:32]
        P.v(lambda e: e.reduce_max(RS[:, 32:33], S8, axis=AX.X), waits=[("act", tok_sg)])
        P.v(lambda e: e.tensor_scalar(RS[:, 40:48], S8, RS[:, 32:33], None, op0=ALU.is_equal))
        P.v(lambda e: e.scalar_tensor_tensor(RS[:, 40:48], RS[:, 40:48], -1.0e30, S8,
                                             op0=ALU.mult, op1=ALU.add))
        P.v(lambda e: e.reduce_max(RS[:, 33:34], RS[:, 40:48], axis=AX.X))
        P.v(lambda e: e.tensor_tensor(RS[:, 34:35], RS[:, 32:33], RS[:, 33:34], op=ALU.add))
        P.v(lambda e: e.tensor_scalar(RS[:, 34:35], RS[:, 34:35], 1.0e-6, None, op0=ALU.add))
        P.v(lambda e: e.reciprocal(RS[:, 35:36], RS[:, 34:35]))
        P.v(lambda e: e.tensor_scalar(RS[:, 35:36], RS[:, 35:36], 0.1, None, op0=ALU.mult))
        P.v(lambda e: e.tensor_scalar(RS[:, 40:48], S8, RS[:, 33:34], None, op0=ALU.is_ge))
        P.v(lambda e: e.tensor_tensor(RS[:, 40:48], S8, RS[:, 40:48], op=ALU.mult))
        tok_w2 = P.v(lambda e: e.tensor_scalar(RS[:, 40:48], RS[:, 40:48], RS[:, 35:36], None,
                                               op0=ALU.mult))
        tok_rs_prev = tok_w2
        TR(PS_RT[:], RS[:, 40:48], waits=[("dve", tok_w2), ("dve", tok_wtp)])
        tok_wtp = P.v(lambda e, t=tc: e.tensor_copy(WT[0:8, t * 128:(t + 1) * 128],
                                                    PS_RT[0:8, :]),
                      waits=[("pe", P.cnt["pe"]), ("dve", tok_wtz)])
    tok_wt = tok_wtp
    if DEBUG:
        P.dma("sync", "sy", lambda e: e.dma_start(out=taps["t_wt"], in_=WT[0:8, :]),
              waits=[("dve", tok_wt)])
    stR.close()

    stE = ExitStack()
    PS_U = [stE.enter_context(nc.psum_tensor(f"PS_U{i}", [128, 512], F32)) for i in range(4)]
    PS_D = [stE.enter_context(nc.psum_tensor(f"PS_D{i}", [128, 512], F32)) for i in range(2)]
    PS_W = [stE.enter_context(nc.psum_tensor(f"PS_W{i}", [128, 512], F32)) for i in range(2)]

    up_free = [0, 0]
    dn_free = [0, 0, 0, 0]
    gelu_tok = {(i, s): 0 for i in range(2) for s in range(2)}
    gelu_last = [0, 0]
    psd_free = [0, 0]
    wb_free = [0, 0]
    wbe_tok = [0, 0]
    comb_tok = {}
    upq = [(e_i, mh, mp) for e_i in range(NEXP) for mh in range(2) for mp in range(16)]

    def up_load(i):
        e_i, mh, mp = upq[i]
        slot = i % 2
        if e_i < 8:
            src = upw_d[e_i, :, mp * 256:(mp + 1) * 256].rearrange("(dc p) f -> p dc f", p=128)
        else:
            src = swup_d[:, mp * 256:(mp + 1) * 256].rearrange("(dc p) f -> p dc f", p=128)
        P.dma("gpsimd", "up0" if slot == 0 else "up1",
              lambda e, s=src, sl=slot: e.dma_start(
                  out=UPWS[sl].ap().rearrange("p (dc f) -> p dc f", f=256), in_=s),
              waits=[("pe", up_free[slot])])

    if upq:
        up_load(0)
        up_load(1)
    upq_i = 2
    gidx = 0
    tok_oh1 = 0
    for e_i in range(NEXP):
        is_sh = e_i == 8
        if not is_sh:
            if e_i > 0:
                P.v(lambda e, i=e_i: e.memset(OHT[:, i - 1:i], 0.0), waits=[("dve", tok_oh1)])
            tok_c1 = P.v(lambda e, i=e_i: e.memset(OHT[:, i:i + 1], 1.0),
                         waits=[("dve", tok_oht)])
            TR(PS_W[0][0:8, 0:128], OHT[:],
               waits=[("dve", tok_c1), ("dve", wbe_tok[0]), ("dve", wbe_tok[1])])
            tok_oh1 = P.v(lambda e: e.tensor_copy(OH1[:], PS_W[0][0:8, 0:128]),
                          waits=[("pe", P.cnt["pe"])])
        for mh in range(2):
            for mp in range(16):
                slot = gidx % 2
                gidx += 1
                for mi2 in range(2):
                    m = mp * 2 + mi2
                    for stq in range(2):
                        bank = mi2 * 2 + stq
                        for dc in range(8):
                            tok_u = MM(PS_U[bank],
                                       UPWS[slot][:, dc * 256 + mi2 * 128:
                                            dc * 256 + mi2 * 128 + 128],
                                       X1BF[:, dc * NTOK + mh * 1024 + stq * 512:
                                            dc * NTOK + mh * 1024 + stq * 512 + 512],
                                       start=(dc == 0), stop=(dc == 7),
                                       waits=([P.wall("up0" if slot == 0 else "up1"),
                                               ("act", gelu_tok[(mi2, stq)]), ("dve", tok_wt)]
                                              + wx1) if dc == 0 else ())
                        bias_col = (UPB[:, e_i * 32 + m: e_i * 32 + m + 1])
                        gelu_tok[(mi2, stq)] = P.s(
                            lambda e, mm_=m, sq=stq, bc=bias_col, bk=bank: e.activation(
                                HACT[:, mm_ * 1024 + sq * 512: mm_ * 1024 + sq * 512 + 512],
                                PS_U[bk][:], AF.Gelu, bias=bc),
                            waits=[("pe", tok_u)], sw=False)
                        gelu_last[stq] = gelu_tok[(mi2, stq)]
                up_free[slot] = P.cnt["pe"]
                if upq_i < len(upq):
                    up_load(upq_i)
                    upq_i += 1
            def dn_load(hc2, mseg, e2=e_i):
                slot = (hc2 * 2 + mseg) % 4
                if e2 < 8:
                    src = downw_d[e2, mseg * 2048:(mseg + 1) * 2048,
                                  hc2 * 128:(hc2 + 1) * 128].rearrange(
                                      "(mm p) f -> p mm f", p=128)
                else:
                    src = swdown_d[mseg * 2048:(mseg + 1) * 2048,
                                   hc2 * 128:(hc2 + 1) * 128].rearrange(
                                       "(mm p) f -> p mm f", p=128)
                P.dma("gpsimd", ("dn0", "dn1", "dn2", "dn3")[slot],
                      lambda e, s=src, sl=slot: e.dma_start(
                          out=DNWS[sl].ap().rearrange("p (mm f) -> p mm f", f=128), in_=s),
                      waits=[("pe", dn_free[slot])])

            for hc in range(8 if MOE_PART >= 2 else 0):
                if hc == 0:
                    dn_load(0, 0)
                    dn_load(0, 1)
                if hc < 7:
                    dn_load(hc + 1, 0)
                    dn_load(hc + 1, 1)
                if hc == 0 and not is_sh and WBC:
                    for stq in range(2):
                        tok_wb = MM(PS_W[stq][:], OH1[:],
                                    WT[0:8, mh * 1024 + stq * 512: mh * 1024 + stq * 512 + 512],
                                    True, True,
                                    waits=[("dve", tok_oh1), ("dve", tok_wt),
                                           ("dve", wbe_tok[stq])])
                        wbe_tok[stq] = P.v(
                            lambda e, sq=stq: e.tensor_copy(WB[sq][:], PS_W[sq][:]),
                            waits=[("pe", tok_wb), ("dve", wb_free[stq])])
                for stq in range(2):
                    for mseg in range(2):
                        slot = (hc * 2 + mseg) % 4
                        wD = [P.wall(("dn0", "dn1", "dn2", "dn3")[slot]),
                              ("act", gelu_last[stq]), ("dve", psd_free[stq])]
                        for mi in range(16):
                            m = mseg * 16 + mi
                            tok_d = MM(PS_D[stq],
                                       DNWS[slot][:, mi * 128:(mi + 1) * 128],
                                       HACT[:, m * 1024 + stq * 512: m * 1024 + stq * 512 + 512],
                                       start=(m == 0), stop=(m == 31),
                                       waits=wD if mi == 0 else ())
                    key = (hc, mh, stq)
                    o = mh * 1024 + stq * 512
                    if MOE_PART < 3:
                        psd_free[stq] = P.v(lambda e, sq=stq: e.tensor_copy(
                            TEP[:], PS_D[sq][:]), waits=[("pe", tok_d)])
                        comb_tok[key] = psd_free[stq]
                    elif not is_sh:
                        tok_t1 = P.v(lambda e, c=hc, i=e_i, sq=stq: e.tensor_scalar(
                            TEP[:], PS_D[sq][:], DNB[:, i * 8 + c: i * 8 + c + 1], None,
                            op0=ALU.add), waits=[("pe", tok_d)])
                        tok_t2 = P.v(lambda e, sq=stq: e.tensor_tensor(
                            TEP[:], TEP[:], WB[sq][:], op=ALU.mult),
                            waits=[("dve", wbe_tok[stq])])
                        wb_free[stq] = tok_t2
                        if e_i == 0:
                            tok_cw = P.v(lambda e, c=hc, o2=o: e.tensor_copy(
                                COMB[:, c * NTOK + o2: c * NTOK + o2 + 512], TEP[:]))
                        else:
                            tok_cw = P.v(lambda e, c=hc, o2=o: e.tensor_tensor(
                                COMB[:, c * NTOK + o2: c * NTOK + o2 + 512],
                                COMB[:, c * NTOK + o2: c * NTOK + o2 + 512], TEP[:],
                                op=ALU.add), waits=[("dve", comb_tok.get(key, 0))])
                    elif is_sh:
                        tok_t1 = P.v(lambda e, c=hc, sq=stq: e.tensor_scalar(
                            TEP[:], PS_D[sq][:], DNB[:, 8 * 8 + c: 8 * 8 + c + 1], 0.1,
                            op0=ALU.add, op1=ALU.mult), waits=[("pe", tok_d)])
                        tok_cw = P.v(lambda e, c=hc, o2=o: e.tensor_tensor(
                            COMB[:, c * NTOK + o2: c * NTOK + o2 + 512],
                            COMB[:, c * NTOK + o2: c * NTOK + o2 + 512], TEP[:],
                            op=ALU.add), waits=[("dve", comb_tok.get(key, 0))])
                    if MOE_PART >= 3:
                        comb_tok[key] = tok_cw
                        psd_free[stq] = tok_t1
                dn_free[(hc * 2) % 4] = P.cnt["pe"]
                dn_free[(hc * 2 + 1) % 4] = P.cnt["pe"]
    stE.close()
    P.barrier()
    stC.close()
    if PHASES <= 5:
        return _trunc_tail()

    stD = ExitStack()
    X1C = [stD.enter_context(nc.sbuf_tensor(f"X1C{i}", [128, 8 * 128], F32)) for i in range(2)]
    FTM = [stD.enter_context(nc.sbuf_tensor(f"FTM{i}", [128, 1024], F32)) for i in range(2)]
    X1TM = [stD.enter_context(nc.sbuf_tensor(f"X1TM{i}", [128, 1024], F32)) for i in range(2)]
    X2 = [stD.enter_context(nc.sbuf_tensor(f"X2_{i}", [128, 1024], F32)) for i in range(2)]
    SQ2 = stD.enter_context(nc.sbuf_tensor("SQ2", [128, 1024], F32))
    DS = stD.enter_context(nc.sbuf_tensor("DS", [128, 8], F32))
    PS_DT = [stD.enter_context(nc.psum_tensor(f"PS_DT{i}", [128, 128], F32)) for i in range(2)]

    x1c_free = [0, 0]
    d_free = [0, 0]

    pe_at_d = P.cnt["pe"]

    def d_load(tc):
        slot = tc % 2
        P.dma("sync", "pf0" if slot == 0 else "pf1", lambda e, t=tc, sl=slot: e.dma_start(
            out=X1C[sl].ap().rearrange("p (dc f) -> p dc f", f=128),
            in_=x1_d[:, t * 128:(t + 1) * 128].rearrange("(dc p) f -> p dc f", p=128)),
            waits=[P.wall("sy"), ("pe", max(pe_at_d, x1c_free[slot]))])

    d_load(0)
    tr_free = [0, 0]
    tok_ln = 0
    tok_s2 = 0
    for tc in range(16):
        slot = tc % 2
        if tc < 15:
            d_load(tc + 1)
        ev_last = 0
        for hc in range(8):
            TR(PS_DT[hc % 2], COMB[:, hc * NTOK + tc * 128: hc * NTOK + tc * 128 + 128],
               waits=[("dve", tr_free[hc % 2])])
            ev_last = P.v(lambda e, c=hc, sl=slot: e.tensor_copy(
                FTM[sl][:, c * 128:(c + 1) * 128], PS_DT[c % 2][:]),
                waits=[("pe", P.cnt["pe"]), ("dve", d_free[slot]),
                       P.wall("out")])
            tr_free[hc % 2] = ev_last
        for dc in range(8):
            TR(PS_DT[dc % 2], X1C[slot][:, dc * 128:(dc + 1) * 128],
               waits=[P.wall("pf0" if slot == 0 else "pf1"), ("dve", tr_free[dc % 2])])
            ev_last = P.v(lambda e, c=dc, sl=slot: e.tensor_copy(
                X1TM[sl][:, c * 128:(c + 1) * 128], PS_DT[c % 2][:]),
                waits=[("pe", P.cnt["pe"])])
            tr_free[dc % 2] = ev_last
        x1c_free[slot] = P.cnt["pe"]
        P.v(lambda e, sl=slot: e.reduce_max(DS[:, 0:1], FTM[sl][:], axis=AX.X,
                                            apply_absolute_value=True),
            waits=[("dve", ev_last)])
        P.v(lambda e: e.tensor_scalar(DS[:, 0:1], DS[:, 0:1], 1.0e-6, None, op0=ALU.add))
        tok_r = P.v(lambda e: e.reciprocal(DS[:, 1:2], DS[:, 0:1]))
        tok_x2 = P.v(lambda e, sl=slot: e.scalar_tensor_tensor(
            X2[sl][:], FTM[sl][:], DS[:, 1:2], X1TM[sl][:], op0=ALU.mult, op1=ALU.add),
            waits=[("dve", tok_r)])
        d_free[slot] = tok_x2
        tok_sq = P.s(lambda e, sl=slot: e.activation(SQ2[:], X2[sl][:], AF.Square),
                     waits=[("dve", tok_x2), ("dve", tok_s2)])
        P.v(lambda e, sl=slot: e.reduce_sum(DS[:, 2:3], X2[sl][:], axis=AX.X))
        tok_s2 = P.v(lambda e: e.reduce_sum(DS[:, 3:4], SQ2[:], axis=AX.X),
                     waits=[("act", tok_sq)])
        P.v(lambda e: e.tensor_scalar(DS[:, 4:5], DS[:, 2:3], 1.0 / D, None, op0=ALU.mult))
        P.v(lambda e: e.tensor_scalar(DS[:, 5:6], DS[:, 3:4], 1.0 / D, None, op0=ALU.mult))
        P.v(lambda e: e.tensor_tensor(DS[:, 6:7], DS[:, 4:5], DS[:, 4:5], op=ALU.mult))
        tok_v = P.v(lambda e: e.tensor_tensor(DS[:, 5:6], DS[:, 5:6], DS[:, 6:7],
                                              op=ALU.subtract))
        P.s(lambda e: e.activation(DS[:, 6:7], DS[:, 5:6], AF.Ln, bias=EPSC[:]),
            waits=[("dve", tok_v), ("dve", tok_eps)])
        tok_rs = P.s(lambda e: e.activation(DS[:, 7:8], DS[:, 6:7], AF.Exp, scale=-0.5))
        tok_ln = P.v(lambda e, sl=slot: e.tensor_scalar(
            X2[sl][:], X2[sl][:], DS[:, 4:5], DS[:, 7:8], op0=ALU.subtract, op1=ALU.mult),
            waits=[("act", tok_rs)])
        P.dma("sync", "out", lambda e, t=tc, sl=slot: e.dma_start(
            out=y_d[t * 128:(t + 1) * 128, :], in_=X2[sl][:]), waits=[("dve", tok_ln)])
    P.emit("sync", lambda e: None,
           waits=[("out", P.cnt["out"]), P.wall("sy"), P.wall("gd"),
                  P.wall("up0"), P.wall("up1"), P.wall("dn0"), P.wall("dn1"),
                  P.wall("dn2"), P.wall("dn3"),
                  P.wall("x0"), P.wall("x1"), P.wall("pf0"), P.wall("pf1"),
                  ("dve", P.cnt["dve"]), ("act", P.cnt["act"]), ("pe", P.cnt["pe"])])
    P.flush()
    stD.close()
    es.close()
    return nc, P.dma_hints


_NC_CACHE = None


def kernel(L0_attn_x, L0_attn_W_attn, L0_attn_W_proj, L0_ffn_gate_w, L0_ffn_bias,
           L0_ffn_up_w, L0_ffn_up_b, L0_ffn_down_w, L0_ffn_down_b, L0_ffn_sw_up,
           L0_ffn_sb_up, L0_ffn_sw_down, L0_ffn_sb_down):
    global _NC_CACHE
    x = np.ascontiguousarray(np.asarray(L0_attn_x, dtype=np.float32))
    common = dict(
        wattn=np.ascontiguousarray(np.asarray(L0_attn_W_attn, dtype=np.float32)),
        wproj=np.ascontiguousarray(np.asarray(L0_attn_W_proj, dtype=np.float32)),
        gatew=np.ascontiguousarray(np.asarray(L0_ffn_gate_w, dtype=np.float32)),
        gateb=np.ascontiguousarray(np.reshape(np.asarray(L0_ffn_bias, dtype=np.float32), (1, NE))),
        upw=np.ascontiguousarray(np.asarray(L0_ffn_up_w, dtype=np.float32)),
        upb=np.ascontiguousarray(np.asarray(L0_ffn_up_b, dtype=np.float32)),
        downw=np.ascontiguousarray(np.asarray(L0_ffn_down_w, dtype=np.float32)),
        downb=np.ascontiguousarray(np.asarray(L0_ffn_down_b, dtype=np.float32)),
        swup=np.ascontiguousarray(np.asarray(L0_ffn_sw_up, dtype=np.float32)),
        sbup=np.ascontiguousarray(np.reshape(np.asarray(L0_ffn_sb_up, dtype=np.float32), (1, DI))),
        swdown=np.ascontiguousarray(np.asarray(L0_ffn_sw_down, dtype=np.float32)),
        sbdown=np.ascontiguousarray(np.reshape(np.asarray(L0_ffn_sb_down, dtype=np.float32), (1, D))),
        consts=_host_consts(),
        consts2=_host_consts2(),
    )
    in_maps = []
    for c in range(8):
        b, hf = c // 2, c % 2
        m = dict(common)
        m["x"] = np.ascontiguousarray(x[b])
        m["xq"] = np.ascontiguousarray(x[b, hf * NTOK:(hf + 1) * NTOK, :])
        in_maps.append(m)
    if _NC_CACHE is None:
        _nc1, hints = build_nc()
        del _nc1
        _NC_CACHE, _ = build_nc(hints)
    res = run_bass_kernel_spmd(_NC_CACHE, in_maps, core_ids=list(range(8)))
    out = np.empty((4, T, D), dtype=np.float32)
    for c in range(8):
        b, hf = c // 2, c % 2
        out[b, hf * NTOK:(hf + 1) * NTOK, :] = np.asarray(res.results[c]["y"])
    return out


# version 16
# speedup vs baseline: 1.1454x; 1.0358x over previous
import numpy as np
from contextlib import ExitStack

import concourse.bass as bass
import concourse.mybir as mybir
from concourse.bass_utils import run_bass_kernel_spmd

F32 = mybir.dt.float32
BF16 = mybir.dt.bfloat16
AF = mybir.ActivationFunctionType
ALU = mybir.AluOpType
AX = mybir.AxisListType

D = 1024
T = 4096
NH = 16
KKEEP = 391
NSEL = KKEEP - 64
NTOK = 2048
NE = 8
DI = 4096
NBIS = 42

DEBUG = False
PHASES = 6
NEXP = 9
MOE_PART = 3
WBC = True
ENGINES = ("tensor", "vector", "scalar", "gpsimd", "sync")


class Prog:
    def __init__(self, nc, blk, sems, dma_hints=None):
        self.nc, self.blk = nc, blk
        self.sems = sems
        self.cnt = {s: 0 for s in sems}
        self.q = {e: [] for e in ENGINES}
        self.known = {e: {s: 0 for s in sems} for e in ENGINES}
        self.measuring = dma_hints is None
        self.dma_hints = {s: [] for s in sems} if self.measuring else dma_hints
        self.hint_idx = {s: 0 for s in sems}

    def emit(self, eng, fn, waits=(), inc=None, dma_sem=None):
        real = []
        for s, v in waits:
            assert v is not None, f"None wait token on {s} for {eng}"
            if v > 0 and self.known[eng][s] < v:
                real.append((s, v))
                self.known[eng][s] = v
        tok = None
        if inc is not None:
            s, n = inc
            self.cnt[s] += n
            tok = self.cnt[s]
        self.q[eng].append((fn, real, inc, dma_sem))
        return tok

    def pe(self, fn, waits=(), inc=True):
        return self.emit("tensor", fn, waits, ("pe", 1) if inc else None)

    def v(self, fn, waits=(), sw=True):
        w = list(waits) + ([("dve", self.cnt["dve"])] if sw else [])
        return self.emit("vector", fn, w, ("dve", 1))

    def s(self, fn, waits=(), sw=True):
        w = list(waits) + ([("act", self.cnt["act"])] if sw else [])
        return self.emit("scalar", fn, w, ("act", 1))

    def dma(self, queue, sem, fn, waits=()):
        if self.measuring:
            inc = 16
        else:
            inc = self.dma_hints[sem][self.hint_idx[sem]]
            self.hint_idx[sem] += 1
        return self.emit(queue, fn, waits, (sem, inc), dma_sem=sem)

    def wall(self, name):
        return (name, self.cnt[name])

    def flush(self):
        for e in ENGINES:
            items = self.q[e]
            if not items:
                continue
            sems = self.sems

            nc = self.nc
            measuring = self.measuring
            hints = self.dma_hints

            def run(eng, items=tuple(items)):
                for fn, waits, inc, dma_sem in items:
                    for s, v in waits:
                        eng.wait_ge(sems[s], v)
                    n0 = nc.n_instructions() if (measuring and dma_sem) else 0
                    ins = fn(eng)
                    if inc is not None:
                        ins.then_inc(sems[inc[0]], 16 if dma_sem else inc[1])
                        if measuring and dma_sem:
                            names = nc.inst_names()[n0:]
                            nupd = 0
                            for nm2 in names:
                                i2 = nc.find_inst_by_name(nm2)
                                si = i2.sync_info if i2 is not None else None
                                if si is not None and any(
                                        u.sync_type == 'semaphore' and u.ant_name == dma_sem
                                        for u in si.on_update):
                                    nupd += 1
                            hints[dma_sem].append(16 * max(1, nupd))
            getattr(self.blk, e)(run)
            self.q[e] = []

    def barrier(self):
        snap = [(s, self.cnt[s]) for s in self.sems if self.cnt[s] > 0]
        for e in ENGINES:
            for s, v in snap:
                if self.known[e][s] < v:
                    self.q[e].append((lambda eng: None, [(s, v)], None, None))
                    self.known[e][s] = v
        self.flush()


def _host_consts():
    C = np.zeros((128, 404), dtype=np.float32)
    C[:, 0:128] = np.eye(128, dtype=np.float32)
    C[:, 128:256] = 1.0
    C[0:64, 256] = 1.0
    C[64:128, 257] = 1.0
    for p in range(128):
        C[p, 258 + p // 8] = 1.0
        g = p // 8
        C[p, 274 + 8 * g:274 + 8 * g + 8] = 1.0
    C[0:64, 402] = 1.0
    return C


def _host_consts2():
    C2 = np.zeros((32, 2048), dtype=np.float32)
    for r in range(32):
        C2[r, :] = (r % 2) * 2048 + np.arange(2048) + 1.0
    return C2


def build_nc(dma_hints=None):
    nc = bass.Bass("TRN2", target_bir_lowering=False)
    ap = lambda t: t.ap()
    x_d = ap(nc.dram_tensor("x", [T, D], F32, kind="ExternalInput"))
    xq_d = ap(nc.dram_tensor("xq", [NTOK, D], F32, kind="ExternalInput"))
    wattn_d = ap(nc.dram_tensor("wattn", [D, 3 * D], F32, kind="ExternalInput"))
    wproj_d = ap(nc.dram_tensor("wproj", [D, D], F32, kind="ExternalInput"))
    gatew_d = ap(nc.dram_tensor("gatew", [D, NE], F32, kind="ExternalInput"))
    gateb_d = ap(nc.dram_tensor("gateb", [1, NE], F32, kind="ExternalInput"))
    upw_d = ap(nc.dram_tensor("upw", [NE, D, DI], F32, kind="ExternalInput"))
    upb_d = ap(nc.dram_tensor("upb", [NE, DI], F32, kind="ExternalInput"))
    downw_d = ap(nc.dram_tensor("downw", [NE, DI, D], F32, kind="ExternalInput"))
    downb_d = ap(nc.dram_tensor("downb", [NE, D], F32, kind="ExternalInput"))
    swup_d = ap(nc.dram_tensor("swup", [D, DI], F32, kind="ExternalInput"))
    sbup_d = ap(nc.dram_tensor("sbup", [1, DI], F32, kind="ExternalInput"))
    swdown_d = ap(nc.dram_tensor("swdown", [DI, D], F32, kind="ExternalInput"))
    sbdown_d = ap(nc.dram_tensor("sbdown", [1, D], F32, kind="ExternalInput"))
    consts_d = ap(nc.dram_tensor("consts", [128, 404], F32, kind="ExternalInput"))
    consts2_d = ap(nc.dram_tensor("consts2", [32, 2048], F32, kind="ExternalInput"))
    y_d = ap(nc.dram_tensor("y", [NTOK, D], F32, kind="ExternalOutput"))
    qfm_d = ap(nc.dram_tensor("qfm_s", [D, NTOK], BF16, kind="Internal"))
    xres_d = ap(nc.dram_tensor("xres_s", [D, NTOK], F32, kind="Internal"))
    x1_d = ap(nc.dram_tensor("x1_s", [D, NTOK], F32, kind="Internal"))
    x1bf_d = ap(nc.dram_tensor("x1bf_s", [D, NTOK], BF16, kind="Internal"))
    ltc_d = ap(nc.dram_tensor("ltc_s", [16, 32 * 128], F32, kind="Internal"))
    kv_d = ap(nc.dram_tensor("kv_s", [T, NH * 128], BF16, kind="Internal"))
    taps = {}
    if DEBUG:
        for nm, shp in [("t_last", [128, 512]), ("t_hi", [128, 16]),
                        ("t_x1", [128, 8 * 512]), ("t_wt", [8, NTOK]),
                        ("t_tc1", [128, 32]), ("t_tc2", [128, 32]),
                        ("t_bt1", [128, 32]), ("t_kvg", [128, 650]),
                        ("t_cmp", [32, 256])]:
            taps[nm] = ap(nc.dram_tensor(nm, shp, F32, kind="ExternalOutput"))

    es = ExitStack()
    sem_names = ["pe", "dve", "act", "sy", "gd", "x0", "x1", "up0", "up1",
                 "dn0", "dn1", "dn2", "dn3", "pf0", "pf1", "out"]
    sems = {n: es.enter_context(nc.semaphore(n)) for n in sem_names}

    CONSTS = es.enter_context(nc.sbuf_tensor("CONSTS", [128, 404], F32))
    QNBUF = es.enter_context(nc.sbuf_tensor("QNBUF", [128, 64], F32))
    KNBUF = es.enter_context(nc.sbuf_tensor("KNBUF", [128, 64], F32))
    XLASTBF = es.enter_context(nc.sbuf_tensor("XLASTBF", [128, 8], BF16))
    QBD = es.enter_context(nc.sbuf_tensor("QBD", [128, 8 * 16], BF16))
    EPSC = es.enter_context(nc.sbuf_tensor("EPSC", [128, 1], F32))
    IDENTB = es.enter_context(nc.sbuf_tensor("IDENTB", [128, 128], BF16))
    _open_stacks = []
    stKV = ExitStack()
    _open_stacks.append(stKV)
    WPROJS = stKV.enter_context(nc.sbuf_tensor("WPROJS", [128, 8 * 1024], BF16))
    IOTA2P = stKV.enter_context(nc.sbuf_tensor("IOTA2P", [32, 2048], F32))
    LTC = stKV.enter_context(nc.sbuf_tensor("LTC", [16, 32 * 128], F32))
    KVG = stKV.enter_context(nc.sbuf_tensor("KVG", [128, NH * 5 * 130], BF16))
    KFG = stKV.enter_context(nc.sbuf_tensor("KFG", [128, 8 * 5 * 128], BF16))

    blk = es.enter_context(nc.Block())
    P = Prog(nc, blk, sems, dma_hints)
    IDENT = CONSTS[:, 0:128]
    ONES = CONSTS[:, 128:256]
    BLK2 = CONSTS[:, 256:258]
    SAME = CONSTS[:, 274:402]
    RECPAT = CONSTS[:, 402:403]

    from concourse.bass_types import AP as _APT

    def _ap(x):
        return x if isinstance(x, _APT) else x[:]

    def MM(ps, lhsT, rhs, start, stop, waits=(), tp=None):
        return P.pe(lambda t, a=_ap(ps), l=_ap(lhsT), r=_ap(rhs), s1=start, s2=stop, q=tp:
                    t.matmul(a, l, r, start=s1, stop=s2, tile_position=q), waits, inc=True)

    def TR(ps, src, waits=()):
        sap = _ap(src)
        n = sap.shape[0]
        if sap.tensor.dtype == BF16:
            idt, w2 = IDENTB, [("gd", tok_idb)]
        else:
            idt, w2 = CONSTS, []
        return P.pe(lambda t, a=_ap(ps), s=sap, i=idt[0:n, 0:n]: t.transpose(a, s, i),
                    list(waits) + w2)


    def _trunc_tail():
        for st_ in reversed(_open_stacks):
            st_.close()
        OUTZ = es.enter_context(nc.sbuf_tensor("OUTZ", [128, 1024], F32))
        tok_z = P.v(lambda e: e.memset(OUTZ[:], 0.0))
        for tc in range(16):
            P.dma("sync", "out", lambda e, t=tc: e.dma_start(
                out=y_d[t * 128:(t + 1) * 128, :], in_=OUTZ[:]), waits=[("dve", tok_z)])
        P.emit("sync", lambda e: None,
               waits=[("out", P.cnt["out"]), P.wall("sy"), P.wall("gd"),
                      P.wall("up0"), P.wall("up1"), P.wall("dn0"), P.wall("dn1"),
                      P.wall("dn2"), P.wall("dn3"),
                      P.wall("x0"), P.wall("x1"), P.wall("pf0"), P.wall("pf1"),
                      ("dve", P.cnt["dve"]), ("act", P.cnt["act"]), ("pe", P.cnt["pe"])])
        P.flush()
        es.close()
        return nc, P.dma_hints

    P.dma("sync", "sy", lambda e: e.dma_start(out=CONSTS[:], in_=consts_d))
    tok_consts = P.dma("sync", "sy", lambda e: e.dma_start(out=IOTA2P[:], in_=consts2_d),
                       waits=[P.wall("sy")])
    tok_idb = P.dma("gpsimd", "gd", lambda e: e.dma_start(out=IDENTB[:], in_=consts_d[:, 0:128]))
    tok_eps = P.v(lambda e: e.memset(EPSC[:], 1.0e-5))
    tok_qnz = P.v(lambda e: e.memset(QNBUF[:], 0.0))
    tok_knz = P.v(lambda e: e.memset(KNBUF[:], 0.0))
    wc = [("sy", tok_consts)]

    stA = ExitStack()
    psbA = lambda name, shape, dt=F32: stA.enter_context(nc.psum_tensor(name, shape, dt))
    WQ1 = stA.enter_context(nc.sbuf_tensor("WQ1", [128, 8 * 1024], BF16))
    WKV = stA.enter_context(nc.sbuf_tensor("WKV", [128, 8 * 2048], BF16))
    XT4 = [stA.enter_context(nc.sbuf_tensor(f"XT4_{i}", [128, D], F32)) for i in range(2)]
    XFMF = stA.enter_context(nc.sbuf_tensor("XFMF", [128, 8 * 512], F32))
    XFM = stA.enter_context(nc.sbuf_tensor("XFM", [128, 8 * 512], BF16))
    QST2 = [stA.enter_context(nc.sbuf_tensor(f"QST2_{i}", [128, 512], BF16)) for i in range(2)]
    KVST = [stA.enter_context(nc.sbuf_tensor(f"KVST{i}", [128, 512], BF16)) for i in range(4)]
    SQA = stA.enter_context(nc.sbuf_tensor("SQA", [128, 512], F32))
    QLASTBF = stA.enter_context(nc.sbuf_tensor("QLASTBF", [128, 8], BF16))
    XROWB = stA.enter_context(nc.sbuf_tensor("XROWB", [1, 1024], BF16))
    U8 = stA.enter_context(nc.sbuf_tensor("U8", [128, 8 * 16], BF16))
    WKT = [stA.enter_context(nc.sbuf_tensor(f"WKT{i}", [128, 128], BF16)) for i in range(2)]
    LT8 = [stA.enter_context(nc.sbuf_tensor(f"LT8_{i}", [128, 16], F32)) for i in range(2)]
    PS_TRF = [psbA(f"PS_TRF{i}", [128, 128]) for i in range(2)]
    PS_MM = [psbA(f"PS_MM{i}", [128, 512]) for i in range(2)]
    PS_N = psbA("PS_N", [2, 512])
    PS_Q1 = psbA("PS_Q1", [128, 16])

    P.dma("gpsimd", "gd", lambda e: e.dma_start(out=XROWB[:], in_=x_d[T - 1:T, :]))
    tok_wq = P.dma("gpsimd", "gd", lambda e: e.dma_start(
        out=WQ1.ap().rearrange("p (dc f) -> p dc f", f=1024),
        in_=wattn_d[:, 0:1024].rearrange("(dc p) f -> p dc f", p=128)))
    xt_free = [0, 0]
    tr_bank = [0, 0]
    mm_bank = [0, 0]
    sq_free = 0
    psn_free = 0
    xstash_sy = 0

    half_i = [0]
    pending = []

    def x_load_next():
        if not pending:
            return
        rows = pending.pop(0)
        slot = half_i[0] % 2
        half_i[0] += 1
        src = rows.rearrange("(tc p) d -> p tc d", p=128)
        P.dma("sync", "x0" if slot == 0 else "x1",
              lambda e, s=src, sl=slot: e.dma_start(
                  out=XT4[sl].ap().rearrange("p (tc d) -> p tc d", d=D), in_=s),
              waits=[("pe", xt_free[slot])])

    def norm_stat(src_bf, buf, col, wtok):
        nonlocal sq_free, psn_free
        t_sq = P.s(lambda e, s=src_bf: e.activation(SQA[:], s, AF.Square),
                   waits=wtok + [("pe", sq_free)])
        tok = MM(PS_N[:], BLK2, SQA[:], True, True,
                 waits=[("act", t_sq), ("dve", psn_free)] + wc)
        sq_free = tok
        psn_free = P.v(lambda e, b=buf, c=col: e.reduce_max(
            b[0:2, c:c + 1], PS_N[0:2, :], axis=AX.X), waits=[("pe", tok)])
        return psn_free

    proc_i = [0]

    def transpose_pass(make_bf16):
        ev = 0
        for h2 in range(4):
            slot = proc_i[0] % 2
            sem = "x0" if slot == 0 else "x1"
            proc_i[0] += 1
            for dcc in range(8):
                src = XT4[slot][:, dcc * 128: dcc * 128 + 128]
                TR(PS_TRF[dcc % 2], src,
                   waits=[P.wall(sem), ("dve", tr_bank[dcc % 2])] + wc)
                ev = P.v(lambda e, b2=dcc % 2, tc=h2, dc=dcc: e.tensor_copy(
                    XFMF[:, dc * 512 + tc * 128: dc * 512 + tc * 128 + 128], PS_TRF[b2][:]),
                    waits=[("pe", P.cnt["pe"]), ("sy", xstash_sy)])
                tr_bank[dcc % 2] = ev
            xt_free[slot] = P.cnt["pe"]
            x_load_next()
        if make_bf16:
            tok_bf = P.v(lambda e: e.tensor_copy(XFM[:], XFMF[:]), waits=[("dve", ev)])
            return ev, tok_bf
        return ev, ev

    tok_xlt = 0
    stXL = ExitStack()
    PS_XLB = stXL.enter_context(nc.psum_tensor("PS_XLB", [128, 16], BF16))
    for dc in range(8):
        TR(PS_XLB[:, 2 * dc:2 * dc + 1], XROWB[0:1, dc * 128:(dc + 1) * 128],
           waits=[P.wall("gd"), ("dve", tok_xlt)] + wc)
        tok_xlt = P.v(lambda e, d=dc: e.tensor_copy(XLASTBF[:, d:d + 1],
                                                    PS_XLB[:, 2 * d:2 * d + 1]),
                      waits=[("pe", P.cnt["pe"])])
    stXL.close()

    for qst in range(4):
        for h2 in range(4):
            pending.append(xq_d[qst * 512 + h2 * 128: qst * 512 + (h2 + 1) * 128, :])
    for st in range(8):
        for h2 in range(4):
            pending.append(x_d[st * 512 + h2 * 128: st * 512 + (h2 + 1) * 128, :])
    x_load_next()
    x_load_next()
    for qst in range(4):
        ev, tok_bf = transpose_pass(True)
        for qfc in range(8):
            for dc in range(8):
                tok_mm = MM(PS_MM[qfc % 2],
                            WQ1[:, dc * 1024 + qfc * 128: dc * 1024 + qfc * 128 + 128],
                            XFM[:, dc * 512:(dc + 1) * 512],
                            start=(dc == 0), stop=(dc == 7),
                            waits=[("dve", tok_bf), ("dve", mm_bank[qfc % 2]),
                                   ("gd", tok_wq)])
            tok_ev = P.v(lambda e, qf=qfc: e.tensor_copy(
                QST2[qf % 2][:], PS_MM[qf % 2][:]),
                waits=[("pe", tok_mm), P.wall("up0" if qfc % 2 == 0 else "up1")])
            mm_bank[qfc % 2] = tok_ev
            norm_stat(QST2[qfc % 2][:], QNBUF, qst * 8 + qfc, [("dve", tok_ev)])
            P.dma("gpsimd", "up0" if qfc % 2 == 0 else "up1",
                  lambda e, q=qst, qf=qfc: e.dma_start(
                      out=qfm_d[qf * 128:(qf + 1) * 128, q * 512:(q + 1) * 512],
                      in_=QST2[qf % 2][:]), waits=[("dve", tok_ev)])
        P.dma("sync", "sy", lambda e, q=qst: e.dma_start(
            out=xres_d.rearrange("(dc p) t -> p dc t", p=128)[:, :, q * 512:(q + 1) * 512],
            in_=XFMF.ap().rearrange("p (dc t) -> p dc t", t=512)),
            waits=[("dve", tok_bf)])
        xstash_sy = P.cnt["sy"]
    tok_qle = 0
    for qfc in range(8):
        for dc in range(8):
            tok_mm = MM(PS_Q1[:, 0:1],
                        WQ1[:, dc * 1024 + qfc * 128: dc * 1024 + qfc * 128 + 128],
                        XLASTBF[:, dc:dc + 1], start=(dc == 0), stop=(dc == 7),
                        waits=[("dve", tok_xlt), ("dve", tok_qle)])
        tok_qle = P.v(lambda e, qf=qfc: e.tensor_copy(QLASTBF[:, qf:qf + 1], PS_Q1[:, 0:1]),
                      waits=[("pe", tok_mm)])
    tok_qbd = P.v(lambda e: e.memset(QBD[:], 0.0))
    for dc in range(8):
        P.v(lambda e, d=dc: e.tensor_copy(QBD[0:64, d * 16 + 2 * d: d * 16 + 2 * d + 1],
                                          QLASTBF[0:64, d:d + 1]),
            waits=[("dve", tok_qle), ("dve", tok_qbd)])
        tok_qbd = P.v(lambda e, d=dc: e.tensor_copy(
            QBD[64:128, d * 16 + 2 * d + 1: d * 16 + 2 * d + 2],
            QLASTBF[64:128, d:d + 1]))

    tok_wkv = P.dma("gpsimd", "gd", lambda e: e.dma_start(
        out=WKV.ap().rearrange("p (dc f) -> p dc f", f=2048),
        in_=wattn_d[:, 1024:3072].rearrange("(dc p) f -> p dc f", p=128)),
        waits=[("pe", P.cnt["pe"])])
    tok_u = 0
    wkt_bank = [0, 0]
    stWK = ExitStack()
    PS_WKT = [stWK.enter_context(nc.psum_tensor(f"PS_WKT{i}", [128, 128], BF16))
              for i in range(2)]
    for dc in range(8):
        for kc in range(8):
            TR(PS_WKT[kc % 2], WKV[:, dc * 2048 + kc * 128: dc * 2048 + kc * 128 + 128],
               waits=[("gd", tok_wkv), ("dve", wkt_bank[kc % 2])])
            tok_wt2 = P.v(lambda e, k=kc % 2: e.tensor_copy(WKT[k][:], PS_WKT[k][:]),
                          waits=[("pe", P.cnt["pe"])])
            wkt_bank[kc % 2] = tok_wt2
            tok_umm = MM(PS_Q1[:], WKT[kc % 2][:], QBD[:, kc * 16:(kc + 1) * 16],
                         start=(kc == 0), stop=(kc == 7),
                         waits=[("dve", tok_wt2), ("dve", tok_qbd), ("dve", tok_u)])
        tok_u = P.v(lambda e, d=dc: e.tensor_copy(U8[:, d * 16:(d + 1) * 16], PS_Q1[:]),
                    waits=[("pe", tok_umm)])
    stWK.close()

    PS_T16A = [psbA(f"PS_T16A{i}", [16, 128]) for i in range(2)]
    kvst_free = [0, 0, 0, 0]
    tok_lt8 = [0, 0]
    lt_bank = [0, 0]
    for st in range(8):
        ev, tok_bf = transpose_pass(True)
        for tc4 in range(4):
            lhs_tok = tok_bf
            for part in range(2):
                for dc in range(8):
                    w = [("gd", tok_wkv), ("dve", lhs_tok)]
                    if dc == 0:
                        w += [("dve", kvst_free[0]), ("dve", kvst_free[1]),
                              ("dve", kvst_free[2]), ("dve", kvst_free[3])]
                    lhsT = XFM[:, dc * 512 + tc4 * 128: dc * 512 + tc4 * 128 + 128]
                    MM(PS_MM[0], lhsT, WKV[:, dc * 2048 + part * 1024: dc * 2048 + part * 1024 + 512],
                       start=(dc == 0), stop=(dc == 7), waits=w)
                    tok_mm = MM(PS_MM[1], lhsT,
                                WKV[:, dc * 2048 + part * 1024 + 512: dc * 2048 + part * 1024 + 1024],
                                start=(dc == 0), stop=(dc == 7))
                for half in range(2):
                    tok_ev = P.v(lambda e, p2=part, h3=half: e.tensor_copy(
                        KVST[p2 * 2 + h3][:], PS_MM[h3][:]),
                        waits=[("pe", tok_mm)])
                    kvst_free[part * 2 + half] = tok_ev
                    P.dma("sync", "out", lambda e, s2=st, t4=tc4, p2=part, h3=half: e.dma_start(
                        out=kv_d.rearrange("t (h z) -> t h z", z=128)[
                            s2 * 512 + t4 * 128: s2 * 512 + (t4 + 1) * 128,
                            h3 * 8:(h3 + 1) * 8, p2 * 64:(p2 + 1) * 64],
                        in_=KVST[p2 * 2 + h3].ap().rearrange("p (h d2) -> p h d2", d2=64)),
                        waits=[("dve", tok_ev)])
            for dc in range(8):
                tok_lmm = MM(PS_Q1[:],
                             XFM[:, dc * 512 + tc4 * 128: dc * 512 + tc4 * 128 + 128],
                             U8[:, dc * 16:(dc + 1) * 16],
                             start=(dc == 0), stop=(dc == 7),
                             waits=[("dve", tok_u), ("dve", tok_lt8[0]),
                                    ("dve", tok_lt8[1])] if dc == 0 else ())
            kb = (st * 4 + tc4) % 2
            tok_l8 = P.v(lambda e, k=kb: e.tensor_copy(LT8[k][:], PS_Q1[:]),
                         waits=[("pe", tok_lmm)])
            tok_lt8[kb] = tok_l8
            TR(PS_T16A[kb], LT8[kb][:], waits=[("dve", tok_l8), ("dve", lt_bank[kb])])
            lt_bank[kb] = P.v(lambda e, k=kb, kc2=st * 4 + tc4: e.tensor_copy(
                LTC[:, kc2 * 128:(kc2 + 1) * 128], PS_T16A[k][0:16, :]),
                waits=[("pe", P.cnt["pe"])])
    P.barrier()
    stA.close()
    if PHASES <= 1:
        return _trunc_tail()

    stLB = ExitStack()
    LAST = stLB.enter_context(nc.sbuf_tensor("LAST", [128, 512], F32))
    SC = stLB.enter_context(nc.sbuf_tensor("SC", [128, 16], F32))
    GS = stLB.enter_context(nc.sbuf_tensor("GS", [1, 8], F32))
    MASKB = stLB.enter_context(nc.sbuf_tensor("MASKB", [128, 512], F32))
    MI = stLB.enter_context(nc.sbuf_tensor("MI", [32, 2048], F32))
    COMPACT = stLB.enter_context(nc.sbuf_tensor("COMPACT", [32, 256], F32))
    TC1 = stLB.enter_context(nc.sbuf_tensor("TC1", [128, 32], F32))
    TC2 = stLB.enter_context(nc.sbuf_tensor("TC2", [128, 32], F32))
    VB1 = stLB.enter_context(nc.sbuf_tensor("VB1", [128, 32], F32))
    VB2 = stLB.enter_context(nc.sbuf_tensor("VB2", [128, 32], F32))
    BT1 = stLB.enter_context(nc.sbuf_tensor("BT1", [128, 32], F32))
    BT2 = stLB.enter_context(nc.sbuf_tensor("BT2", [128, 32], F32))
    BREC = stLB.enter_context(nc.sbuf_tensor("BREC", [128, 1], F32))
    IDX1 = stLB.enter_context(nc.sbuf_tensor("IDX1", [128, 32], mybir.dt.int32))
    IDX2 = stLB.enter_context(nc.sbuf_tensor("IDX2", [128, 32], mybir.dt.int32))
    IDF = stLB.enter_context(nc.sbuf_tensor("IDF", [128, 32], F32))
    M0COL = stLB.enter_context(nc.sbuf_tensor("M0COL", [128, 1], F32))
    OFF2 = stLB.enter_context(nc.sbuf_tensor("OFF2", [128, 1], F32))
    KNM = stLB.enter_context(nc.sbuf_tensor("KNM", [128, 4], F32))
    SQG = stLB.enter_context(nc.sbuf_tensor("SQG", [128, 64], F32))

    stL2 = ExitStack()
    PS_C = stL2.enter_context(nc.psum_tensor("PS_C", [128, 4], F32))
    PS_T16 = [stL2.enter_context(nc.psum_tensor(f"PS_T16{i}", [16, 128], F32))
              for i in range(2)]
    PS_TX = stL2.enter_context(nc.psum_tensor("PS_TX", [128, 32], F32))
    PS_KT = [stL2.enter_context(nc.psum_tensor(f"PS_KT{i}", [64, 128], BF16))
             for i in range(2)]

    tok_wproj = P.dma("gpsimd", "gd", lambda e: e.dma_start(
        out=WPROJS.ap().rearrange("p (hp f) -> p hp f", f=1024),
        in_=wproj_d.rearrange("(hp p) f -> p hp f", p=128)),
        waits=[("pe", P.cnt["pe"])])

    tok_neg = P.v(lambda e: e.memset(LTC[0:16, 4032:4096], -1.0e30),
                  waits=[("dve", lt_bank[0]), ("dve", lt_bank[1])])
    P.dma("sync", "sy", lambda e: e.dma_start(out=ltc_d, in_=LTC[:]),
          waits=[("dve", tok_neg), ("pe", P.cnt["pe"])])
    P.dma("sync", "sy", lambda e: e.dma_start(
        out=LAST[:], in_=ltc_d.rearrange("h (j f) -> (h j) f", j=8)),
        waits=[P.wall("sy")])
    wlast = [P.wall("sy")]
    tok_rmin = P.v(lambda e: e.tensor_reduce(SC[:, 0:1], LAST[:, 0:448], axis=AX.X, op=ALU.min),
                   waits=wlast)
    tok_rmax = P.v(lambda e: e.reduce_max(SC[:, 1:2], LAST[:], axis=AX.X), waits=wlast)
    TR(PS_T16[0][0:1, :], SC[:, 0:1], waits=[("dve", tok_rmax), ("dve", tok_rmin)])
    TR(PS_T16[1][0:1, :], SC[:, 1:2], waits=[("dve", tok_rmax)])
    tok_tr2 = P.cnt["pe"]
    tok_gmin = P.v(lambda e: e.tensor_reduce(GS[0:1, 0:1], PS_T16[0][0:1, :], axis=AX.X,
                                             op=ALU.min), waits=[("pe", tok_tr2)])
    tok_gmax = P.v(lambda e: e.reduce_max(GS[0:1, 1:2], PS_T16[1][0:1, :], axis=AX.X))
    tok_gm2 = P.v(lambda e: e.tensor_scalar(GS[0:1, 0:1], GS[0:1, 0:1], -10.0, None, op0=ALU.add))
    tok_mmlo = MM(PS_C[:, 0:2], ONES[0:1, :], GS[0:1, 0:2], True, True,
                  waits=[("dve", tok_gm2), ("dve", tok_gmax)])
    tok_dve = P.v(lambda e: e.tensor_copy(SC[:, 0:2], PS_C[:, 0:2]), waits=[("pe", tok_mmlo)])
    for it in range(NBIS):
        P.v(lambda e: e.tensor_tensor(SC[:, 2:3], SC[:, 0:1], SC[:, 1:2], op=ALU.add),
            waits=[("dve", tok_dve)])
        P.v(lambda e: e.tensor_scalar(SC[:, 2:3], SC[:, 2:3], 0.5, None, op0=ALU.mult))
        P.v(lambda e: e.tensor_scalar(MASKB[:], LAST[:], SC[:, 2:3], None, op0=ALU.is_gt),
            waits=wlast)
        tok_cp = P.v(lambda e: e.reduce_sum(SC[:, 3:4], MASKB[:], axis=AX.X))
        tok_cmm = MM(PS_C[:, 2:3], SAME, SC[:, 3:4], True, True,
                     waits=[("dve", tok_cp)] + wc)
        P.v(lambda e: e.tensor_copy(SC[:, 4:5], PS_C[:, 2:3]), waits=[("pe", tok_cmm)])
        P.v(lambda e: e.tensor_scalar(SC[:, 5:6], SC[:, 4:5], float(NSEL), None, op0=ALU.is_gt))
        P.v(lambda e: e.tensor_tensor(SC[:, 6:7], SC[:, 2:3], SC[:, 0:1], op=ALU.subtract))
        P.v(lambda e: e.tensor_tensor(SC[:, 7:8], SC[:, 2:3], SC[:, 1:2], op=ALU.subtract))
        P.v(lambda e: e.tensor_scalar(SC[:, 6:7], SC[:, 6:7], SC[:, 5:6], None, op0=ALU.mult))
        P.v(lambda e: e.tensor_scalar(SC[:, 8:9], SC[:, 7:8], SC[:, 5:6], None, op0=ALU.mult))
        P.v(lambda e: e.tensor_tensor(SC[:, 7:8], SC[:, 7:8], SC[:, 8:9], op=ALU.subtract))
        P.v(lambda e: e.tensor_tensor(SC[:, 0:1], SC[:, 0:1], SC[:, 6:7], op=ALU.add))
        tok_dve = P.v(lambda e: e.tensor_tensor(SC[:, 1:2], SC[:, 1:2], SC[:, 7:8], op=ALU.add))
    tok_mask = P.v(lambda e: e.tensor_scalar(MASKB[:], LAST[:], SC[:, 1:2], None, op0=ALU.is_gt),
                   waits=[("dve", tok_dve)])
    P.dma("sync", "sy", lambda e: e.dma_start(
        out=ltc_d.rearrange("h (j f) -> (h j) f", j=8), in_=MASKB[:]),
        waits=[("dve", tok_mask), P.wall("sy")])
    tok_m2 = P.dma("sync", "sy", lambda e: e.dma_start(
        out=MI[:], in_=ltc_d.rearrange("h (j2 f) -> (h j2) f", j2=2)),
        waits=[P.wall("sy")])
    P.v(lambda e: e.tensor_tensor(MI[:], MI[:], IOTA2P[:], op=ALU.mult),
        waits=[("sy", tok_m2), ("sy", tok_consts)])
    tok_mi = P.v(lambda e: e.tensor_scalar(MI[:], MI[:], -1.0, None, op0=ALU.add))
    for it in range(32):
        P.v(lambda e, i2=it: e.max(out=COMPACT[:, i2 * 8:(i2 + 1) * 8], in_=MI[0:32, :]),
            waits=[("dve", tok_mi)])
        tok_mi = P.v(lambda e, i2=it: e.match_replace(
            out=MI[0:32, :], in_to_replace=COMPACT[:, i2 * 8:(i2 + 1) * 8],
            in_values=MI[0:32, :], imm_value=-1.0))
    tok_ext = tok_mi
    TR(PS_TX, COMPACT[0:32, 0:128], waits=[("dve", tok_ext)])
    tok_tc1 = P.v(lambda e: e.tensor_copy(TC1[:], PS_TX[:]), waits=[("pe", P.cnt["pe"])])
    TR(PS_TX, COMPACT[0:32, 128:256], waits=[("dve", tok_tc1)])
    tok_tc2 = P.v(lambda e: e.tensor_copy(TC2[:], PS_TX[:]), waits=[("pe", P.cnt["pe"])])
    tok_v1 = P.v(lambda e: e.tensor_scalar(VB1[:], TC1[:], -0.5, None, op0=ALU.is_gt),
                 waits=[("dve", tok_tc2)])
    tok_v2 = P.v(lambda e: e.tensor_scalar(VB2[:], TC2[:], -0.5, None, op0=ALU.is_gt))
    P.v(lambda e: e.tensor_tensor(IDF[:], TC1[:], VB1[:], op=ALU.mult))
    tok_idx1 = P.v(lambda e: e.tensor_copy(IDX1[:], IDF[:]))
    P.v(lambda e: e.tensor_tensor(IDF[:], TC2[:], VB2[:], op=ALU.mult))
    tok_idx2 = P.v(lambda e: e.tensor_copy(IDX2[:], IDF[:]))
    tok_kvgz = P.v(lambda e: e.memset(KVG[:], 0.0))
    tok_ones = P.v(lambda e: e.memset(
        KVG.ap().rearrange("p (b z) -> p b z", z=130)[:, :, 128:129], 1.0),
        waits=[("dve", tok_kvgz)])
    gidx_map = {0: (IDX1, VB1, 0), 1: (IDX2, VB2, 0), 2: (IDX1, VB1, 1), 3: (IDX2, VB2, 1)}
    wg = [("dve", tok_idx1), ("dve", tok_idx2), ("dve", tok_ones), P.wall("out")]
    from concourse.bass_core_rust import IndirectOffsetOnAxis as IOA
    for h in range(NH):
        for c in range(4):
            IDXt, VBt, jj = gidx_map[c]
            col = 2 * h + jj
            P.dma("gpsimd", "gd", lambda e, h2=h, c2=c, it=IDXt, cl=col: e.dma_start(
                out=KVG[:, (h2 * 5 + c2) * 130: (h2 * 5 + c2) * 130 + 128],
                in_=kv_d[:, h2 * 128:(h2 + 1) * 128]) if False else
                e.indirect_dma_start(
                    KVG[:, (h2 * 5 + c2) * 130: (h2 * 5 + c2) * 130 + 128],
                    None,
                    kv_d,
                    IOA(it[:, cl:cl + 1], 0),
                    element_offset=h2 * 128),
                waits=wg)
        P.dma("gpsimd", "gd", lambda e, h2=h: e.dma_start(
            out=KVG[0:64, (h2 * 5 + 4) * 130: (h2 * 5 + 4) * 130 + 128],
            in_=kv_d[T - 64:T, h2 * 128:(h2 + 1) * 128]), waits=wg)
    tok_gath = P.cnt["gd"]
    tok_knz2 = P.v(lambda e: e.memset(KNM[:], 0.0))
    kt_bank = [0, 0]
    tok_knm = tok_knz2
    for h in range(NH):
        for c in range(5):
            blk = (h * 5 + c) * 130
            TR(PS_KT[c % 2], KVG[:, blk: blk + 64],
               waits=[("gd", tok_gath), ("dve", kt_bank[c % 2])])
            kt_bank[c % 2] = P.v(lambda e, h2=h, c2=c, k=c % 2: e.tensor_copy(
                KFG[(h2 % 2) * 64:(h2 % 2) * 64 + 64,
                    (h2 // 2) * 640 + c2 * 128: (h2 // 2) * 640 + (c2 + 1) * 128],
                PS_KT[k][:]), waits=[("pe", P.cnt["pe"])])
            t_s = P.s(lambda e, b=blk: e.activation(SQG[:], KVG[:, b:b + 64], AF.Square),
                      waits=[("gd", tok_gath)])
            P.v(lambda e: e.reduce_sum(KNM[:, 1:2], SQG[:], axis=AX.X),
                waits=[("act", t_s)])
            tok_knm = P.v(lambda e: e.tensor_tensor(KNM[:, 0:1], KNM[:, 0:1], KNM[:, 1:2],
                                                    op=ALU.max))
    tok_qn = P.v(lambda e: e.reduce_max(SC[:, 9:10], QNBUF[:, 0:32], axis=AX.X))
    TR(PS_T16[0][0:1, :], SC[:, 9:10], waits=[("dve", tok_qn)])
    TR(PS_T16[1][0:1, :], KNM[:, 0:1], waits=[("dve", tok_knm)])
    tok_tr3 = P.cnt["pe"]
    tok_nm = P.v(lambda e: e.reduce_max(GS[0:1, 2:3], PS_T16[0][0:1, :], axis=AX.X),
                 waits=[("pe", tok_tr3)])
    tok_nm2 = P.v(lambda e: e.reduce_max(GS[0:1, 3:4], PS_T16[1][0:1, :], axis=AX.X))
    tok_sq1 = P.s(lambda e: e.activation(GS[0:1, 4:5], GS[0:1, 2:3], AF.Sqrt),
                  waits=[("dve", tok_nm), ("dve", tok_nm2)])
    tok_sq2 = P.s(lambda e: e.activation(GS[0:1, 5:6], GS[0:1, 3:4], AF.Sqrt))
    tok_m0 = P.v(lambda e: e.tensor_tensor(GS[0:1, 6:7], GS[0:1, 4:5], GS[0:1, 5:6], op=ALU.mult),
                 waits=[("act", tok_sq2)])
    tok_m0mm = MM(PS_C[:, 3:4], ONES[0:1, :], GS[0:1, 6:7], True, True, waits=[("dve", tok_m0)])
    tok_m0c = P.v(lambda e: e.tensor_copy(M0COL[:], PS_C[:, 3:4]), waits=[("pe", tok_m0mm)])
    tok_off = P.v(lambda e: e.tensor_scalar(OFF2[:], M0COL[:], -0.125, 1.0e30,
                                            op0=ALU.mult, op1=ALU.add))
    tok_b1 = P.v(lambda e: e.tensor_scalar(BT1[:], VB1[:], OFF2[:], None, op0=ALU.mult),
                 waits=[("dve", tok_off), ("dve", tok_v1)])
    tok_b1b = P.v(lambda e: e.tensor_scalar(BT1[:], BT1[:], -1.0e30, None, op0=ALU.add))
    tok_b2 = P.v(lambda e: e.tensor_scalar(BT2[:], VB2[:], OFF2[:], None, op0=ALU.mult),
                 waits=[("dve", tok_v2)])
    tok_b2b = P.v(lambda e: e.tensor_scalar(BT2[:], BT2[:], -1.0e30, None, op0=ALU.add))
    tok_brec = P.v(lambda e: e.tensor_scalar(BREC[:], RECPAT, OFF2[:], None, op0=ALU.mult),
                   waits=[("dve", tok_off)] + wc)
    tok_bias = P.v(lambda e: e.tensor_scalar(BREC[:], BREC[:], -1.0e30, None, op0=ALU.add))
    if DEBUG:
        P.dma("sync", "sy", lambda e: e.dma_start(out=taps["t_last"], in_=LAST[:]),
              waits=[("dve", tok_mask)])
        P.dma("sync", "sy", lambda e: e.dma_start(out=taps["t_hi"], in_=SC[:]),
              waits=[("dve", tok_bias)])
        P.dma("sync", "sy", lambda e: e.dma_start(out=taps["t_tc1"], in_=TC1[:]),
              waits=[("dve", tok_bias)])
        P.dma("sync", "sy", lambda e: e.dma_start(out=taps["t_tc2"], in_=TC2[:]),
              waits=[("dve", tok_bias)])
        P.dma("sync", "sy", lambda e: e.dma_start(out=taps["t_bt1"], in_=BT1[:]),
              waits=[("dve", tok_bias)])
        P.dma("sync", "sy", lambda e: e.dma_start(out=taps["t_cmp"], in_=COMPACT[:]),
              waits=[("dve", tok_bias)])
        P.dma("gpsimd", "gd", lambda e: e.dma_start(out=taps["t_kvg"], in_=KVG[:, 0:650]),
              waits=[("gd", tok_gath)])
    P.barrier()
    stL2.close()

    QB = stLB.enter_context(nc.sbuf_tensor("QB", [128, 8 * 512], BF16))
    XRES = stLB.enter_context(nc.sbuf_tensor("XRES", [128, 8 * 512], F32))
    AO = stLB.enter_context(nc.sbuf_tensor("AO", [128, 8 * 512], BF16))
    UT = [stLB.enter_context(nc.sbuf_tensor(f"UT{i}", [128, 512], BF16)) for i in range(4)]
    RECIP = stLB.enter_context(nc.sbuf_tensor("RECIP", [1, 512], F32))
    RB = stLB.enter_context(nc.sbuf_tensor("RB", [64, 512], F32))
    SQB = [stLB.enter_context(nc.sbuf_tensor("SQB0", [128, 512], F32))] * 2
    ROWS = stLB.enter_context(nc.sbuf_tensor("ROWS", [1, 4 * 512], F32))
    AE = stLB.enter_context(nc.sbuf_tensor("AE", [128, 512], F32))
    BE = stLB.enter_context(nc.sbuf_tensor("BE", [128, 512], F32))
    T1B = stLB.enter_context(nc.sbuf_tensor("T1B", [128, 512], F32))
    X1BST = [stLB.enter_context(nc.sbuf_tensor(f"X1BST{i}", [128, 512], BF16)) for i in range(2)]

    def bias_col(h, c):
        if c == 4:
            return BREC[:, 0:1]
        t = BT1 if c in (0, 2) else BT2
        col = 2 * h + (c // 2)
        return t[:, col:col + 1]

    x1b_free = [0, 0]
    PS_S = [stLB.enter_context(nc.psum_tensor(f"PS_S{i}", [128, 512], F32))
            for i in range(4)]
    PS_AV = [stLB.enter_context(nc.psum_tensor(f"PS_AV{i}", [65, 512], F32))
             for i in range(2)]
    PS_PJ = [stLB.enter_context(nc.psum_tensor(f"PS_PJ{i}", [128, 512], F32))
             for i in range(2)]
    exp_toks = [0, 0, 0, 0]
    stat_toks = [0, 0, 0, 0]
    pj_bank = [0, 0]
    for qst in range(4):
        pf = "pf0" if qst % 2 == 0 else "pf1"
        wq = [("pe", P.cnt["pe"]), ("dve", P.cnt["dve"]), ("act", P.cnt["act"]),
              P.wall("sy"), P.wall("up0"), P.wall("up1")]
        if qst == 0:
            P.dma("sync", pf, lambda e, q=qst: e.dma_start(
                out=QB.ap().rearrange("p (qf t) -> p qf t", t=512),
                in_=qfm_d.rearrange("(qf p) t -> p qf t", p=128)[:, :, q * 512:(q + 1) * 512]),
                waits=wq)
        P.dma("sync", pf, lambda e, q=qst: e.dma_start(
            out=XRES.ap().rearrange("p (dc t) -> p dc t", t=512),
            in_=xres_d.rearrange("(dc p) t -> p dc t", p=128)[:, :, q * 512:(q + 1) * 512]),
            waits=wq)
        wqb = [P.wall(pf)]
        av_toks = {}
        ao_toks = [0, 0]
        tok_rbe = 0
        for hp in range(8):
            def emit_av(kc2, sub, hp2=hp):
                h2 = 2 * hp2 + sub
                av_toks[(kc2, sub)] = MM(
                    PS_AV[sub],
                    KVG[:, (h2 * 5 + kc2) * 130 + 64: (h2 * 5 + kc2) * 130 + 129],
                    UT[(kc2 * 2 + sub) % 4][:], start=(kc2 == 0), stop=(kc2 == 4),
                    waits=[("act", exp_toks[(kc2 * 2 + sub) % 4])] +
                          ([("dve", ao_toks[sub])] if kc2 == 0 else []))

            for kc in range(5):
                if kc >= 2:
                    emit_av(kc - 2, 0)
                    emit_av(kc - 2, 1)
                for sub in range(2):
                    h = 2 * hp + sub
                    bank = (kc * 2 + sub) % 4
                    wS = [("act", exp_toks[bank]), ("dve", tok_bias),
                          ("dve", stat_toks[bank])] + wqb
                    tok_s = MM(PS_S[bank],
                               KFG[sub * 64:sub * 64 + 64,
                                   hp * 640 + kc * 128: hp * 640 + kc * 128 + 128],
                               QB[sub * 64:sub * 64 + 64, hp * 512:(hp + 1) * 512],
                               start=True, stop=True, waits=wS, tp=(sub * 64, 0))
                    exp_toks[bank] = P.s(
                        lambda e, k=bank, bc=bias_col(h, kc): e.activation(
                            UT[k][:], PS_S[k][:], AF.Exp, bias=bc, scale=0.125),
                        waits=[("pe", tok_s)] +
                              ([("pe", av_toks[(kc - 2, sub)])] if (kc - 2, sub) in av_toks
                               else []),
                        sw=False)
            for kc2 in (3, 4):
                emit_av(kc2, 0)
                emit_av(kc2, 1)
            av31 = [av_toks[(4, 0)], av_toks[(4, 1)]]
            av_toks = {}
            for sub in range(2):
                tok_rc = P.v(lambda e, sb=sub: e.reciprocal(RECIP[:], PS_AV[sb][64:65, :]),
                             waits=[("pe", av31[sub])])
                tok_rb = MM(PS_PJ[sub][0:64, :], ONES[0:1, 0:64], RECIP[:], True, True,
                            waits=[("dve", tok_rc), ("dve", tok_rbe),
                                   ("dve", pj_bank[sub])])
                tok_rbe = P.v(lambda e, sb=sub: e.tensor_copy(RB[:], PS_PJ[sb][0:64, :]),
                              waits=[("pe", tok_rb)])
                pj_bank[sub] = tok_rbe
                ao_toks[sub] = P.v(lambda e, kt2=hp, b2=sub * 64, sb=sub: e.tensor_tensor(
                    AO[b2:b2 + 64, kt2 * 512:(kt2 + 1) * 512], PS_AV[sb][0:64, :], RB[:],
                    op=ALU.mult), waits=[("dve", tok_rbe)])
        tok_ao = ao_toks[1]
        if qst < 3:
            pf2 = "pf0" if (qst + 1) % 2 == 0 else "pf1"
            P.dma("sync", pf2, lambda e, q=qst + 1: e.dma_start(
                out=QB.ap().rearrange("p (qf t) -> p qf t", t=512),
                in_=qfm_d.rearrange("(qf p) t -> p qf t", p=128)[:, :, q * 512:(q + 1) * 512]),
                waits=[("pe", P.cnt["pe"])])
        ev_toks = []
        for hc in range(8):
            for hp in range(8):
                tok_pj = MM(PS_PJ[hc % 2],
                            WPROJS[:, hp * 1024 + hc * 128: hp * 1024 + hc * 128 + 128],
                            AO[:, hp * 512:(hp + 1) * 512],
                            start=(hp == 0), stop=(hp == 7),
                            waits=([("dve", tok_ao), ("gd", tok_wproj),
                                    ("dve", pj_bank[hc % 2])] +
                                   ([("dve", ev_toks[hc - 2])] if hc >= 2 else []))
                            if hp == 0 else ())
            ev_toks.append(P.v(lambda e, c=hc: e.tensor_tensor(
                XRES[:, c * 512:(c + 1) * 512], PS_PJ[c % 2][:],
                XRES[:, c * 512:(c + 1) * 512], op=ALU.add),
                waits=[("pe", tok_pj)] + wqb))
        pj_bank = [ev_toks[6], ev_toks[7]]
        sq_tok = [0, 0]
        tok_sq_mm = 0
        ev_all = ev_toks[7]
        wact = [("act", P.cnt["act"])]
        for hc in range(8):
            sq_tok[hc % 2] = P.s(lambda e, c=hc: e.activation(
                SQB[c % 2][:], XRES[:, c * 512:(c + 1) * 512], AF.Square),
                waits=[("dve", ev_toks[hc]), ("pe", tok_sq_mm)])
            MM(PS_S[0][0:1, :], ONES[:, 0:1], XRES[:, hc * 512:(hc + 1) * 512],
               start=(hc == 0), stop=(hc == 7), waits=[("dve", ev_all)] + wact)
            tok_sq_mm = MM(PS_S[1][0:1, :], ONES[:, 0:1], SQB[hc % 2][:],
                           start=(hc == 0), stop=(hc == 7),
                           waits=[("act", sq_tok[hc % 2])] + wact)
        R = lambda i: ROWS[0:1, i * 512:(i + 1) * 512]
        tok_m1 = P.v(lambda e: e.tensor_scalar(R(0), PS_S[0][0:1, :], 1.0 / D, None,
                                               op0=ALU.mult), waits=[("pe", tok_sq_mm)])
        P.v(lambda e: e.tensor_scalar(R(1), PS_S[1][0:1, :], 1.0 / D, None, op0=ALU.mult))
        P.v(lambda e: e.tensor_tensor(R(2), R(0), R(0), op=ALU.mult))
        tok_var = P.v(lambda e: e.tensor_tensor(R(3), R(1), R(2), op=ALU.subtract))
        P.s(lambda e: e.activation(R(1), R(3), AF.Ln, bias=EPSC[0:1, 0:1]),
            waits=[("dve", tok_var), ("dve", tok_eps)])
        tok_rstd = P.s(lambda e: e.activation(R(2), R(1), AF.Exp, scale=-0.5))
        P.v(lambda e: e.tensor_tensor(R(3), R(0), R(2), op=ALU.mult), waits=[("act", tok_rstd)])
        tok_nmb = P.v(lambda e: e.tensor_scalar(R(3), R(3), -1.0, None, op0=ALU.mult))
        tok_bc1 = MM(PS_S[2][:], ONES[0:1, :], R(2), True, True,
                     waits=[("dve", tok_nmb), ("act", P.cnt["act"])])
        tok_ae = P.v(lambda e: e.tensor_copy(AE[:], PS_S[2][:]), waits=[("pe", tok_bc1)])
        tok_bc2 = MM(PS_S[3][:], ONES[0:1, :], R(3), True, True,
                     waits=[("dve", tok_nmb), ("act", P.cnt["act"])])
        tok_be = P.v(lambda e: e.tensor_copy(BE[:], PS_S[3][:]), waits=[("pe", tok_bc2)])
        stat_toks = [tok_m1, tok_m1, tok_ae, tok_be]
        for hc in range(8):
            P.v(lambda e, c=hc: e.tensor_tensor(T1B[:], XRES[:, c * 512:(c + 1) * 512],
                                                AE[:], op=ALU.mult),
                waits=[("dve", tok_ae), ("dve", tok_be)])
            tok_x1 = P.v(lambda e, c=hc: e.tensor_tensor(
                XRES[:, c * 512:(c + 1) * 512], T1B[:], BE[:], op=ALU.add))
            tok_x1b = P.v(lambda e, c=hc: e.tensor_copy(X1BST[c % 2][:],
                                                        XRES[:, c * 512:(c + 1) * 512]),
                          waits=[("dve", tok_x1),
                                 ("dn0" if hc % 2 == 0 else "dn1", x1b_free[hc % 2])])
            P.dma("sync", "sy", lambda e, c=hc, q=qst: e.dma_start(
                out=x1_d[c * 128:(c + 1) * 128, q * 512:(q + 1) * 512],
                in_=XRES[:, c * 512:(c + 1) * 512]), waits=[("dve", tok_x1)])
            x1b_free[hc % 2] = P.dma(
                "gpsimd", "dn0" if hc % 2 == 0 else "dn1",
                lambda e, c=hc, q=qst: e.dma_start(
                    out=x1bf_d[c * 128:(c + 1) * 128, q * 512:(q + 1) * 512],
                    in_=X1BST[c % 2][:]), waits=[("dve", tok_x1b)])
        if DEBUG and qst == 0:
            P.dma("sync", "sy", lambda e: e.dma_start(out=taps["t_x1"], in_=XRES[:]),
                  waits=[("dve", tok_x1)])
    P.barrier()
    stLB.close()
    stKV.close()
    if PHASES <= 4:
        return _trunc_tail()

    COMB = es.enter_context(nc.sbuf_tensor("COMB", [128, 8 * NTOK], F32))
    stC = ExitStack()
    X1BF = stC.enter_context(nc.sbuf_tensor("X1BF", [128, 8 * NTOK], BF16))
    HACT = stC.enter_context(nc.sbuf_tensor("HACT", [128, 32 * 1024], BF16))
    UPWS = [stC.enter_context(nc.sbuf_tensor(f"UPWS{i}", [128, 8 * 256], BF16)) for i in range(2)]
    DNWS = [stC.enter_context(nc.sbuf_tensor(f"DNWS{i}", [128, 16 * 128], BF16)) for i in range(4)]
    WB = [stC.enter_context(nc.sbuf_tensor(f"WB{i}", [128, 512], F32)) for i in range(2)]
    WT = stC.enter_context(nc.sbuf_tensor("WT", [128, NTOK], F32))
    TEP = stC.enter_context(nc.sbuf_tensor("TEP", [128, 512], F32))
    GW = stC.enter_context(nc.sbuf_tensor("GW", [128, 8 * 8], BF16))
    GBT = stC.enter_context(nc.sbuf_tensor("GBT", [128, 8], F32))
    RS = stC.enter_context(nc.sbuf_tensor("RS", [128, 48], F32))
    UPB = stC.enter_context(nc.sbuf_tensor("UPB", [128, 9 * 32], F32))
    DNB = stC.enter_context(nc.sbuf_tensor("DNB", [128, 9 * 8], F32))
    BRAW = stC.enter_context(nc.sbuf_tensor("BRAW", [32, 128], F32))
    OHT = stC.enter_context(nc.sbuf_tensor("OHT", [128, 8], F32))
    OH1 = stC.enter_context(nc.sbuf_tensor("OH1", [8, 128], F32))

    tok_x1bf = P.dma("sync", "pf0", lambda e: e.dma_start(
        out=X1BF.ap().rearrange("p (dc t) -> p dc t", t=NTOK),
        in_=x1bf_d.rearrange("(dc p) t -> p dc t", p=128)),
        waits=[P.wall("dn0"), P.wall("dn1")])
    wx1 = [P.wall("pf0")]
    tok_gw = P.dma("gpsimd", "gd", lambda e: e.dma_start(
        out=GW.ap().rearrange("p (dc g) -> p dc g", g=8),
        in_=gatew_d.rearrange("(dc p) g -> p dc g", p=128)),
        waits=[("pe", P.cnt["pe"])])

    stR = ExitStack()
    PS_R8 = stR.enter_context(nc.psum_tensor("PS_R8", [128, 8], F32))
    PS_RT = stR.enter_context(nc.psum_tensor("PS_RT", [8, 128], F32))
    PS_B = stR.enter_context(nc.psum_tensor("PS_B", [128, 40], F32))

    tok_gb = P.dma("sync", "pf1", lambda e: e.dma_start(out=RS[0:1, 0:8], in_=gateb_d))
    tok_gbmm = MM(PS_R8[:, 0:8], ONES[0:1, :], RS[0:1, 0:8], True, True,
                  waits=[P.wall("pf1")] + wc)
    tok_gbt = P.v(lambda e: e.tensor_copy(GBT[:], PS_R8[:, 0:8]), waits=[("pe", tok_gbmm)])
    tok_tr_prev = tok_gbt
    for e_i in range(9):
        if e_i < 8:
            src_u = upb_d[e_i, :].rearrange("(m p) -> m p", p=128)
            src_d = downb_d[e_i, :].rearrange("(m p) -> m p", p=128)
        else:
            src_u = sbup_d[0, :].rearrange("(m p) -> m p", p=128)
            src_d = sbdown_d[0, :].rearrange("(m p) -> m p", p=128)
        P.dma("sync", "pf1", lambda e, s=src_u: e.dma_start(out=BRAW[:], in_=s),
              waits=[("dve", tok_tr_prev)])
        TR(PS_B[:, 0:32], BRAW[:], waits=[P.wall("pf1")])
        tok_tru = P.v(lambda e, i=e_i: e.tensor_copy(UPB[:, i * 32:(i + 1) * 32], PS_B[:, 0:32]),
                      waits=[("pe", P.cnt["pe"])])
        P.dma("sync", "pf1", lambda e, s=src_d: e.dma_start(out=BRAW[0:8, :], in_=s),
              waits=[("dve", tok_tru)])
        TR(PS_B[:, 32:40], BRAW[0:8, :], waits=[P.wall("pf1"), ("dve", tok_tru)])
        tok_tr_prev = P.v(lambda e, i=e_i: e.tensor_copy(DNB[:, i * 8:(i + 1) * 8], PS_B[:, 32:40]),
                          waits=[("pe", P.cnt["pe"])])
    tok_wtz = P.v(lambda e: e.memset(WT[:], 0.0))
    tok_oht = P.v(lambda e: e.memset(OHT[:], 0.0))
    tok_wtp = 0
    tok_rs_prev = 0
    for tc in range(16):
        for dc in range(8):
            tok_r = MM(PS_R8[:, 0:8],
                       X1BF[:, dc * NTOK + tc * 128: dc * NTOK + tc * 128 + 128],
                       GW[:, dc * 8:(dc + 1) * 8], start=(dc == 0), stop=(dc == 7),
                       waits=wx1 + [P.wall("gd"), ("dve", tok_gbt), ("dve", tok_rs_prev)])
        tok_a = P.v(lambda e: e.tensor_tensor(RS[:, 16:24], PS_R8[:, 0:8], GBT[:], op=ALU.add),
                    waits=[("pe", tok_r)])
        tok_sg = P.s(lambda e: e.activation(RS[:, 24:32], RS[:, 16:24], AF.Sigmoid),
                     waits=[("dve", tok_a)])
        S8 = RS[:, 24:32]
        P.v(lambda e: e.reduce_max(RS[:, 32:33], S8, axis=AX.X), waits=[("act", tok_sg)])
        P.v(lambda e: e.tensor_scalar(RS[:, 40:48], S8, RS[:, 32:33], None, op0=ALU.is_equal))
        P.v(lambda e: e.scalar_tensor_tensor(RS[:, 40:48], RS[:, 40:48], -1.0e30, S8,
                                             op0=ALU.mult, op1=ALU.add))
        P.v(lambda e: e.reduce_max(RS[:, 33:34], RS[:, 40:48], axis=AX.X))
        P.v(lambda e: e.tensor_tensor(RS[:, 34:35], RS[:, 32:33], RS[:, 33:34], op=ALU.add))
        P.v(lambda e: e.tensor_scalar(RS[:, 34:35], RS[:, 34:35], 1.0e-6, None, op0=ALU.add))
        P.v(lambda e: e.reciprocal(RS[:, 35:36], RS[:, 34:35]))
        P.v(lambda e: e.tensor_scalar(RS[:, 35:36], RS[:, 35:36], 0.1, None, op0=ALU.mult))
        P.v(lambda e: e.tensor_scalar(RS[:, 40:48], S8, RS[:, 33:34], None, op0=ALU.is_ge))
        P.v(lambda e: e.tensor_tensor(RS[:, 40:48], S8, RS[:, 40:48], op=ALU.mult))
        tok_w2 = P.v(lambda e: e.tensor_scalar(RS[:, 40:48], RS[:, 40:48], RS[:, 35:36], None,
                                               op0=ALU.mult))
        tok_rs_prev = tok_w2
        TR(PS_RT[:], RS[:, 40:48], waits=[("dve", tok_w2), ("dve", tok_wtp)])
        tok_wtp = P.v(lambda e, t=tc: e.tensor_copy(WT[0:8, t * 128:(t + 1) * 128],
                                                    PS_RT[0:8, :]),
                      waits=[("pe", P.cnt["pe"]), ("dve", tok_wtz)])
    tok_wt = tok_wtp
    if DEBUG:
        P.dma("sync", "sy", lambda e: e.dma_start(out=taps["t_wt"], in_=WT[0:8, :]),
              waits=[("dve", tok_wt)])
    stR.close()

    stE = ExitStack()
    PS_U = [stE.enter_context(nc.psum_tensor(f"PS_U{i}", [128, 512], F32)) for i in range(4)]
    PS_D = [stE.enter_context(nc.psum_tensor(f"PS_D{i}", [128, 512], F32)) for i in range(2)]
    PS_W = [stE.enter_context(nc.psum_tensor(f"PS_W{i}", [128, 512], F32)) for i in range(2)]

    up_free = [0, 0]
    dn_free = [0, 0, 0, 0]
    gelu_tok = {(i, s): 0 for i in range(2) for s in range(2)}
    gelu_last = [0, 0]
    psd_free = [0, 0]
    wb_free = [0, 0]
    wbe_tok = [0, 0]
    comb_tok = {}
    upq = [(e_i, mh, mp) for e_i in range(NEXP) for mh in range(2) for mp in range(16)]

    def up_load(i):
        e_i, mh, mp = upq[i]
        slot = i % 2
        if e_i < 8:
            src = upw_d[e_i, :, mp * 256:(mp + 1) * 256].rearrange("(dc p) f -> p dc f", p=128)
        else:
            src = swup_d[:, mp * 256:(mp + 1) * 256].rearrange("(dc p) f -> p dc f", p=128)
        P.dma("gpsimd", "up0" if slot == 0 else "up1",
              lambda e, s=src, sl=slot: e.dma_start(
                  out=UPWS[sl].ap().rearrange("p (dc f) -> p dc f", f=256), in_=s),
              waits=[("pe", up_free[slot])])

    if upq:
        up_load(0)
        up_load(1)
    upq_i = 2
    gidx = 0
    tok_oh1 = 0
    for e_i in range(NEXP):
        is_sh = e_i == 8
        if not is_sh:
            if e_i > 0:
                P.v(lambda e, i=e_i: e.memset(OHT[:, i - 1:i], 0.0), waits=[("dve", tok_oh1)])
            tok_c1 = P.v(lambda e, i=e_i: e.memset(OHT[:, i:i + 1], 1.0),
                         waits=[("dve", tok_oht)])
            TR(PS_W[0][0:8, 0:128], OHT[:],
               waits=[("dve", tok_c1), ("dve", wbe_tok[0]), ("dve", wbe_tok[1])])
            tok_oh1 = P.v(lambda e: e.tensor_copy(OH1[:], PS_W[0][0:8, 0:128]),
                          waits=[("pe", P.cnt["pe"])])
        for mh in range(2):
            for mp in range(16):
                slot = gidx % 2
                gidx += 1
                for mi2 in range(2):
                    m = mp * 2 + mi2
                    for stq in range(2):
                        bank = mi2 * 2 + stq
                        for dc in range(8):
                            tok_u = MM(PS_U[bank],
                                       UPWS[slot][:, dc * 256 + mi2 * 128:
                                            dc * 256 + mi2 * 128 + 128],
                                       X1BF[:, dc * NTOK + mh * 1024 + stq * 512:
                                            dc * NTOK + mh * 1024 + stq * 512 + 512],
                                       start=(dc == 0), stop=(dc == 7),
                                       waits=([P.wall("up0" if slot == 0 else "up1"),
                                               ("act", gelu_tok[(mi2, stq)]), ("dve", tok_wt)]
                                              + wx1) if dc == 0 else ())
                        bias_col = (UPB[:, e_i * 32 + m: e_i * 32 + m + 1])
                        gelu_tok[(mi2, stq)] = P.s(
                            lambda e, mm_=m, sq=stq, bc=bias_col, bk=bank: e.activation(
                                HACT[:, mm_ * 1024 + sq * 512: mm_ * 1024 + sq * 512 + 512],
                                PS_U[bk][:], AF.Gelu, bias=bc),
                            waits=[("pe", tok_u)], sw=False)
                        gelu_last[stq] = gelu_tok[(mi2, stq)]
                up_free[slot] = P.cnt["pe"]
                if upq_i < len(upq):
                    up_load(upq_i)
                    upq_i += 1
            def dn_load(hc2, mseg, e2=e_i):
                slot = (hc2 * 2 + mseg) % 4
                if e2 < 8:
                    src = downw_d[e2, mseg * 2048:(mseg + 1) * 2048,
                                  hc2 * 128:(hc2 + 1) * 128].rearrange(
                                      "(mm p) f -> p mm f", p=128)
                else:
                    src = swdown_d[mseg * 2048:(mseg + 1) * 2048,
                                   hc2 * 128:(hc2 + 1) * 128].rearrange(
                                       "(mm p) f -> p mm f", p=128)
                P.dma("gpsimd", ("dn0", "dn1", "dn2", "dn3")[slot],
                      lambda e, s=src, sl=slot: e.dma_start(
                          out=DNWS[sl].ap().rearrange("p (mm f) -> p mm f", f=128), in_=s),
                      waits=[("pe", dn_free[slot])])

            for hc in range(8 if MOE_PART >= 2 else 0):
                if hc == 0:
                    dn_load(0, 0)
                    dn_load(0, 1)
                if hc < 7:
                    dn_load(hc + 1, 0)
                    dn_load(hc + 1, 1)
                if hc == 0 and not is_sh and WBC:
                    for stq in range(2):
                        tok_wb = MM(PS_W[stq][:], OH1[:],
                                    WT[0:8, mh * 1024 + stq * 512: mh * 1024 + stq * 512 + 512],
                                    True, True,
                                    waits=[("dve", tok_oh1), ("dve", tok_wt),
                                           ("dve", wbe_tok[stq])])
                        wbe_tok[stq] = P.v(
                            lambda e, sq=stq: e.tensor_copy(WB[sq][:], PS_W[sq][:]),
                            waits=[("pe", tok_wb), ("dve", wb_free[stq])])
                for stq in range(2):
                    for mseg in range(2):
                        slot = (hc * 2 + mseg) % 4
                        wD = [P.wall(("dn0", "dn1", "dn2", "dn3")[slot]),
                              ("act", gelu_last[stq]), ("dve", psd_free[stq])]
                        for mi in range(16):
                            m = mseg * 16 + mi
                            tok_d = MM(PS_D[stq],
                                       DNWS[slot][:, mi * 128:(mi + 1) * 128],
                                       HACT[:, m * 1024 + stq * 512: m * 1024 + stq * 512 + 512],
                                       start=(m == 0), stop=(m == 31),
                                       waits=wD if mi == 0 else ())
                    key = (hc, mh, stq)
                    o = mh * 1024 + stq * 512
                    if MOE_PART < 3:
                        psd_free[stq] = P.v(lambda e, sq=stq: e.tensor_copy(
                            TEP[:], PS_D[sq][:]), waits=[("pe", tok_d)])
                        comb_tok[key] = psd_free[stq]
                    elif not is_sh:
                        tok_t1 = P.v(lambda e, c=hc, i=e_i, sq=stq: e.tensor_scalar(
                            TEP[:], PS_D[sq][:], DNB[:, i * 8 + c: i * 8 + c + 1], None,
                            op0=ALU.add), waits=[("pe", tok_d)])
                        tok_t2 = P.v(lambda e, sq=stq: e.tensor_tensor(
                            TEP[:], TEP[:], WB[sq][:], op=ALU.mult),
                            waits=[("dve", wbe_tok[stq])])
                        wb_free[stq] = tok_t2
                        if e_i == 0:
                            tok_cw = P.v(lambda e, c=hc, o2=o: e.tensor_copy(
                                COMB[:, c * NTOK + o2: c * NTOK + o2 + 512], TEP[:]))
                        else:
                            tok_cw = P.v(lambda e, c=hc, o2=o: e.tensor_tensor(
                                COMB[:, c * NTOK + o2: c * NTOK + o2 + 512],
                                COMB[:, c * NTOK + o2: c * NTOK + o2 + 512], TEP[:],
                                op=ALU.add), waits=[("dve", comb_tok.get(key, 0))])
                    elif is_sh:
                        tok_t1 = P.v(lambda e, c=hc, sq=stq: e.tensor_scalar(
                            TEP[:], PS_D[sq][:], DNB[:, 8 * 8 + c: 8 * 8 + c + 1], 0.1,
                            op0=ALU.add, op1=ALU.mult), waits=[("pe", tok_d)])
                        tok_cw = P.v(lambda e, c=hc, o2=o: e.tensor_tensor(
                            COMB[:, c * NTOK + o2: c * NTOK + o2 + 512],
                            COMB[:, c * NTOK + o2: c * NTOK + o2 + 512], TEP[:],
                            op=ALU.add), waits=[("dve", comb_tok.get(key, 0))])
                    if MOE_PART >= 3:
                        comb_tok[key] = tok_cw
                        psd_free[stq] = tok_t1
                dn_free[(hc * 2) % 4] = P.cnt["pe"]
                dn_free[(hc * 2 + 1) % 4] = P.cnt["pe"]
    stE.close()
    P.barrier()
    stC.close()
    if PHASES <= 5:
        return _trunc_tail()

    stD = ExitStack()
    X1C = [stD.enter_context(nc.sbuf_tensor(f"X1C{i}", [128, 8 * 128], F32)) for i in range(2)]
    FTM = [stD.enter_context(nc.sbuf_tensor(f"FTM{i}", [128, 1024], F32)) for i in range(2)]
    X1TM = [stD.enter_context(nc.sbuf_tensor(f"X1TM{i}", [128, 1024], F32)) for i in range(2)]
    X2 = [stD.enter_context(nc.sbuf_tensor(f"X2_{i}", [128, 1024], F32)) for i in range(2)]
    SQ2 = stD.enter_context(nc.sbuf_tensor("SQ2", [128, 1024], F32))
    DS = stD.enter_context(nc.sbuf_tensor("DS", [128, 8], F32))
    PS_DT = [stD.enter_context(nc.psum_tensor(f"PS_DT{i}", [128, 128], F32)) for i in range(2)]

    x1c_free = [0, 0]
    d_free = [0, 0]

    pe_at_d = P.cnt["pe"]

    def d_load(tc):
        slot = tc % 2
        P.dma("sync", "pf0" if slot == 0 else "pf1", lambda e, t=tc, sl=slot: e.dma_start(
            out=X1C[sl].ap().rearrange("p (dc f) -> p dc f", f=128),
            in_=x1_d[:, t * 128:(t + 1) * 128].rearrange("(dc p) f -> p dc f", p=128)),
            waits=[P.wall("sy"), ("pe", max(pe_at_d, x1c_free[slot]))])

    d_load(0)
    tr_free = [0, 0]
    tok_ln = 0
    tok_s2 = 0
    for tc in range(16):
        slot = tc % 2
        if tc < 15:
            d_load(tc + 1)
        ev_last = 0
        for hc in range(8):
            TR(PS_DT[hc % 2], COMB[:, hc * NTOK + tc * 128: hc * NTOK + tc * 128 + 128],
               waits=[("dve", tr_free[hc % 2])])
            ev_last = P.v(lambda e, c=hc, sl=slot: e.tensor_copy(
                FTM[sl][:, c * 128:(c + 1) * 128], PS_DT[c % 2][:]),
                waits=[("pe", P.cnt["pe"]), ("dve", d_free[slot]),
                       P.wall("out")])
            tr_free[hc % 2] = ev_last
        for dc in range(8):
            TR(PS_DT[dc % 2], X1C[slot][:, dc * 128:(dc + 1) * 128],
               waits=[P.wall("pf0" if slot == 0 else "pf1"), ("dve", tr_free[dc % 2])])
            ev_last = P.v(lambda e, c=dc, sl=slot: e.tensor_copy(
                X1TM[sl][:, c * 128:(c + 1) * 128], PS_DT[c % 2][:]),
                waits=[("pe", P.cnt["pe"])])
            tr_free[dc % 2] = ev_last
        x1c_free[slot] = P.cnt["pe"]
        P.v(lambda e, sl=slot: e.reduce_max(DS[:, 0:1], FTM[sl][:], axis=AX.X,
                                            apply_absolute_value=True),
            waits=[("dve", ev_last)])
        P.v(lambda e: e.tensor_scalar(DS[:, 0:1], DS[:, 0:1], 1.0e-6, None, op0=ALU.add))
        tok_r = P.v(lambda e: e.reciprocal(DS[:, 1:2], DS[:, 0:1]))
        tok_x2 = P.v(lambda e, sl=slot: e.scalar_tensor_tensor(
            X2[sl][:], FTM[sl][:], DS[:, 1:2], X1TM[sl][:], op0=ALU.mult, op1=ALU.add),
            waits=[("dve", tok_r)])
        d_free[slot] = tok_x2
        tok_sq = P.s(lambda e, sl=slot: e.activation(SQ2[:], X2[sl][:], AF.Square),
                     waits=[("dve", tok_x2), ("dve", tok_s2)])
        P.v(lambda e, sl=slot: e.reduce_sum(DS[:, 2:3], X2[sl][:], axis=AX.X))
        tok_s2 = P.v(lambda e: e.reduce_sum(DS[:, 3:4], SQ2[:], axis=AX.X),
                     waits=[("act", tok_sq)])
        P.v(lambda e: e.tensor_scalar(DS[:, 4:5], DS[:, 2:3], 1.0 / D, None, op0=ALU.mult))
        P.v(lambda e: e.tensor_scalar(DS[:, 5:6], DS[:, 3:4], 1.0 / D, None, op0=ALU.mult))
        P.v(lambda e: e.tensor_tensor(DS[:, 6:7], DS[:, 4:5], DS[:, 4:5], op=ALU.mult))
        tok_v = P.v(lambda e: e.tensor_tensor(DS[:, 5:6], DS[:, 5:6], DS[:, 6:7],
                                              op=ALU.subtract))
        P.s(lambda e: e.activation(DS[:, 6:7], DS[:, 5:6], AF.Ln, bias=EPSC[:]),
            waits=[("dve", tok_v), ("dve", tok_eps)])
        tok_rs = P.s(lambda e: e.activation(DS[:, 7:8], DS[:, 6:7], AF.Exp, scale=-0.5))
        tok_ln = P.v(lambda e, sl=slot: e.tensor_scalar(
            X2[sl][:], X2[sl][:], DS[:, 4:5], DS[:, 7:8], op0=ALU.subtract, op1=ALU.mult),
            waits=[("act", tok_rs)])
        P.dma("sync", "out", lambda e, t=tc, sl=slot: e.dma_start(
            out=y_d[t * 128:(t + 1) * 128, :], in_=X2[sl][:]), waits=[("dve", tok_ln)])
    P.emit("sync", lambda e: None,
           waits=[("out", P.cnt["out"]), P.wall("sy"), P.wall("gd"),
                  P.wall("up0"), P.wall("up1"), P.wall("dn0"), P.wall("dn1"),
                  P.wall("dn2"), P.wall("dn3"),
                  P.wall("x0"), P.wall("x1"), P.wall("pf0"), P.wall("pf1"),
                  ("dve", P.cnt["dve"]), ("act", P.cnt["act"]), ("pe", P.cnt["pe"])])
    P.flush()
    stD.close()
    es.close()
    return nc, P.dma_hints


_NC_CACHE = None


def kernel(L0_attn_x, L0_attn_W_attn, L0_attn_W_proj, L0_ffn_gate_w, L0_ffn_bias,
           L0_ffn_up_w, L0_ffn_up_b, L0_ffn_down_w, L0_ffn_down_b, L0_ffn_sw_up,
           L0_ffn_sb_up, L0_ffn_sw_down, L0_ffn_sb_down):
    global _NC_CACHE
    x = np.ascontiguousarray(np.asarray(L0_attn_x, dtype=np.float32))
    common = dict(
        wattn=np.ascontiguousarray(np.asarray(L0_attn_W_attn, dtype=np.float32)),
        wproj=np.ascontiguousarray(np.asarray(L0_attn_W_proj, dtype=np.float32)),
        gatew=np.ascontiguousarray(np.asarray(L0_ffn_gate_w, dtype=np.float32)),
        gateb=np.ascontiguousarray(np.reshape(np.asarray(L0_ffn_bias, dtype=np.float32), (1, NE))),
        upw=np.ascontiguousarray(np.asarray(L0_ffn_up_w, dtype=np.float32)),
        upb=np.ascontiguousarray(np.asarray(L0_ffn_up_b, dtype=np.float32)),
        downw=np.ascontiguousarray(np.asarray(L0_ffn_down_w, dtype=np.float32)),
        downb=np.ascontiguousarray(np.asarray(L0_ffn_down_b, dtype=np.float32)),
        swup=np.ascontiguousarray(np.asarray(L0_ffn_sw_up, dtype=np.float32)),
        sbup=np.ascontiguousarray(np.reshape(np.asarray(L0_ffn_sb_up, dtype=np.float32), (1, DI))),
        swdown=np.ascontiguousarray(np.asarray(L0_ffn_sw_down, dtype=np.float32)),
        sbdown=np.ascontiguousarray(np.reshape(np.asarray(L0_ffn_sb_down, dtype=np.float32), (1, D))),
        consts=_host_consts(),
        consts2=_host_consts2(),
    )
    in_maps = []
    for c in range(8):
        b, hf = c // 2, c % 2
        m = dict(common)
        m["x"] = np.ascontiguousarray(x[b])
        m["xq"] = np.ascontiguousarray(x[b, hf * NTOK:(hf + 1) * NTOK, :])
        in_maps.append(m)
    if _NC_CACHE is None:
        _nc1, hints = build_nc()
        del _nc1
        _NC_CACHE, _ = build_nc(hints)
    res = run_bass_kernel_spmd(_NC_CACHE, in_maps, core_ids=list(range(8)))
    out = np.empty((4, T, D), dtype=np.float32)
    for c in range(8):
        b, hf = c // 2, c % 2
        out[b, hf * NTOK:(hf + 1) * NTOK, :] = np.asarray(res.results[c]["y"])
    return out
